# version 10
# baseline (speedup 1.0000x reference)
import contextlib
import numpy as np
import concourse.bass as bass
import concourse.mybir as mybir
from concourse.bass_utils import run_bass_kernel_spmd

F32 = mybir.dt.float32
BF16 = mybir.dt.bfloat16
AF = mybir.ActivationFunctionType
ALU = mybir.AluOpType
AX = mybir.AxisListType

NCORES = 8
D = 1024
T_CORE = 4096
TILE = 512
N_TILES = T_CORE // TILE
SUB = TILE // 128
CC = D // 128
HID = 2048
NQ = 4
HQ = HID // NQ
HEADS = 16
DH = 64
CS = 256
NCH = TILE // CS
NEXP = 10
NSH = 2
NCHSP = 24
NSLSP = NCHSP * 512
BIG = 1.0e30
HB = 4096


class _Op:
    __slots__ = ("eng", "kind", "ch", "fn", "deps", "need_inc", "val", "seq")

    def __init__(self, eng, kind, ch, fn, deps, seq):
        self.eng, self.kind, self.ch, self.fn = eng, kind, ch, fn
        self.deps = deps
        self.need_inc = kind == "d"
        self.val = None
        self.seq = seq


class Prog:
    ENGINES = ("tensor", "vector", "scalar", "gpsimd", "sync")

    def __init__(self):
        self.ops = []

    def op(self, eng, fn, deps=()):
        o = _Op(eng, "c", "", fn, [d for d in deps if d is not None], len(self.ops))
        self.ops.append(o)
        return o

    def dma(self, eng, fn, deps=(), ch=""):
        o = _Op(eng, "d", ch, fn, [d for d in deps if d is not None], len(self.ops))
        self.ops.append(o)
        return o

    def emit(self, nc, stack):
        for o in self.ops:
            for d in o.deps:
                d.need_inc = True
        cnt = {}
        for o in self.ops:
            if o.need_inc:
                key = (o.eng, o.kind, o.ch)
                step = 16 if o.kind == "d" else 1
                cnt[key] = cnt.get(key, 0) + step
                o.val = (key, cnt[key])
        sems = {k: stack.enter_context(nc.semaphore(f"s_{k[0]}_{k[1]}{k[2]}"))
                for k in cnt}
        block = stack.enter_context(nc.Block())
        per_eng = {e: [o for o in self.ops if o.eng == e] for e in self.ENGINES}
        for eng_name in self.ENGINES:
            if not per_eng[eng_name]:
                continue

            def make(engn):
                ops = per_eng[engn]

                def emitter(eng):
                    waited = {}
                    for o in ops:
                        need = {}
                        for d in o.deps:
                            if d.val is None:
                                continue
                            key, v = d.val
                            if v > need.get(key, 0):
                                need[key] = v
                        for key, v in need.items():
                            if waited.get(key, 0) >= v:
                                continue
                            eng.wait_ge(sems[key], v)
                            waited[key] = v
                        instr = o.fn(eng)
                        if o.need_inc:
                            key, _ = o.val
                            instr.then_inc(sems[key], 16 if o.kind == "d" else 1)
                return emitter
            getattr(block, eng_name)(make(eng_name))


class Reg:
    __slots__ = ("w", "r")

    def __init__(self):
        self.w = []
        self.r = []

    def for_read(self):
        return list(self.w)

    def for_write(self):
        return list(self.w) + list(self.r)

    def did_read(self, op):
        self.r.append(op)

    def did_write(self, op):
        if self.r:
            self.w = [op]
            self.r = []
        else:
            self.w.append(op)


def build_program(debug=False):
    nc = bass.Bass("TRN2", target_bir_lowering=False)
    p = Prog()
    stack = contextlib.ExitStack()
    E = stack.enter_context

    def din(name, shape):
        return nc.dram_tensor(name, list(shape), F32, kind="ExternalInput").ap()

    x_in = din("x", (T_CORE, D))
    wqkvp = din("wqkvp", (6, 128, CC * 512))
    woutp = din("woutp", (2, 128, CC * 512))
    w1p = din("w1p", (NEXP, NQ, 128, CC * HQ))
    w3p = din("w3p", (NEXP, NQ, 128, CC * HQ))
    w2p = din("w2p", (NEXP, NQ, 128, NQ * D))
    rwp = din("rwp", (128, CC * 8))
    boutp = din("boutp", (128, CC))
    ebias_in = din("ebias", (128, 1))
    bl1_in = din("bl1", (64, 1))
    bl2_in = din("bl2", (16, 1))
    wl1row = din("wl1row", (16, 64))
    wl2t = din("wl2t", (64, 16))
    idx8_in = din("idx8", (128, 8))

    y_out = nc.dram_tensor("y", [T_CORE, D], F32, kind="ExternalOutput").ap()
    lam_scr = nc.dram_tensor("lam_scr", [HEADS, 2048], F32, kind="Internal").ap()
    cw_scr = nc.dram_tensor("cw_scr", [8, TILE], F32, kind="Internal").ap()
    wqkv_s = nc.dram_tensor("wqkv_s", [6, 128, 2 * HB], BF16, kind="Internal").ap()
    w1_s = nc.dram_tensor("w1_s", [NEXP, NQ, 128, 2 * HB], BF16, kind="Internal").ap()
    w3_s = nc.dram_tensor("w3_s", [NEXP, NQ, 128, 2 * HB], BF16, kind="Internal").ap()
    w2_s = nc.dram_tensor("w2_s", [NEXP, NQ, 128, 2 * HB], BF16, kind="Internal").ap()
    xn_scr = nc.dram_tensor("xn_scr", [T_CORE, D], F32, kind="Internal").ap()
    ysh_scr = nc.dram_tensor("ysh_scr", [N_TILES, 128, CC * TILE], F32, kind="Internal").ap()
    og_scr = nc.dram_tensor("og_scr", [NSLSP, D], F32, kind="Internal").ap()
    tok_scr = nc.dram_tensor("tok_scr", [NSLSP, 1], F32, kind="Internal").ap()
    cw2_scr = nc.dram_tensor("cw2_scr", [NSLSP, 1], F32, kind="Internal").ap()
    idx_scr = nc.dram_tensor("idx_scr", [N_TILES, TILE, 6], F32, kind="Internal").ap()

    dbg = {}
    if debug:
        def dout(name, shape):
            dbg[name] = nc.dram_tensor("dbg_" + name, list(shape), F32,
                                       kind="ExternalOutput").ap()
        dout("q_t", (128, CC * TILE))
        dout("k_t", (128, CC * TILE))
        dout("v_tok", (128, SUB * D))
        dout("attn_out", (128, CC * TILE))
        dout("y_proj", (128, CC * TILE))
        dout("lg_tok", (128, SUB * 8))
        dout("cw_fm", (8, TILE))
        dout("lam_fm", (HEADS, 2048))
        dout("y_ffn", (128, CC * TILE))

    def S(name, shape, dtype=F32):
        return E(nc.sbuf_tensor(name, list(shape), dtype))

    RXs = S("RX", (128, SUB * D))
    RAs = S("RA", (128, SUB * D))
    RBs = S("RB", (128, SUB * D))
    RDs = S("RD", (128, SUB * D))
    XNs = S("XN", (128, 2 * HB), BF16)
    RYs = S("RY", (128, CC * TILE))
    RHs = S("RH", (128, 4096), BF16)
    SGs = [S(f"SG{i}", (128, TILE)) for i in range(2)]
    LAMs = [S("LAM0", (128, TILE))]
    CWs = [S(f"CW{i}", (128, TILE)) for i in range(2)]
    RSTGs = [S("RSTG0", (128, 128))]
    ATTs = S("ATT", (128, 2 * CS))
    ATNs = S("ATN", (128, 2 * CS))
    STATs = S("STAT", (128, 64))
    STATAs = S("STATA", (128, 16))
    LGs = S("LG", (8, TILE))
    LGTs = S("LGT", (128, SUB * 8))
    R1s = S("R1", (128, 8))
    R2s = S("R2", (128, 8))
    R3s = S("R3", (128, 8))
    RSs = S("RS", (128, 16))
    CWTs = S("CWT", (128, SUB * 8))
    CWFs = S("CWF", (8, TILE))
    IDs = S("ID", (128, 128))
    IDX8s = S("IDX8", (128, 8))
    EBs = S("EB", (128, 1))
    BL1s = S("BL1", (64, 1))
    BL2s = S("BL2", (16, 1))
    WL1s = S("WL1", (16, 64))
    WL2s = S("WL2", (64, 16))
    RWs = S("RW", (128, CC * 8))
    BOUTs = S("BOUT", (128, CC))
    LTs = S("LT", (128, 128))
    ONESs = S("ONES", (128, 128))
    IOTAPs = S("IOTAP", (128, 1))
    M8s = S("M8", (128, 8))
    PRKs = S("PRK", (128, 8))
    IDXTs = S("IDXT", (128, SUB * 6))
    RBASEs = S("RBASE", (1, 8))
    SEGs = S("SEG", (8, 4))
    SEGRs = S("SEGR", (1, 8))
    SEGBCs = S("SEGBC", (128, 8))
    KTHs = S("KTH", (8, 24))
    EOCRs = S("EOCR", (1, 24))
    IDXALLs = S("IDXALL", (128, 8 * 8), mybir.dt.uint32)
    TOKSBs = S("TOKSB", (128, 4), mybir.dt.uint32)
    WIDXs = S("WIDX", (128, 96), mybir.dt.uint32)
    VALFs = S("VALF", (128, 2))
    NSLOT = 5
    SLOTs = [S(f"SLOT{i}", (128, 2 * HB), BF16) for i in range(NSLOT)]

    def slotf(i):
        return SLOTs[i][:, :].bitcast(F32)

    PBs = [E(nc.psum_tensor(f"pb{i}", [128, 512], F32)) for i in range(8)]

    RX, RA, RB, RD, RXN, RY, RH = (Reg() for _ in range(7))
    RSG = [Reg(), Reg()]
    RLAM = [Reg()]
    RCW = [Reg(), Reg()]
    RSTG = [Reg()]
    RATT, RATN, RSTAT, RSTATA = (Reg() for _ in range(4))
    RLG, RLGT, RR1, RR2, RR3, RRS, RCWT, RCWF = (Reg() for _ in range(8))
    RCONST = Reg()
    RLAMSCR, RCWSCR, RWSPLIT = Reg(), Reg(), Reg()
    RXNSCR, RYSH, ROG, RTOKSCR, RCW2SCR, RIDXSCR = (Reg() for _ in range(6))
    RM8, RPRK, RIDXT, RRBASE, RSEG, RSEGR, RSEGBC = (Reg() for _ in range(7))
    RTHR, RKTH, REOCR, RIDXALL, RTOKSB, RTOKROW, RWIDX, RVAL = (Reg() for _ in range(8))
    PB = [Reg() for _ in range(8)]
    SL = [Reg() for _ in range(NSLOT)]

    def do(eng, fn, reads=(), writes=(), extra=()):
        deps = [d for r in reads for d in r.for_read()]
        deps += [d for w in writes for d in w.for_write()]
        deps += [d for d in extra if d is not None]
        o = p.op(eng, fn, deps)
        for r in reads:
            r.did_read(o)
        for w in writes:
            w.did_write(o)
        return o

    def dod(eng, fn, reads=(), writes=(), extra=(), ch=""):
        deps = [d for r in reads for d in r.for_read()]
        deps += [d for w in writes for d in w.for_write()]
        deps += [d for d in extra if d is not None]
        o = p.dma(eng, fn, deps, ch=ch)
        for r in reads:
            r.did_read(o)
        for w in writes:
            w.did_write(o)
        return o

    for t, ap_ in [(IDX8s, idx8_in), (EBs, ebias_in),
                   (BL1s, bl1_in), (BL2s, bl2_in),
                   (WL1s, wl1row), (WL2s, wl2t),
                   (RWs, rwp), (BOUTs, boutp)]:
        o = dod("sync", lambda e, t=t, a=ap_: e.dma_start(out=t[:, :], in_=a),
                ch="c0")
        RCONST.w.append(o)

    o = do("gpsimd", lambda e: e.iota(IDs[:, :], pattern=[[1, 128]], base=0,
                                      channel_multiplier=-1,
                                      allow_small_or_imprecise_dtypes=True),
           writes=[RCONST])
    do("vector", lambda e: e.tensor_scalar(IDs[:, :], IDs[:, :], 0.0, None,
                                           op0=ALU.is_equal), writes=[RCONST])
    o = do("gpsimd", lambda e: e.iota(LTs[:, :], pattern=[[1, 128]], base=0,
                                      channel_multiplier=-1,
                                      allow_small_or_imprecise_dtypes=True),
           writes=[RCONST])
    do("vector", lambda e: e.tensor_scalar(LTs[:, :], LTs[:, :], 0.0, None,
                                           op0=ALU.is_gt), writes=[RCONST])
    do("vector", lambda e: e.memset(ONESs[:, :], 1.0), writes=[RCONST])
    do("gpsimd", lambda e: e.iota(IOTAPs[:, :], pattern=[[0, 1]], base=0,
                                  channel_multiplier=1,
                                  allow_small_or_imprecise_dtypes=True),
       writes=[RCONST])
    do("vector", lambda e: e.memset(RBASEs[:, :], 0.0), writes=[RRBASE])
    zop = do("vector", lambda e: e.memset(M8s[:, :], 0.0), writes=[RM8])
    for zi in range(12):
        dod("scalar", lambda e, zi=zi: e.dma_start(
            out=tok_scr[zi * 1024:(zi + 1) * 1024].rearrange("(p a) b -> p (a b)", p=128),
            in_=M8s[:, 0:8]), reads=[RM8], writes=[RTOKSCR], ch="z")
        dod("scalar", lambda e, zi=zi: e.dma_start(
            out=cw2_scr[zi * 1024:(zi + 1) * 1024].rearrange("(p a) b -> p (a b)", p=128),
            in_=M8s[:, 0:8]), reads=[RM8], writes=[RCW2SCR], ch="z")

    h1v = RYs[0:64, 0:2048]
    lamv = RYs[0:16, 2048:4096]
    lp_hi = RDs[64:65, 0:2048]
    lpv = RDs[0:1, 0:2048]
    o = do("gpsimd", lambda e: e.iota(lp_hi, pattern=[[1, 2048]], base=1,
                                      channel_multiplier=0,
                                      allow_small_or_imprecise_dtypes=True),
           writes=[RD])
    do("scalar", lambda e: e.activation(lpv, lp_hi, AF.Ln),
       reads=[RD], writes=[RD])
    for sl in range(4):
        mm = do("tensor", lambda e, sl=sl: e.matmul(
            PBs[0][0:64, :], WL1s[0:1, :], lpv[0:1, sl * 512:(sl + 1) * 512],
            start=True, stop=True), reads=[RCONST, RD], writes=[PB[0]])
        do("scalar", lambda e, sl=sl: e.activation(
            h1v[:, sl * 512:(sl + 1) * 512], PBs[0][0:64, :], AF.Relu,
            bias=BL1s[:, 0:1]), reads=[PB[0], RCONST], writes=[RY])
    for sl in range(4):
        mm = do("tensor", lambda e, sl=sl: e.matmul(
            PBs[1][0:16, :], WL2s[:, :], h1v[:, sl * 512:(sl + 1) * 512],
            start=True, stop=True), reads=[RCONST, RY], writes=[PB[1]])
        do("scalar", lambda e, sl=sl: e.activation(
            lamv[:, sl * 512:(sl + 1) * 512], PBs[1][0:16, :], AF.Sigmoid,
            bias=BL2s[:, 0:1]), reads=[PB[1], RCONST], writes=[RY])
    dod("scalar", lambda e: e.dma_start(out=lam_scr, in_=lamv[:, :]),
        reads=[RY], writes=[RLAMSCR], ch="lsm")
    if debug:
        dod("scalar", lambda e: e.dma_start(out=dbg["lam_fm"], in_=lamv[:, :]),
            reads=[RY], ch="dbg")

    units = [(wqkvp[ms], wqkv_s[ms]) for ms in range(6)]
    for ei in range(NSH):
        for q in range(NQ):
            units.append((w1p[ei, q], w1_s[ei, q]))
            units.append((w3p[ei, q], w3_s[ei, q]))
            units.append((w2p[ei, q], w2_s[ei, q]))
    def record_split_unit(u, src_, dst):
        a, b = (u % 2) * 2, (u % 2) * 2 + 1
        dod("sync", lambda e, src_=src_, a=a: e.dma_start(out=slotf(a), in_=src_),
            writes=[SL[a]], ch=f"sl{a}")
        do("scalar", lambda e, a=a, b=b: e.activation(
            SLOTs[b][:, 0:HB], slotf(a), AF.Copy),
           reads=[SL[a]], writes=[SL[b]])
        do("vector", lambda e, a=a, b=b: e.tensor_tensor(
            SLOTs[b][:, HB:2 * HB], slotf(a), SLOTs[b][:, 0:HB], op=ALU.subtract),
           reads=[SL[a]], writes=[SL[b]])
        dod("scalar", lambda e, b=b, dst=dst: e.dma_start(out=dst, in_=SLOTs[b][:, :]),
            reads=[SL[b]], writes=[RWSPLIT], ch=f"ws{b}")

    for u, (src_, dst) in enumerate(units[:30]):
        record_split_unit(u, src_, dst)
    routed_units = units[30:]

    qkv_bank_rr = [4, 5]
    att_bank_rr = [4, 5, 6, 7]
    rr_state = {"qkv": 0, "att": 0, "slot": 0, "gu": 0, "w2": 0, "stg": 0}

    def next_bank(pool, key):
        i = pool[rr_state[key] % len(pool)]
        rr_state[key] += 1
        return i

    def next_slot():
        i = rr_state["slot"] % NSLOT
        rr_state["slot"] += 1
        return i

    from concourse.bass import IndirectOffsetOnAxis as IOA

    def record_index_finalize():
        cdep = RRBASE.for_read() + RCONST.for_read() + PB[6].for_write()
        c1 = p.op("tensor", lambda e: e.matmul(PBs[6][0:8, 0:1], RBASEs[0:1, :],
                                               ONESs[0:1, 0:1], start=True, stop=True),
                  deps=cdep)
        RRBASE.did_read(c1)
        PB[6].did_write(c1)
        do("vector", lambda e: e.tensor_copy(SEGs[0:8, 0:1], PBs[6][0:8, 0:1]),
           reads=[PB[6]], writes=[RSEG])
        do("gpsimd", lambda e: e.iota(KTHs[0:8, 0:16], pattern=[[512, 16]], base=0,
                                      channel_multiplier=0,
                                      allow_small_or_imprecise_dtypes=True),
           writes=[RKTH])
        do("vector", lambda e: e.tensor_scalar(KTHs[0:8, 0:16], KTHs[0:8, 0:16],
                                               SEGs[0:8, 0:1], None, op0=ALU.is_lt),
           reads=[RSEG], writes=[RKTH])
        do("vector", lambda e: e.tensor_reduce(SEGs[0:8, 1:2], KTHs[0:8, 0:16],
                                               axis=AX.X, op=ALU.add),
           reads=[RKTH], writes=[RSEG])
        do("vector", lambda e: e.tensor_scalar(SEGs[0:8, 1:2], SEGs[0:8, 1:2], 512.0,
                                               None, op0=ALU.mult), writes=[RSEG])
        sdep = RSEG.for_read() + RCONST.for_read() + PB[6].for_write()
        smm = p.op("tensor", lambda e: e.matmul(PBs[6][0:8, 4:5], LTs[0:8, 0:8],
                                                SEGs[0:8, 1:2], start=True, stop=True),
                   deps=sdep)
        RSEG.did_read(smm)
        PB[6].did_write(smm)
        do("vector", lambda e: e.tensor_copy(SEGs[0:8, 2:3], PBs[6][0:8, 4:5]),
           reads=[PB[6]], writes=[RSEG])
        do("vector", lambda e: e.tensor_tensor(SEGs[0:8, 3:4], SEGs[0:8, 2:3],
                                               SEGs[0:8, 1:2], op=ALU.add), writes=[RSEG])
        rdep = RSEG.for_read() + RCONST.for_read() + PB[6].for_write()
        rmm = p.op("tensor", lambda e: e.matmul(PBs[6][0:1, 8:16], SEGs[0:8, 2:3],
                                                IDs[0:8, 0:8], start=True, stop=True),
                   deps=rdep)
        RSEG.did_read(rmm)
        PB[6].did_write(rmm)
        do("vector", lambda e: e.tensor_copy(SEGRs[0:1, :], PBs[6][0:1, 8:16]),
           reads=[PB[6]], writes=[RSEGR])
        bdep = RSEGR.for_read() + RCONST.for_read() + PB[6].for_write()
        bmm = p.op("tensor", lambda e: e.matmul(PBs[6][:, 16:24], ONESs[0:1, :],
                                                SEGRs[0:1, :], start=True, stop=True),
                   deps=bdep)
        RSEGR.did_read(bmm)
        PB[6].did_write(bmm)
        do("vector", lambda e: e.tensor_copy(SEGBCs[:, :], PBs[6][:, 16:24]),
           reads=[PB[6]], writes=[RSEGBC])
        do("gpsimd", lambda e: e.iota(KTHs[:, :], pattern=[[512, 24]], base=0,
                                      channel_multiplier=0,
                                      allow_small_or_imprecise_dtypes=True),
           writes=[RKTH])
        do("vector", lambda e: e.tensor_scalar(IDXTs[0:8, 0:24], KTHs[0:8, :],
                                               SEGs[0:8, 2:3], None, op0=ALU.is_ge),
           reads=[RKTH, RSEG], writes=[RIDXT])
        do("vector", lambda e: e.tensor_scalar(KTHs[0:8, :], KTHs[0:8, :],
                                               SEGs[0:8, 3:4], None, op0=ALU.is_lt),
           reads=[RSEG], writes=[RKTH])
        do("vector", lambda e: e.tensor_tensor(IDXTs[0:8, 0:24], IDXTs[0:8, 0:24],
                                               KTHs[0:8, :], op=ALU.mult),
           reads=[RKTH], writes=[RIDXT])
        edep = RIDXT.for_read() + RCONST.for_read() + PB[6].for_write()
        emm = p.op("tensor", lambda e: e.matmul(PBs[6][0:1, 24:48], IOTAPs[0:8, 0:1],
                                                IDXTs[0:8, 0:24], start=True, stop=True),
                   deps=edep)
        RIDXT.did_read(emm)
        PB[6].did_write(emm)
        do("vector", lambda e: e.tensor_copy(EOCRs[0:1, :], PBs[6][0:1, 24:48]),
           reads=[PB[6]], writes=[REOCR])
        for k in range(NCHSP):
            wdep = REOCR.for_read() + RCONST.for_read() + PB[7].for_write()
            wmm = p.op("tensor", lambda e, k=k: e.matmul(
                PBs[7][:, 8:9], ONESs[0:1, :], EOCRs[0:1, k:k + 1],
                start=True, stop=True), deps=wdep)
            REOCR.did_read(wmm)
            PB[7].did_write(wmm)
            for q in range(NQ):
                do("vector", lambda e, q=q: e.tensor_scalar(
                    VALFs[:, 0:1], PBs[7][:, 8:9], 512.0, float(q * 128 + NSH * NQ * 128),
                    op0=ALU.mult, op1=ALU.add), reads=[PB[7]], writes=[RVAL])
                do("vector", lambda e: e.tensor_tensor(
                    VALFs[:, 0:1], VALFs[:, 0:1], IOTAPs[:, :], op=ALU.add),
                   reads=[RCONST], writes=[RVAL])
                do("vector", lambda e, k=k, q=q: e.tensor_copy(
                    WIDXs[:, k * 4 + q:k * 4 + q + 1], VALFs[:, 0:1]),
                   reads=[RVAL], writes=[RWIDX])

        for tile in range(N_TILES):
            t0 = tile * TILE
            dod("scalar", lambda e, tile=tile: e.dma_start(
                out=IDXTs[:, :].rearrange("p (s r) -> p s r", s=SUB),
                in_=idx_scr[tile].rearrange("(s p) r -> p s r", p=128)),
                reads=[RIDXSCR], writes=[RIDXT], ch="ix")
            for s in range(SUB):
                for li in range(2):
                    icol = tile * 8 + s * 2 + li
                    do("vector", lambda e, s=s, li=li: e.tensor_scalar(
                        R1s[:, :], IDX8s[:, :], IDXTs[:, s * 6 + 2 + li:s * 6 + 3 + li],
                        None, op0=ALU.is_equal),
                       reads=[RCONST, RIDXT], writes=[RR1])
                    do("vector", lambda e: e.tensor_tensor(
                        R3s[:, :], R1s[:, :], SEGBCs[:, :], op=ALU.mult),
                       reads=[RR1, RSEGBC], writes=[RR3])
                    do("vector", lambda e: e.tensor_reduce(
                        RSs[:, 0:1], R3s[:, :], axis=AX.X, op=ALU.add),
                       reads=[RR3], writes=[RRS])
                    do("vector", lambda e, s=s, li=li: e.tensor_tensor(
                        VALFs[:, 0:1], RSs[:, 0:1], IDXTs[:, s * 6 + li:s * 6 + 1 + li],
                        op=ALU.add), reads=[RRS, RIDXT], writes=[RVAL])
                    do("vector", lambda e, icol=icol: e.tensor_copy(
                        IDXALLs[:, icol:icol + 1], VALFs[:, 0:1]),
                       reads=[RVAL], writes=[RIDXALL])
                    do("vector", lambda e, t0=t0, s=s: e.tensor_scalar(
                        VALFs[:, 1:2], IOTAPs[:, :], float(t0 + s * 128), None,
                        op0=ALU.add), reads=[RCONST], writes=[RVAL])
                    dod("gpsimd", lambda e, icol=icol: e.indirect_dma_start(
                        out=tok_scr, out_offset=IOA(IDXALLs[:, icol:icol + 1], 0),
                        in_=VALFs[:, 1:2], in_offset=None),
                        reads=[RVAL, RIDXALL], writes=[RTOKSCR], ch="sc")
                    dod("gpsimd", lambda e, icol=icol, s=s, li=li: e.indirect_dma_start(
                        out=cw2_scr, out_offset=IOA(IDXALLs[:, icol:icol + 1], 0),
                        in_=IDXTs[:, s * 6 + 4 + li:s * 6 + 5 + li], in_offset=None),
                        reads=[RIDXT, RIDXALL], writes=[RCW2SCR], ch="sc")



    store_ops = []

    for tile in range(N_TILES):
        t0 = tile * TILE
        pos0 = t0 % 2048

        dod("scalar", lambda e, t0=t0: e.dma_start(
            out=RXs[:, :],
            in_=x_in[t0:t0 + TILE, :].rearrange("(s p) c -> p s c", p=128)),
            writes=[RX], ch="x")

        for s in range(SUB):
            xsl = RXs[:, s * D:(s + 1) * D]
            c0 = 16 * s
            do("vector", lambda e, xsl=xsl, c0=c0: e.tensor_reduce(
                STATs[:, c0:c0 + 8],
                xsl.rearrange("p (a b) -> p a b", a=8), axis=AX.X, op=ALU.add),
               reads=[RX], writes=[RSTAT])
            do("vector", lambda e, c0=c0: e.tensor_reduce(
                STATs[:, c0 + 8:c0 + 9], STATs[:, c0:c0 + 8], axis=AX.X, op=ALU.add),
               writes=[RSTAT])
            do("vector", lambda e, c0=c0: e.tensor_scalar(
                STATs[:, c0 + 9:c0 + 10], STATs[:, c0 + 8:c0 + 9], -1.0 / D, None,
                op0=ALU.mult), writes=[RSTAT])
            do("scalar", lambda e, xsl=xsl, c0=c0: e.activation(
                RDs[:, 0:D], xsl, AF.Square, bias=STATs[:, c0 + 9:c0 + 10]),
               reads=[RX, RSTAT], writes=[RD])
            do("vector", lambda e, c0=c0: e.tensor_reduce(
                STATs[:, c0:c0 + 8],
                RDs[:, 0:D].rearrange("p (a b) -> p a b", a=8), axis=AX.X,
                op=ALU.add), reads=[RD], writes=[RSTAT])
            do("vector", lambda e, c0=c0: e.tensor_reduce(
                STATs[:, c0 + 10:c0 + 11], STATs[:, c0:c0 + 8], axis=AX.X,
                op=ALU.add), writes=[RSTAT])
            do("vector", lambda e, c0=c0: e.tensor_scalar(
                STATs[:, c0 + 11:c0 + 12], STATs[:, c0 + 10:c0 + 11], 1.0 / D, 1e-5,
                op0=ALU.mult, op1=ALU.add), writes=[RSTAT])
            do("scalar", lambda e, c0=c0: e.activation(
                STATs[:, c0 + 12:c0 + 13], STATs[:, c0 + 11:c0 + 12], AF.Sqrt),
               writes=[RSTAT])
            do("vector", lambda e, c0=c0: e.reciprocal(
                STATs[:, c0 + 13:c0 + 14], STATs[:, c0 + 12:c0 + 13]),
               writes=[RSTAT])
            do("vector", lambda e, xsl=xsl, s=s, c0=c0: e.tensor_scalar(
                RAs[:, s * D:(s + 1) * D], xsl, STATs[:, c0 + 9:c0 + 10],
                STATs[:, c0 + 13:c0 + 14], op0=ALU.add, op1=ALU.mult),
               reads=[RX, RSTAT], writes=[RA])

        for i in range(SUB * CC):
            s, cc = divmod(i, CC)
            bk = 6 + (i % 2)
            tr = do("tensor", lambda e, s=s, cc=cc, bk=bk: e.transpose(
                PBs[bk][:, 0:128],
                RAs[:, s * D + cc * 128:s * D + cc * 128 + 128], IDs[:, :]),
                reads=[RA, RCONST], writes=[PB[bk]])
            off = cc * TILE + s * 128
            x1op = do("scalar", lambda e, off=off, bk=bk: e.activation(
                XNs[:, off:off + 128], PBs[bk][:, 0:128], AF.Copy),
                reads=[PB[bk]], writes=[RXN])
            do("vector", lambda e, off=off, bk=bk: e.tensor_tensor(
                XNs[:, HB + off:HB + off + 128], PBs[bk][:, 0:128],
                XNs[:, off:off + 128], op=ALU.subtract),
               reads=[PB[bk], RXN], writes=[RXN])
            g = 0
            do("vector", lambda e, bk=bk, g=g: e.tensor_copy(
                RSTGs[g][:, :], PBs[bk][:, 0:128]),
               reads=[PB[bk]], writes=[RSTG[g]])
            lg_deps = (RCONST.for_read() + RSTG[g].for_read()
                       + (PB[3].for_write() if cc == 0 else []))
            lgmm = p.op("tensor", lambda e, s=s, cc=cc, g=g: e.matmul(
                PBs[3][0:8, s * 128:(s + 1) * 128], RWs[:, cc * 8:(cc + 1) * 8],
                RSTGs[g][:, :], start=(cc == 0), stop=(cc == CC - 1)),
                deps=lg_deps)
            RSTG[g].did_read(lgmm)
            if cc == CC - 1:
                PB[3].did_write(lgmm)
        do("vector", lambda e: e.tensor_scalar(
            LGs[:, :], PBs[3][0:8, :], EBs[0:8, 0:1], None, op0=ALU.add),
           reads=[PB[3], RCONST], writes=[RLG])

        dod("scalar", lambda e, t0=t0: e.dma_start(
            out=xn_scr[t0:t0 + TILE, :].rearrange("(s p) c -> p s c", p=128),
            in_=RAs[:, :]), reads=[RA], writes=[RXNSCR], ch="xn")

        for tc in range(SUB):
            do("tensor", lambda e, tc=tc: e.transpose(
                PBs[6][:, tc * 8:(tc + 1) * 8],
                LGs[:, tc * 128:(tc + 1) * 128], IDs[0:8, 0:8]),
               reads=[RLG, RCONST], writes=[PB[6]])
        do("vector", lambda e: e.tensor_copy(
            LGTs[:, :], PBs[6][:, 0:SUB * 8]), reads=[PB[6]], writes=[RLGT])
        if debug and tile == 0:
            dod("scalar", lambda e: e.dma_start(out=dbg["lg_tok"], in_=LGTs[:, :]),
                reads=[RLGT], ch="dbg")
        for tc in range(SUB):
            lg = LGTs[:, tc * 8:(tc + 1) * 8]
            do("vector", lambda e, lg=lg: e.tensor_reduce(
                RSs[:, 0:1], lg, axis=AX.X, op=ALU.max), reads=[RLGT], writes=[RRS])
            do("vector", lambda e, lg=lg: e.tensor_scalar(
                R1s[:, :], lg, RSs[:, 0:1], None, op0=ALU.is_ge),
               reads=[RLGT, RRS], writes=[RR1])
            do("vector", lambda e: e.scalar_tensor_tensor(
                R2s[:, :], IDX8s[:, :], 9.0, R1s[:, :], op0=ALU.subtract, op1=ALU.mult),
               reads=[RCONST, RR1], writes=[RR2])
            do("vector", lambda e: e.tensor_reduce(
                RSs[:, 1:2], R2s[:, :], axis=AX.X, op=ALU.min),
               reads=[RR2], writes=[RRS])
            do("vector", lambda e: e.tensor_scalar(
                RSs[:, 1:2], RSs[:, 1:2], 9.0, None, op0=ALU.add), writes=[RRS])
            do("vector", lambda e: e.tensor_scalar(
                R1s[:, :], IDX8s[:, :], RSs[:, 1:2], None, op0=ALU.is_equal),
               reads=[RCONST, RRS], writes=[RR1])
            do("vector", lambda e, lg=lg: e.scalar_tensor_tensor(
                R3s[:, :], R1s[:, :], -BIG, lg, op0=ALU.mult, op1=ALU.add),
               reads=[RR1, RLGT], writes=[RR3])
            do("vector", lambda e: e.tensor_reduce(
                RSs[:, 2:3], R3s[:, :], axis=AX.X, op=ALU.max),
               reads=[RR3], writes=[RRS])
            do("vector", lambda e: e.tensor_scalar(
                R2s[:, :], R3s[:, :], RSs[:, 2:3], None, op0=ALU.is_ge),
               reads=[RR3, RRS], writes=[RR2])
            do("vector", lambda e: e.scalar_tensor_tensor(
                R2s[:, :], IDX8s[:, :], 9.0, R2s[:, :], op0=ALU.subtract, op1=ALU.mult),
               reads=[RCONST], writes=[RR2])
            do("vector", lambda e: e.tensor_reduce(
                RSs[:, 3:4], R2s[:, :], axis=AX.X, op=ALU.min),
               reads=[RR2], writes=[RRS])
            do("vector", lambda e: e.tensor_scalar(
                RSs[:, 3:4], RSs[:, 3:4], 9.0, None, op0=ALU.add), writes=[RRS])
            do("vector", lambda e: e.tensor_scalar(
                R2s[:, :], IDX8s[:, :], RSs[:, 3:4], None, op0=ALU.is_equal),
               reads=[RCONST, RRS], writes=[RR2])
            do("vector", lambda e: e.tensor_tensor(
                M8s[:, :], R1s[:, :], R2s[:, :], op=ALU.add),
               reads=[RR1, RR2], writes=[RM8])
            pf_deps = (RCONST.for_read() + RM8.for_read() + RRBASE.for_read()
                       + PB[3].for_write())
            pf1 = p.op("tensor", lambda e, tc=tc: e.matmul(
                PBs[3][:, tc * 8:(tc + 1) * 8], LTs[:, :], M8s[:, :],
                start=True, stop=False), deps=pf_deps)
            pf2 = p.op("tensor", lambda e, tc=tc: e.matmul(
                PBs[3][:, tc * 8:(tc + 1) * 8], ONESs[0:1, :], RBASEs[0:1, :],
                start=False, stop=True), deps=[])
            RM8.did_read(pf2)
            RRBASE.did_read(pf2)
            PB[3].did_write(pf2)
            cnt_deps = RM8.for_read() + RCONST.for_read() + PB[3].for_write()
            cntmm = p.op("tensor", lambda e, tc=tc: e.matmul(
                PBs[3][0:1, 32 + tc * 8:40 + tc * 8], ONESs[:, 0:1], M8s[:, :],
                start=True, stop=True), deps=cnt_deps)
            RM8.did_read(cntmm)
            PB[3].did_write(cntmm)
            do("vector", lambda e, tc=tc: e.tensor_copy(
                PRKs[:, :], PBs[3][:, tc * 8:(tc + 1) * 8]),
               reads=[PB[3]], writes=[RPRK])
            do("vector", lambda e: e.tensor_tensor(
                R3s[:, :], R1s[:, :], PRKs[:, :], op=ALU.mult),
               reads=[RR1, RPRK], writes=[RR3])
            do("vector", lambda e, tc=tc: e.tensor_reduce(
                IDXTs[:, tc * 6:tc * 6 + 1], R3s[:, :], axis=AX.X, op=ALU.add),
               reads=[RR3], writes=[RIDXT])
            do("vector", lambda e: e.tensor_tensor(
                R3s[:, :], R2s[:, :], PRKs[:, :], op=ALU.mult),
               reads=[RR2, RPRK], writes=[RR3])
            do("vector", lambda e, tc=tc: e.tensor_reduce(
                IDXTs[:, tc * 6 + 1:tc * 6 + 2], R3s[:, :], axis=AX.X, op=ALU.add),
               reads=[RR3], writes=[RIDXT])
            do("vector", lambda e, tc=tc: e.tensor_tensor(
                RBASEs[0:1, :], RBASEs[0:1, :], PBs[3][0:1, 32 + tc * 8:40 + tc * 8],
                op=ALU.add), reads=[PB[3]], writes=[RRBASE])
            do("vector", lambda e, tc=tc: e.tensor_copy(
                IDXTs[:, tc * 6 + 2:tc * 6 + 3], RSs[:, 1:2]),
               reads=[RRS], writes=[RIDXT])
            do("vector", lambda e, tc=tc: e.tensor_copy(
                IDXTs[:, tc * 6 + 3:tc * 6 + 4], RSs[:, 3:4]),
               reads=[RRS], writes=[RIDXT])
            do("vector", lambda e: e.tensor_tensor(
                RSs[:, 4:5], RSs[:, 0:1], RSs[:, 2:3], op=ALU.subtract), writes=[RRS])
            do("scalar", lambda e: e.activation(
                RSs[:, 4:5], RSs[:, 4:5], AF.Sigmoid), writes=[RRS])
            do("vector", lambda e: e.tensor_scalar(
                RSs[:, 5:6], RSs[:, 4:5], -1.0, 1.0, op0=ALU.mult, op1=ALU.add),
               writes=[RRS])
            do("vector", lambda e: e.tensor_scalar(
                R1s[:, :], R1s[:, :], RSs[:, 4:5], None, op0=ALU.mult),
               reads=[RRS], writes=[RR1])
            do("vector", lambda e, tc=tc: e.scalar_tensor_tensor(
                CWTs[:, tc * 8:(tc + 1) * 8], R2s[:, :], RSs[:, 5:6], R1s[:, :],
                op0=ALU.mult, op1=ALU.add),
               reads=[RR2, RRS, RR1], writes=[RCWT])
            do("vector", lambda e, tc=tc: e.tensor_copy(
                IDXTs[:, tc * 6 + 4:tc * 6 + 5], RSs[:, 4:5]),
               reads=[RRS], writes=[RIDXT])
            do("vector", lambda e, tc=tc: e.tensor_copy(
                IDXTs[:, tc * 6 + 5:tc * 6 + 6], RSs[:, 5:6]),
               reads=[RRS], writes=[RIDXT])
        for tc in range(SUB):
            do("tensor", lambda e, tc=tc: e.transpose(
                PBs[7][0:8, tc * 128:(tc + 1) * 128],
                CWTs[:, tc * 8:(tc + 1) * 8], IDs[:, :]),
               reads=[RCWT, RCONST], writes=[PB[7]])
        do("vector", lambda e: e.tensor_copy(CWFs[:, :], PBs[7][0:8, 0:TILE]),
           reads=[PB[7]], writes=[RCWF])
        dod("scalar", lambda e: e.dma_start(out=cw_scr, in_=CWFs[:, :]),
            reads=[RCWF], writes=[RCWSCR], ch="sm")
        dod("scalar", lambda e, tile=tile: e.dma_start(
            out=idx_scr[tile].rearrange("(s p) r -> p s r", p=128),
            in_=IDXTs[:, :].rearrange("p (s r) -> p s r", s=SUB)),
            reads=[RIDXT], writes=[RIDXSCR], ch="sm2")
        if tile == N_TILES - 1:
            record_index_finalize()
        if debug and tile == 0:
            dod("scalar", lambda e: e.dma_start(out=dbg["cw_fm"], in_=CWFs[:, :]),
                reads=[RCWF], ch="dbg")

        def mm3(bk, lhs_pairs, rhs_pairs, lreads, out_ap=None):
            n = len(lhs_pairs)
            first_deps = ([d for r in lreads for d in r.for_read()]
                          + PB[bk].for_write())
            last = None
            for j in range(n):
                for pi, (lh, rh) in enumerate([(0, 0), (0, 1), (1, 0)]):
                    st_ = (j == 0 and pi == 0)
                    sp_ = (j == n - 1 and pi == 2)
                    last = p.op("tensor",
                                lambda e, j=j, lh=lh, rh=rh, st_=st_, sp_=sp_: e.matmul(
                                    (out_ap or PBs[bk][:, :]),
                                    lhs_pairs[j](lh), rhs_pairs[j](rh),
                                    start=st_, stop=sp_),
                                deps=first_deps if st_ else [])
            for r in lreads:
                r.did_read(last)
            PB[bk].did_write(last)
            return last

        def xn_half(cc, h):
            return XNs[:, h * HB + cc * TILE:h * HB + cc * TILE + TILE]

        def xn_half_tc(cc, tc, h):
            o = h * HB + cc * TILE + tc * 128
            return XNs[:, o:o + 128]

        for ms in range(6):
            sl_i = next_slot()
            dod("sync", lambda e, ms=ms, sl_i=sl_i: e.dma_start(
                out=SLOTs[sl_i][:, :], in_=wqkv_s[ms]), writes=[SL[sl_i]],
                reads=[RWSPLIT] if tile == 0 else (), ch=f"sl{sl_i}")
            if ms < 4:
                for mc in range(4):
                    mg = ms * 4 + mc
                    bk = next_bank(qkv_bank_rr, "qkv")
                    lhs = [(lambda e_cc: (lambda h, sl_i=sl_i, e_cc=e_cc, mc=mc:
                            SLOTs[sl_i][:, h * HB + e_cc * 512 + mc * 128:
                                         h * HB + e_cc * 512 + mc * 128 + 128]))(cc)
                           for cc in range(CC)]
                    rhs = [(lambda e_cc: (lambda h, e_cc=e_cc: xn_half(e_cc, h)))(cc)
                           for cc in range(CC)]
                    mm3(bk, lhs, rhs, [SL[sl_i], RXN])
                    if mg < 8:
                        hp = mg
                        do("scalar", lambda e, hp=hp, bk=bk: e.activation(
                            RAs[:, hp * TILE:(hp + 1) * TILE], PBs[bk][:, :],
                            AF.Copy, scale=0.125),
                           reads=[PB[bk]], writes=[RA])
                    else:
                        hp = mg - 8
                        lsl = 0
                        for half in range(2):
                            dod("scalar", lambda e, hp=hp, lsl=lsl, pos0=pos0, half=half: e.dma_start(
                                out=LAMs[lsl][half * 64:(half + 1) * 64, :],
                                in_=lam_scr[2 * hp + half:2 * hp + half + 1,
                                            pos0:pos0 + TILE].broadcast_to((64, TILE))),
                                reads=[RLAMSCR], writes=[RLAM[lsl]], ch=f"lam{lsl}")
                        do("vector", lambda e, hp=hp, lsl=lsl, bk=bk: e.tensor_tensor(
                            RBs[:, hp * TILE:(hp + 1) * TILE], PBs[bk][:, :],
                            LAMs[lsl][:, :], op=ALU.mult),
                           reads=[PB[bk], RLAM[lsl]], writes=[RB])
            else:
                vs = ms - 4
                for tc in range(SUB):
                    bk = next_bank(qkv_bank_rr, "qkv")
                    lhs = [(lambda e_cc: (lambda h, e_cc=e_cc, tc=tc:
                            xn_half_tc(e_cc, tc, h)))(cc) for cc in range(CC)]
                    rhs = [(lambda e_cc: (lambda h, sl_i=sl_i, e_cc=e_cc:
                            SLOTs[sl_i][:, h * HB + e_cc * 512:
                                         h * HB + e_cc * 512 + 512]))(cc)
                           for cc in range(CC)]
                    mm3(bk, lhs, rhs, [SL[sl_i], RXN])
                    do("vector", lambda e, tc=tc, vs=vs, bk=bk: e.tensor_copy(
                        RXs[:, tc * D + vs * 512:tc * D + vs * 512 + 512],
                        PBs[bk][:, :]), reads=[PB[bk]], writes=[RX])
        if debug and tile == 0:
            dod("scalar", lambda e: e.dma_start(out=dbg["q_t"], in_=RAs[:, :]),
                reads=[RA], ch="dbg")
            dod("scalar", lambda e: e.dma_start(out=dbg["k_t"], in_=RBs[:, :]),
                reads=[RB], ch="dbg")
            dod("scalar", lambda e: e.dma_start(out=dbg["v_tok"], in_=RXs[:, :]),
                reads=[RX], ch="dbg")

        for it in range(NCH * HEADS):
            ci, h = divmod(it, HEADS)
            hp, off = h // 2, 64 * (h % 2)
            par = (it % 2) * 8
            bkS = next_bank(att_bank_rr, "att")
            for qs in range(2):
                do("tensor", lambda e, qs=qs, ci=ci, hp=hp, off=off, bkS=bkS: e.matmul(
                    PBs[bkS][:, qs * CS:(qs + 1) * CS],
                    RAs[off:off + 64,
                        hp * TILE + ci * CS + qs * 128:hp * TILE + ci * CS + qs * 128 + 128],
                    RBs[off:off + 64, hp * TILE + ci * CS:hp * TILE + ci * CS + CS],
                    start=True, stop=True),
                   reads=[RA, RB], writes=[PB[bkS]])
            do("vector", lambda e, bkS=bkS, par=par: e.tensor_reduce(
                STATAs[:, par:par + 2],
                PBs[bkS][:, :].rearrange("p (a b) -> p a b", a=2),
                axis=AX.X, op=ALU.max, negate=True),
               reads=[PB[bkS]], writes=[RSTATA])
            for qs in range(2):
                do("scalar", lambda e, qs=qs, bkS=bkS, par=par: e.activation(
                    ATTs[:, qs * CS:(qs + 1) * CS], PBs[bkS][:, qs * CS:(qs + 1) * CS],
                    AF.Exp, bias=STATAs[:, par + qs:par + qs + 1],
                    accum_out=STATAs[:, par + 2 + qs:par + 3 + qs]),
                   reads=[PB[bkS], RSTATA], writes=[RATT, RSTATA])
            do("vector", lambda e, par=par: e.reciprocal(
                STATAs[:, par + 4:par + 6], STATAs[:, par + 2:par + 4]),
               writes=[RSTATA])
            for qs in range(2):
                do("vector", lambda e, qs=qs, par=par: e.tensor_scalar(
                    ATNs[:, qs * CS:(qs + 1) * CS], ATTs[:, qs * CS:(qs + 1) * CS],
                    STATAs[:, par + 4 + qs:par + 5 + qs], None, op0=ALU.mult),
                   reads=[RATT, RSTATA], writes=[RATN])
            bkT = next_bank(att_bank_rr, "att")
            for qs in range(2):
                for kc in range(2):
                    do("tensor", lambda e, qs=qs, kc=kc, bkT=bkT: e.transpose(
                        PBs[bkT][:, (qs * 2 + kc) * 128:(qs * 2 + kc) * 128 + 128],
                        ATNs[:, qs * CS + kc * 128:qs * CS + kc * 128 + 128],
                        IDs[:, :]),
                       reads=[RATN, RCONST], writes=[PB[bkT]])
            for qs in range(2):
                for kc in range(2):
                    do("vector", lambda e, qs=qs, kc=kc, bkT=bkT: e.tensor_copy(
                        ATTs[:, kc * CS + qs * 128:kc * CS + qs * 128 + 128],
                        PBs[bkT][:, (qs * 2 + kc) * 128:(qs * 2 + kc) * 128 + 128]),
                       reads=[PB[bkT]], writes=[RATT])
            bkA = next_bank(att_bank_rr, "att")
            av_deps = RX.for_read() + RATT.for_read() + PB[bkA].for_write()
            avmm = None
            for kc in range(2):
                s = ci * 2 + kc
                avmm = p.op("tensor", lambda e, s=s, h=h, kc=kc, bkA=bkA: e.matmul(
                    PBs[bkA][0:64, 0:CS],
                    RXs[:, s * D + h * DH:s * D + h * DH + DH],
                    ATTs[:, kc * CS:kc * CS + CS],
                    start=(kc == 0), stop=(kc == 1)),
                    deps=av_deps if kc == 0 else [])
            RX.did_read(avmm)
            RATT.did_read(avmm)
            PB[bkA].did_write(avmm)
            do("vector", lambda e, off=off, hp=hp, ci=ci, bkA=bkA: e.tensor_copy(
                RDs[off:off + 64, hp * TILE + ci * CS:hp * TILE + ci * CS + CS],
                PBs[bkA][0:64, 0:CS]),
               reads=[PB[bkA]], writes=[RD])
        if debug and tile == 0:
            dod("scalar", lambda e: e.dma_start(out=dbg["attn_out"], in_=RDs[:, :]),
                reads=[RD], ch="dbg")

        for os_ in range(2):
            sl_i = next_slot()
            dod("sync", lambda e, os_=os_, sl_i=sl_i: e.dma_start(
                out=slotf(sl_i), in_=woutp[os_]), writes=[SL[sl_i]],
                ch=f"sl{sl_i}")
            for oc4 in range(4):
                oc = os_ * 4 + oc4
                bk = next_bank(qkv_bank_rr, "qkv")
                pr_deps = (SL[sl_i].for_read() + RD.for_read()
                           + PB[bk].for_write())
                prmm = None
                for cc in range(CC):
                    prmm = p.op("tensor", lambda e, sl_i=sl_i, cc=cc, oc4=oc4, bk=bk: e.matmul(
                        PBs[bk][:, :],
                        slotf(sl_i)[:, cc * 512 + oc4 * 128:cc * 512 + oc4 * 128 + 128],
                        RDs[:, cc * TILE:cc * TILE + TILE],
                        start=(cc == 0), stop=(cc == CC - 1)),
                        deps=pr_deps if cc == 0 else [])
                SL[sl_i].did_read(prmm)
                RD.did_read(prmm)
                PB[bk].did_write(prmm)
                do("vector", lambda e, oc=oc, bk=bk: e.tensor_scalar(
                    RYs[:, oc * TILE:(oc + 1) * TILE], PBs[bk][:, :],
                    BOUTs[:, oc:oc + 1], None, op0=ALU.add),
                   reads=[PB[bk], RCONST], writes=[RY])
        if debug and tile == 0:
            dod("scalar", lambda e: e.dma_start(out=dbg["y_proj"], in_=RYs[:, :]),
                reads=[RY], ch="dbg")

        for ei in range(NSH):
            routed = False
            csl = 0
            for q in range(NQ):
                sl1, sl3, sl2 = next_slot(), next_slot(), next_slot()
                dod("sync", lambda e, ei=ei, q=q, sl1=sl1: e.dma_start(
                    out=SLOTs[sl1][:, :], in_=w1_s[ei, q]), writes=[SL[sl1]],
                    reads=[RWSPLIT] if tile == 0 else (), ch=f"sl{sl1}")
                dod("sync", lambda e, ei=ei, q=q, sl3=sl3: e.dma_start(
                    out=SLOTs[sl3][:, :], in_=w3_s[ei, q]), writes=[SL[sl3]],
                    reads=[RWSPLIT] if tile == 0 else (), ch=f"sl{sl3}")
                dod("sync", lambda e, ei=ei, q=q, sl2=sl2: e.dma_start(
                    out=SLOTs[sl2][:, :], in_=w2_s[ei, q]), writes=[SL[sl2]],
                    reads=[RWSPLIT] if tile == 0 else (), ch=f"sl{sl2}")
                for hc in range(4):
                    bkG = [0, 2][rr_state["gu"] % 2]
                    bkU = bkG + 1
                    rr_state["gu"] += 1

                    lhs1 = [(lambda e_cc: (lambda h, sl1=sl1, e_cc=e_cc, hc=hc:
                             SLOTs[sl1][:, h * HB + e_cc * HQ + hc * 128:
                                        h * HB + e_cc * HQ + hc * 128 + 128]))(cc)
                            for cc in range(CC)]
                    lhs3 = [(lambda e_cc: (lambda h, sl3=sl3, e_cc=e_cc, hc=hc:
                             SLOTs[sl3][:, h * HB + e_cc * HQ + hc * 128:
                                        h * HB + e_cc * HQ + hc * 128 + 128]))(cc)
                            for cc in range(CC)]
                    rhs = [(lambda e_cc: (lambda h, e_cc=e_cc: xn_half(e_cc, h)))(cc)
                           for cc in range(CC)]
                    mm3(bkG, lhs1, rhs, [SL[sl1], RXN])
                    mm3(bkU, lhs3, rhs, [SL[sl3], RXN])
                    do("scalar", lambda e, bkG=bkG: e.activation(
                        SGs[0][:, :], PBs[bkG][:, :], AF.Silu),
                       reads=[PB[bkG]], writes=[RSG[0]])
                    do("vector", lambda e, bkU=bkU: e.tensor_tensor(
                        SGs[1][:, :], PBs[bkU][:, :], SGs[0][:, :], op=ALU.mult),
                       reads=[PB[bkU], RSG[0]], writes=[RSG[1]])
                    if routed:
                        do("vector", lambda e, csl=csl: e.tensor_tensor(
                            SGs[0][:, :], SGs[1][:, :], CWs[csl][:, :], op=ALU.mult),
                           reads=[RSG[1], RCW[csl]], writes=[RSG[0]])
                        fin, rfin = SGs[0], RSG[0]
                    else:
                        fin, rfin = SGs[1], RSG[1]
                    do("scalar", lambda e, hc=hc, fin=fin: e.activation(
                        RHs[:, hc * 512:hc * 512 + 512], fin[:, :], AF.Copy),
                       reads=[rfin], writes=[RH])
                    do("vector", lambda e, hc=hc, fin=fin: e.tensor_tensor(
                        RHs[:, 2048 + hc * 512:2048 + hc * 512 + 512], fin[:, :],
                        RHs[:, hc * 512:hc * 512 + 512], op=ALU.subtract),
                       reads=[rfin], writes=[RH])
                for oc in range(CC):
                    bkW = [4, 5][rr_state["w2"] % 2]
                    rr_state["w2"] += 1
                    lhs2 = [(lambda e_hc: (lambda h, sl2=sl2, e_hc=e_hc, oc=oc:
                             SLOTs[sl2][:, h * HB + e_hc * D + oc * 128:
                                        h * HB + e_hc * D + oc * 128 + 128]))(hc)
                            for hc in range(4)]
                    rhsh = [(lambda e_hc: (lambda h, e_hc=e_hc:
                             RHs[:, h * 2048 + e_hc * 512:h * 2048 + e_hc * 512 + 512]))(hc)
                            for hc in range(4)]
                    mm3(bkW, lhs2, rhsh, [SL[sl2], RH])
                    do("vector", lambda e, oc=oc, bkW=bkW: e.tensor_tensor(
                        RYs[:, oc * TILE:(oc + 1) * TILE],
                        RYs[:, oc * TILE:(oc + 1) * TILE], PBs[bkW][:, :],
                        op=ALU.add), reads=[PB[bkW]], writes=[RY])
        if debug and tile == 0:
            dod("scalar", lambda e: e.dma_start(out=dbg["y_ffn"], in_=RYs[:, :]),
                reads=[RY], ch="dbg")

        dod("scalar", lambda e, tile=tile: e.dma_start(
            out=ysh_scr[tile], in_=RYs[:, :]), reads=[RY], writes=[RYSH], ch="ysh")


    w1flat = w1p.rearrange("a b p c -> (a b p) c")
    w3flat = w3p.rearrange("a b p c -> (a b p) c")
    w2flat = w2p.rearrange("a b p c -> (a b p) c")
    for k in range(NCHSP):
        xnb = XNs if k % 2 == 0 else RBs[:, :].bitcast(BF16)
        rxnb = RXN if k % 2 == 0 else RB
        dod("scalar", lambda e, k=k: e.dma_start(
            out=LGs[0:1, :],
            in_=tok_scr[512 * k:512 * (k + 1), :].rearrange("a b -> (a b)").unsqueeze(0)),
            reads=[RTOKSCR], writes=[RTOKROW], ch="tk")
        for s in range(SUB):
            tdep = RTOKROW.for_read() + RCONST.for_read() + PB[7].for_write()
            tmm = p.op("tensor", lambda e, s=s: e.transpose(
                PBs[7][:, s:s + 1], LGs[0:1, s * 128:(s + 1) * 128],
                IDs[0:1, 0:1]), deps=tdep)
            RTOKROW.did_read(tmm)
            PB[7].did_write(tmm)
            do("vector", lambda e, s=s: e.tensor_copy(
                TOKSBs[:, s:s + 1], PBs[7][:, s:s + 1]),
               reads=[PB[7]], writes=[RTOKSB])
            dod("gpsimd", lambda e, s=s: e.indirect_dma_start(
                out=RXs[:, s * D:(s + 1) * D], out_offset=None,
                in_=xn_scr, in_offset=IOA(TOKSBs[:, s:s + 1], 0)),
                reads=[RTOKSB, RXNSCR], writes=[RX], ch="xg")
        for i in range(SUB * CC):
            s, cc = divmod(i, CC)
            bk = 6 + (i % 2)
            do("tensor", lambda e, s=s, cc=cc, bk=bk: e.transpose(
                PBs[bk][:, 0:128],
                RXs[:, s * D + cc * 128:s * D + cc * 128 + 128], IDs[:, :]),
               reads=[RX, RCONST], writes=[PB[bk]])
            off = cc * TILE + s * 128
            do("scalar", lambda e, off=off, bk=bk, xnb=xnb: e.activation(
                xnb[:, off:off + 128], PBs[bk][:, 0:128], AF.Copy),
               reads=[PB[bk]], writes=[rxnb])
            do("vector", lambda e, off=off, bk=bk, xnb=xnb: e.tensor_tensor(
                xnb[:, HB + off:HB + off + 128], PBs[bk][:, 0:128],
                xnb[:, off:off + 128], op=ALU.subtract),
               reads=[PB[bk]], writes=[rxnb])
        csl = k % 2
        dod("scalar", lambda e, k=k, csl=csl: e.dma_start(
            out=CWs[csl][:, :],
            in_=cw2_scr[512 * k:512 * (k + 1), :].rearrange("a b -> (a b)")
            .unsqueeze(0).broadcast_to((128, TILE))),
            reads=[RCW2SCR], writes=[RCW[csl]], ch=f"cw{csl}")
        for q in range(NQ):
            kq = k * 4 + q
            sraw = next_slot()
            sl1, sl3, sl2 = next_slot(), next_slot(), next_slot()
            for sl_i, wflat in [(sl1, w1flat), (sl3, w3flat), (sl2, w2flat)]:
                dod("gpsimd", lambda e, sraw=sraw, wflat=wflat, kq=kq: e.indirect_dma_start(
                    out=slotf(sraw), out_offset=None,
                    in_=wflat, in_offset=IOA(WIDXs[:, kq:kq + 1], 0)),
                    reads=[RWIDX], writes=[SL[sraw]], ch="wg")
                do("scalar", lambda e, sraw=sraw, sl_i=sl_i: e.activation(
                    SLOTs[sl_i][:, 0:HB], slotf(sraw), AF.Copy),
                   reads=[SL[sraw]], writes=[SL[sl_i]])
                do("vector", lambda e, sraw=sraw, sl_i=sl_i: e.tensor_tensor(
                    SLOTs[sl_i][:, HB:2 * HB], slotf(sraw), SLOTs[sl_i][:, 0:HB],
                    op=ALU.subtract), reads=[SL[sraw]], writes=[SL[sl_i]])
            for hc in range(4):
                bkG = [0, 2][rr_state["gu"] % 2]
                bkU = bkG + 1
                rr_state["gu"] += 1
                lhs1 = [(lambda e_cc: (lambda h, sl1=sl1, e_cc=e_cc, hc=hc:
                         SLOTs[sl1][:, h * HB + e_cc * HQ + hc * 128:
                                    h * HB + e_cc * HQ + hc * 128 + 128]))(cc)
                        for cc in range(CC)]
                lhs3 = [(lambda e_cc: (lambda h, sl3=sl3, e_cc=e_cc, hc=hc:
                         SLOTs[sl3][:, h * HB + e_cc * HQ + hc * 128:
                                    h * HB + e_cc * HQ + hc * 128 + 128]))(cc)
                        for cc in range(CC)]
                rhs = [(lambda e_cc: (lambda h, e_cc=e_cc, xnb=xnb:
                        xnb[:, h * HB + e_cc * TILE:h * HB + e_cc * TILE + TILE]))(cc)
                       for cc in range(CC)]
                mm3(bkG, lhs1, rhs, [SL[sl1], rxnb])
                mm3(bkU, lhs3, rhs, [SL[sl3], rxnb])
                do("scalar", lambda e, bkG=bkG: e.activation(
                    SGs[0][:, :], PBs[bkG][:, :], AF.Silu),
                   reads=[PB[bkG]], writes=[RSG[0]])
                do("vector", lambda e, bkU=bkU: e.tensor_tensor(
                    SGs[1][:, :], PBs[bkU][:, :], SGs[0][:, :], op=ALU.mult),
                   reads=[PB[bkU], RSG[0]], writes=[RSG[1]])
                do("vector", lambda e, csl=csl: e.tensor_tensor(
                    SGs[0][:, :], SGs[1][:, :], CWs[csl][:, :], op=ALU.mult),
                   reads=[RSG[1], RCW[csl]], writes=[RSG[0]])
                do("scalar", lambda e, hc=hc: e.activation(
                    RHs[:, hc * 512:hc * 512 + 512], SGs[0][:, :], AF.Copy),
                   reads=[RSG[0]], writes=[RH])
                do("vector", lambda e, hc=hc: e.tensor_tensor(
                    RHs[:, 2048 + hc * 512:2048 + hc * 512 + 512], SGs[0][:, :],
                    RHs[:, hc * 512:hc * 512 + 512], op=ALU.subtract),
                   reads=[RSG[0]], writes=[RH])
            for oc in range(CC):
                bkW = [4, 5][rr_state["w2"] % 2]
                rr_state["w2"] += 1
                lhs2 = [(lambda e_hc: (lambda h, sl2=sl2, e_hc=e_hc, oc=oc:
                         SLOTs[sl2][:, h * HB + e_hc * D + oc * 128:
                                    h * HB + e_hc * D + oc * 128 + 128]))(hc)
                        for hc in range(4)]
                rhsh = [(lambda e_hc: (lambda h, e_hc=e_hc:
                         RHs[:, h * 2048 + e_hc * 512:h * 2048 + e_hc * 512 + 512]))(hc)
                        for hc in range(4)]
                mm3(bkW, lhs2, rhsh, [SL[sl2], RH])
                if q == 0:
                    do("vector", lambda e, oc=oc, bkW=bkW: e.tensor_copy(
                        RYs[:, oc * TILE:(oc + 1) * TILE], PBs[bkW][:, :]),
                       reads=[PB[bkW]], writes=[RY])
                else:
                    do("vector", lambda e, oc=oc, bkW=bkW: e.tensor_tensor(
                        RYs[:, oc * TILE:(oc + 1) * TILE],
                        RYs[:, oc * TILE:(oc + 1) * TILE], PBs[bkW][:, :],
                        op=ALU.add), reads=[PB[bkW]], writes=[RY])
        tgroup = []
        for i in range(SUB * CC):
            oc, s = divmod(i, SUB)
            bk = 6 + (i // 4) % 2
            quarter = i % 4
            do("tensor", lambda e, oc=oc, s=s, bk=bk, quarter=quarter: e.transpose(
                PBs[bk][:, quarter * 128:(quarter + 1) * 128],
                RYs[:, oc * TILE + s * 128:oc * TILE + s * 128 + 128], IDs[:, :]),
               reads=[RY, RCONST], writes=[PB[bk]])
            tgroup.append((oc, s, bk, quarter))
            if quarter == 3:
                for (oc2, s2, bk2, q2) in tgroup:
                    do("vector", lambda e, oc2=oc2, s2=s2, bk2=bk2, q2=q2: e.tensor_copy(
                        RAs[:, s2 * D + oc2 * 128:s2 * D + oc2 * 128 + 128],
                        PBs[bk2][:, q2 * 128:(q2 + 1) * 128]),
                       reads=[PB[bk2]], writes=[RA])
                tgroup = []
        dod("scalar", lambda e, k=k: e.dma_start(
            out=og_scr[512 * k:512 * (k + 1), :].rearrange("(s p) c -> p s c", p=128),
            in_=RAs[:, :]), reads=[RA], writes=[ROG], ch="og")

    for tile in range(N_TILES):
        t0 = tile * TILE
        dod("scalar", lambda e, tile=tile: e.dma_start(
            out=RDs[:, :], in_=ysh_scr[tile]), reads=[RYSH], writes=[RD], ch="ysh2")
        dod("scalar", lambda e, t0=t0: e.dma_start(
            out=RXs[:, :],
            in_=x_in[t0:t0 + TILE, :].rearrange("(s p) c -> p s c", p=128)),
            writes=[RX], ch="x")
        for s in range(SUB):
            for li, (buf, reg) in enumerate([(RAs, RA), (RBs, RB)]):
                icol = tile * 8 + s * 2 + li
                dod("gpsimd", lambda e, s=s, buf=buf, icol=icol: e.indirect_dma_start(
                    out=buf[:, s * D:(s + 1) * D], out_offset=None,
                    in_=og_scr, in_offset=IOA(IDXALLs[:, icol:icol + 1], 0)),
                    reads=[RIDXALL, ROG], writes=[reg], ch="fg")
        tgroup = []
        for i in range(SUB * CC):
            oc, s = divmod(i, SUB)
            bk = 6 + (i // 4) % 2
            quarter = i % 4
            do("tensor", lambda e, oc=oc, s=s, bk=bk, quarter=quarter: e.transpose(
                PBs[bk][:, quarter * 128:(quarter + 1) * 128],
                RDs[:, oc * TILE + s * 128:oc * TILE + s * 128 + 128], IDs[:, :]),
               reads=[RD, RCONST], writes=[PB[bk]])
            tgroup.append((oc, s, bk, quarter))
            if quarter == 3:
                for (oc2, s2, bk2, q2) in tgroup:
                    off = s2 * D + oc2 * 128
                    do("vector", lambda e, off=off, bk2=bk2, q2=q2: e.tensor_tensor(
                        RXs[:, off:off + 128], PBs[bk2][:, q2 * 128:(q2 + 1) * 128],
                        RXs[:, off:off + 128], op=ALU.add),
                       reads=[PB[bk2]], writes=[RX])
                    do("vector", lambda e, off=off: e.tensor_tensor(
                        RXs[:, off:off + 128], RXs[:, off:off + 128],
                        RAs[:, off:off + 128], op=ALU.add),
                       reads=[RA], writes=[RX])
                    do("vector", lambda e, off=off: e.tensor_tensor(
                        RXs[:, off:off + 128], RXs[:, off:off + 128],
                        RBs[:, off:off + 128], op=ALU.add),
                       reads=[RB], writes=[RX])
                tgroup = []
        st = dod("scalar", lambda e, t0=t0: e.dma_start(
            out=y_out[t0:t0 + TILE, :].rearrange("(s p) c -> p s c", p=128),
            in_=RXs[:, :]), reads=[RX], writes=[], ch="st")
        store_ops.append(st)

    p.op("vector", lambda e: e.memset(R3s[0:1, 0:1], 0.0), deps=store_ops)

    p.emit(nc, stack)
    return nc, stack, dbg


def _pack(inputs):
    f32 = np.float32
    Wqkv = np.asarray(inputs["L0_attn_Wqkv"], f32)
    Wout = np.asarray(inputs["L0_attn_Wout"], f32)
    w1 = np.asarray(inputs["L0_ffn_w1"], f32)
    w3 = np.asarray(inputs["L0_ffn_w3"], f32)
    w2 = np.asarray(inputs["L0_ffn_w2"], f32)
    sw1 = np.asarray(inputs["L0_ffn_sw1"], f32)
    sw3 = np.asarray(inputs["L0_ffn_sw3"], f32)
    sw2 = np.asarray(inputs["L0_ffn_sw2"], f32)

    wqkvp = np.ascontiguousarray(
        Wqkv.reshape(6, 512, 8, 128).transpose(0, 3, 2, 1)).reshape(6, 128, 4096)
    woutp = np.ascontiguousarray(
        Wout.reshape(2, 512, 8, 128).transpose(0, 3, 2, 1)).reshape(2, 128, 4096)

    def pack13(w):
        e = w.shape[0]
        return np.ascontiguousarray(
            w.reshape(e, NQ, HQ, 8, 128).transpose(0, 1, 4, 3, 2)).reshape(
                e, NQ, 128, 4096)

    def pack2(w):
        e = w.shape[0]
        return np.ascontiguousarray(
            w.reshape(e, D, NQ, 4, 128).transpose(0, 2, 4, 3, 1)).reshape(
                e, NQ, 128, 4096)

    w1p = np.concatenate([pack13(sw1), pack13(w1)], axis=0)
    w3p = np.concatenate([pack13(sw3), pack13(w3)], axis=0)
    w2p = np.concatenate([pack2(sw2), pack2(w2)], axis=0)

    router_w = np.asarray(inputs["L0_ffn_router_w"], f32)
    rwp = np.ascontiguousarray(
        router_w.reshape(8, 8, 128).transpose(2, 1, 0)).reshape(128, 64)
    boutp = np.ascontiguousarray(
        np.asarray(inputs["L0_attn_bout"], f32).reshape(8, 128).T)

    common = {
        "wqkvp": wqkvp, "woutp": woutp,
        "w1p": w1p, "w3p": w3p, "w2p": w2p,
        "rwp": rwp, "boutp": boutp,
        "ebias": np.pad(np.asarray(inputs["L0_ffn_expert_bias"], f32).reshape(8, 1),
                        ((0, 120), (0, 0))),
        "bl1": np.asarray(inputs["L0_attn_bl1"], f32).reshape(64, 1),
        "bl2": np.asarray(inputs["L0_attn_bl2"], f32).reshape(16, 1),
        "wl1row": np.pad(np.asarray(inputs["L0_attn_Wl1"], f32).T, ((0, 15), (0, 0))),
        "wl2t": np.ascontiguousarray(np.asarray(inputs["L0_attn_Wl2"], f32).T),
        "idx8": np.tile(np.arange(8, dtype=f32), (128, 1)),
    }
    x_all = np.ascontiguousarray(np.asarray(inputs["L0_attn_x"], f32)).reshape(
        16 * 2048, D)
    return common, x_all


def kernel(**inputs):
    common, x_all = _pack(inputs)
    nc, stack, _ = build_program(debug=False)
    in_maps = []
    for c in range(NCORES):
        m = dict(common)
        m["x"] = np.ascontiguousarray(x_all[c * T_CORE:(c + 1) * T_CORE])
        in_maps.append(m)
    res = run_bass_kernel_spmd(nc, in_maps, core_ids=list(range(NCORES)))
    out = np.concatenate([r["y"] for r in res.results], axis=0)
    return out.reshape(16, 2048, D)


# version 11
# speedup vs baseline: 1.2021x; 1.2021x over previous
import contextlib
import numpy as np
import concourse.bass as bass
import concourse.mybir as mybir
from concourse.bass_utils import run_bass_kernel_spmd

F32 = mybir.dt.float32
BF16 = mybir.dt.bfloat16
AF = mybir.ActivationFunctionType
ALU = mybir.AluOpType
AX = mybir.AxisListType

NCORES = 8
D = 1024
T_CORE = 4096
TILE = 512
N_TILES = T_CORE // TILE
SUB = TILE // 128
CC = D // 128
HID = 2048
NQ = 4
HQ = HID // NQ
HEADS = 16
DH = 64
CS = 256
NCH = TILE // CS
NEXP = 10
NSH = 2
NCHSP = 23
NSLSP = NCHSP * 512
BIG = 1.0e30
HB = 4096


class _Op:
    __slots__ = ("eng", "kind", "ch", "fn", "deps", "need_inc", "val", "seq")

    def __init__(self, eng, kind, ch, fn, deps, seq):
        self.eng, self.kind, self.ch, self.fn = eng, kind, ch, fn
        self.deps = deps
        self.need_inc = kind == "d"
        self.val = None
        self.seq = seq


class Prog:
    ENGINES = ("tensor", "vector", "scalar", "gpsimd", "sync")

    def __init__(self):
        self.ops = []

    def op(self, eng, fn, deps=()):
        o = _Op(eng, "c", "", fn, [d for d in deps if d is not None], len(self.ops))
        self.ops.append(o)
        return o

    def dma(self, eng, fn, deps=(), ch=""):
        o = _Op(eng, "d", ch, fn, [d for d in deps if d is not None], len(self.ops))
        self.ops.append(o)
        return o

    def emit(self, nc, stack):
        for o in self.ops:
            for d in o.deps:
                d.need_inc = True
        cnt = {}
        for o in self.ops:
            if o.need_inc:
                key = (o.eng, o.kind, o.ch)
                step = 16 if o.kind == "d" else 1
                cnt[key] = cnt.get(key, 0) + step
                o.val = (key, cnt[key])
        sems = {k: stack.enter_context(nc.semaphore(f"s_{k[0]}_{k[1]}{k[2]}"))
                for k in cnt}
        block = stack.enter_context(nc.Block())
        per_eng = {e: [o for o in self.ops if o.eng == e] for e in self.ENGINES}
        for eng_name in self.ENGINES:
            if not per_eng[eng_name]:
                continue

            def make(engn):
                ops = per_eng[engn]

                def emitter(eng):
                    waited = {}
                    for o in ops:
                        need = {}
                        for d in o.deps:
                            if d.val is None:
                                continue
                            key, v = d.val
                            if v > need.get(key, 0):
                                need[key] = v
                        for key, v in need.items():
                            if waited.get(key, 0) >= v:
                                continue
                            eng.wait_ge(sems[key], v)
                            waited[key] = v
                        instr = o.fn(eng)
                        if o.need_inc:
                            key, _ = o.val
                            instr.then_inc(sems[key], 16 if o.kind == "d" else 1)
                return emitter
            getattr(block, eng_name)(make(eng_name))


class Reg:
    __slots__ = ("w", "r")

    def __init__(self):
        self.w = []
        self.r = []

    def for_read(self):
        return list(self.w)

    def for_write(self):
        return list(self.w) + list(self.r)

    def did_read(self, op):
        self.r.append(op)

    def did_write(self, op):
        if self.r:
            self.w = [op]
            self.r = []
        else:
            self.w.append(op)


def build_program(debug=False):
    nc = bass.Bass("TRN2", target_bir_lowering=False)
    p = Prog()
    stack = contextlib.ExitStack()
    E = stack.enter_context

    def din(name, shape):
        return nc.dram_tensor(name, list(shape), F32, kind="ExternalInput").ap()

    x_in = din("x", (T_CORE, D))
    wqkvp = din("wqkvp", (6, 128, CC * 512))
    woutp = din("woutp", (2, 128, CC * 512))
    w1p = din("w1p", (NEXP, NQ, 128, CC * HQ))
    w3p = din("w3p", (NEXP, NQ, 128, CC * HQ))
    w2p = din("w2p", (NEXP, NQ, 128, NQ * D))
    rwp = din("rwp", (128, CC * 8))
    boutp = din("boutp", (128, CC))
    ebias_in = din("ebias", (128, 1))
    bl1_in = din("bl1", (64, 1))
    bl2_in = din("bl2", (16, 1))
    wl1row = din("wl1row", (16, 64))
    wl2t = din("wl2t", (64, 16))
    idx8_in = din("idx8", (128, 8))

    y_out = nc.dram_tensor("y", [T_CORE, D], F32, kind="ExternalOutput").ap()
    lam_scr = nc.dram_tensor("lam_scr", [HEADS, 2048], F32, kind="Internal").ap()
    cw_scr = nc.dram_tensor("cw_scr", [8, TILE], F32, kind="Internal").ap()
    wqkv_s = nc.dram_tensor("wqkv_s", [6, 128, 2 * HB], BF16, kind="Internal").ap()
    w1_s = nc.dram_tensor("w1_s", [NEXP, NQ, 128, 2 * HB], BF16, kind="Internal").ap()
    w3_s = nc.dram_tensor("w3_s", [NEXP, NQ, 128, 2 * HB], BF16, kind="Internal").ap()
    w2_s = nc.dram_tensor("w2_s", [NEXP, NQ, 128, 2 * HB], BF16, kind="Internal").ap()
    xn_scr = nc.dram_tensor("xn_scr", [T_CORE, D], F32, kind="Internal").ap()
    ysh_scr = nc.dram_tensor("ysh_scr", [N_TILES, 128, CC * TILE], F32, kind="Internal").ap()
    og_scr = nc.dram_tensor("og_scr", [NSLSP, D], F32, kind="Internal").ap()
    tok_scr = nc.dram_tensor("tok_scr", [NSLSP, 1], F32, kind="Internal").ap()
    cw2_scr = nc.dram_tensor("cw2_scr", [NSLSP, 1], F32, kind="Internal").ap()
    idx_scr = nc.dram_tensor("idx_scr", [N_TILES, TILE, 6], F32, kind="Internal").ap()

    dbg = {}
    if debug:
        def dout(name, shape):
            dbg[name] = nc.dram_tensor("dbg_" + name, list(shape), F32,
                                       kind="ExternalOutput").ap()
        dout("q_t", (128, CC * TILE))
        dout("k_t", (128, CC * TILE))
        dout("v_tok", (128, SUB * D))
        dout("attn_out", (128, CC * TILE))
        dout("y_proj", (128, CC * TILE))
        dout("lg_tok", (128, SUB * 8))
        dout("cw_fm", (8, TILE))
        dout("lam_fm", (HEADS, 2048))
        dout("y_ffn", (128, CC * TILE))

    def S(name, shape, dtype=F32):
        return E(nc.sbuf_tensor(name, list(shape), dtype))

    RXs = S("RX", (128, SUB * D))
    RAs = S("RA", (128, SUB * D))
    RBs = S("RB", (128, SUB * D))
    RDs = S("RD", (128, SUB * D))
    XNs = S("XN", (128, 2 * HB), BF16)
    RYs = S("RY", (128, CC * TILE))
    RHs = S("RH", (128, 4096), BF16)
    SGs = [S(f"SG{i}", (128, TILE)) for i in range(2)]
    LAMs = [S("LAM0", (128, TILE))]
    CWs = [S(f"CW{i}", (128, TILE)) for i in range(2)]
    RSTGs = [S("RSTG0", (128, 512))]
    STATs = S("STAT", (128, 64))
    STATAs = S("STATA", (128, 24))
    LGs = S("LG", (8, TILE))
    LGTs = S("LGT", (128, SUB * 8))
    R1s = S("R1", (128, 8))
    R2s = S("R2", (128, 8))
    R3s = S("R3", (128, 8))
    RSs = S("RS", (128, 16))
    CWTs = S("CWT", (128, SUB * 8))
    CWFs = S("CWF", (8, TILE))
    IDs = S("ID", (128, 128))
    IDX8s = S("IDX8", (128, 8))
    EBs = S("EB", (128, 1))
    BL1s = S("BL1", (64, 1))
    BL2s = S("BL2", (16, 1))
    WL1s = S("WL1", (16, 64))
    WL2s = S("WL2", (64, 16))
    RWs = S("RW", (128, CC * 8))
    BOUTs = S("BOUT", (128, CC))
    LTs = S("LT", (128, 128))
    ONESs = S("ONES", (128, 128))
    IOTAPs = S("IOTAP", (128, 1))
    M8s = S("M8", (128, 8))
    PRKs = S("PRK", (128, 8))
    IDXTs = S("IDXT", (128, SUB * 6))
    RBASEs = S("RBASE", (1, 8))
    SEGs = S("SEG", (8, 4))
    SEGRs = S("SEGR", (1, 8))
    SEGBCs = S("SEGBC", (128, 8))
    KTHs = S("KTH", (8, 24))
    EOCRs = S("EOCR", (1, 24))
    IDXALLs = S("IDXALL", (128, 8 * 8), mybir.dt.uint32)
    TOKSBs = S("TOKSB", (128, 4), mybir.dt.uint32)
    WIDXs = S("WIDX", (128, 96), mybir.dt.uint32)
    VALFs = S("VALF", (128, 2))
    NSLOT = 5
    SLOTs = [S(f"SLOT{i}", (128, 2 * HB), BF16) for i in range(NSLOT)]

    def slotf(i):
        return SLOTs[i][:, :].bitcast(F32)

    PBs = [E(nc.psum_tensor(f"pb{i}", [128, 512], F32)) for i in range(8)]

    RX, RA, RB, RD, RXN, RY, RH = (Reg() for _ in range(7))
    RSG = [Reg(), Reg()]
    RLAM = [Reg()]
    RCW = [Reg(), Reg()]
    RSTG = [Reg()]
    RSTAT = Reg()
    RSTATA3 = [Reg(), Reg(), Reg()]
    RLG, RLGT, RR1, RR2, RR3, RRS, RCWT, RCWF = (Reg() for _ in range(8))
    RCONST = Reg()
    RLAMSCR, RCWSCR, RWSPLIT = Reg(), Reg(), Reg()
    RXNSCR, RYSH, ROG, RTOKSCR, RCW2SCR, RIDXSCR = (Reg() for _ in range(6))
    RM8, RPRK, RIDXT, RRBASE, RSEG, RSEGR, RSEGBC = (Reg() for _ in range(7))
    RTHR, RKTH, REOCR, RIDXALL, RTOKSB, RTOKROW, RWIDX, RVAL = (Reg() for _ in range(8))
    PB = [Reg() for _ in range(8)]
    SL = [Reg() for _ in range(NSLOT)]

    def do(eng, fn, reads=(), writes=(), extra=()):
        deps = [d for r in reads for d in r.for_read()]
        deps += [d for w in writes for d in w.for_write()]
        deps += [d for d in extra if d is not None]
        o = p.op(eng, fn, deps)
        for r in reads:
            r.did_read(o)
        for w in writes:
            w.did_write(o)
        return o

    def dod(eng, fn, reads=(), writes=(), extra=(), ch=""):
        deps = [d for r in reads for d in r.for_read()]
        deps += [d for w in writes for d in w.for_write()]
        deps += [d for d in extra if d is not None]
        o = p.dma(eng, fn, deps, ch=ch)
        for r in reads:
            r.did_read(o)
        for w in writes:
            w.did_write(o)
        return o

    for t, ap_ in [(IDX8s, idx8_in), (EBs, ebias_in),
                   (BL1s, bl1_in), (BL2s, bl2_in),
                   (WL1s, wl1row), (WL2s, wl2t),
                   (RWs, rwp), (BOUTs, boutp)]:
        o = dod("sync", lambda e, t=t, a=ap_: e.dma_start(out=t[:, :], in_=a),
                ch="c0")
        RCONST.w.append(o)

    o = do("gpsimd", lambda e: e.iota(IDs[:, :], pattern=[[1, 128]], base=0,
                                      channel_multiplier=-1,
                                      allow_small_or_imprecise_dtypes=True),
           writes=[RCONST])
    do("vector", lambda e: e.tensor_scalar(IDs[:, :], IDs[:, :], 0.0, None,
                                           op0=ALU.is_equal), writes=[RCONST])
    o = do("gpsimd", lambda e: e.iota(LTs[:, :], pattern=[[1, 128]], base=0,
                                      channel_multiplier=-1,
                                      allow_small_or_imprecise_dtypes=True),
           writes=[RCONST])
    do("vector", lambda e: e.tensor_scalar(LTs[:, :], LTs[:, :], 0.0, None,
                                           op0=ALU.is_gt), writes=[RCONST])
    do("vector", lambda e: e.memset(ONESs[:, :], 1.0), writes=[RCONST])
    do("gpsimd", lambda e: e.iota(IOTAPs[:, :], pattern=[[0, 1]], base=0,
                                  channel_multiplier=1,
                                  allow_small_or_imprecise_dtypes=True),
       writes=[RCONST])
    do("vector", lambda e: e.memset(RBASEs[:, :], 0.0), writes=[RRBASE])
    zop = do("vector", lambda e: e.memset(M8s[:, :], 0.0), writes=[RM8])
    for zi in range(NCHSP):
        dod("scalar", lambda e, zi=zi: e.dma_start(
            out=tok_scr[zi * 512:(zi + 1) * 512].rearrange("(p a) b -> p (a b)", p=128),
            in_=M8s[:, 0:4]), reads=[RM8], writes=[RTOKSCR], ch="z")
        dod("scalar", lambda e, zi=zi: e.dma_start(
            out=cw2_scr[zi * 512:(zi + 1) * 512].rearrange("(p a) b -> p (a b)", p=128),
            in_=M8s[:, 0:4]), reads=[RM8], writes=[RCW2SCR], ch="z")

    h1v = RYs[0:64, 0:2048]
    lamv = RYs[0:16, 2048:4096]
    lp_hi = RDs[64:65, 0:2048]
    lpv = RDs[0:1, 0:2048]
    o = do("gpsimd", lambda e: e.iota(lp_hi, pattern=[[1, 2048]], base=1,
                                      channel_multiplier=0,
                                      allow_small_or_imprecise_dtypes=True),
           writes=[RD])
    do("scalar", lambda e: e.activation(lpv, lp_hi, AF.Ln),
       reads=[RD], writes=[RD])
    for sl in range(4):
        mm = do("tensor", lambda e, sl=sl: e.matmul(
            PBs[0][0:64, :], WL1s[0:1, :], lpv[0:1, sl * 512:(sl + 1) * 512],
            start=True, stop=True), reads=[RCONST, RD], writes=[PB[0]])
        do("scalar", lambda e, sl=sl: e.activation(
            h1v[:, sl * 512:(sl + 1) * 512], PBs[0][0:64, :], AF.Relu,
            bias=BL1s[:, 0:1]), reads=[PB[0], RCONST], writes=[RY])
    for sl in range(4):
        mm = do("tensor", lambda e, sl=sl: e.matmul(
            PBs[1][0:16, :], WL2s[:, :], h1v[:, sl * 512:(sl + 1) * 512],
            start=True, stop=True), reads=[RCONST, RY], writes=[PB[1]])
        do("scalar", lambda e, sl=sl: e.activation(
            lamv[:, sl * 512:(sl + 1) * 512], PBs[1][0:16, :], AF.Sigmoid,
            bias=BL2s[:, 0:1]), reads=[PB[1], RCONST], writes=[RY])
    dod("scalar", lambda e: e.dma_start(out=lam_scr, in_=lamv[:, :]),
        reads=[RY], writes=[RLAMSCR], ch="lsm")
    if debug:
        dod("scalar", lambda e: e.dma_start(out=dbg["lam_fm"], in_=lamv[:, :]),
            reads=[RY], ch="dbg")

    units = [(wqkvp[ms], wqkv_s[ms]) for ms in range(6)]
    for ei in range(NSH):
        for q in range(NQ):
            units.append((w1p[ei, q], w1_s[ei, q]))
            units.append((w3p[ei, q], w3_s[ei, q]))
            units.append((w2p[ei, q], w2_s[ei, q]))
    def record_split_unit(u, src_, dst):
        a, b = (u % 2) * 2, (u % 2) * 2 + 1
        dod("sync", lambda e, src_=src_, a=a: e.dma_start(out=slotf(a), in_=src_),
            writes=[SL[a]], ch=f"sl{a}")
        do("scalar", lambda e, a=a, b=b: e.activation(
            SLOTs[b][:, 0:HB], slotf(a), AF.Copy),
           reads=[SL[a]], writes=[SL[b]])
        do("vector", lambda e, a=a, b=b: e.tensor_tensor(
            SLOTs[b][:, HB:2 * HB], slotf(a), SLOTs[b][:, 0:HB], op=ALU.subtract),
           reads=[SL[a]], writes=[SL[b]])
        dod("scalar", lambda e, b=b, dst=dst: e.dma_start(out=dst, in_=SLOTs[b][:, :]),
            reads=[SL[b]], writes=[RWSPLIT], ch=f"ws{b}")

    for u, (src_, dst) in enumerate(units[:30]):
        record_split_unit(u, src_, dst)
    routed_units = units[30:]

    qkv_bank_rr = [4, 5]
    att_bank_rr = [3, 4, 5, 6, 7]
    rr_state = {"qkv": 0, "att": 0, "slot": 0, "gu": 0, "w2": 0, "stg": 0}

    def next_bank(pool, key):
        i = pool[rr_state[key] % len(pool)]
        rr_state[key] += 1
        return i

    def next_slot():
        i = rr_state["slot"] % NSLOT
        rr_state["slot"] += 1
        return i

    from concourse.bass import IndirectOffsetOnAxis as IOA

    def record_index_finalize():
        cdep = RRBASE.for_read() + RCONST.for_read() + PB[6].for_write()
        c1 = p.op("tensor", lambda e: e.matmul(PBs[6][0:8, 0:1], RBASEs[0:1, :],
                                               ONESs[0:1, 0:1], start=True, stop=True),
                  deps=cdep)
        RRBASE.did_read(c1)
        PB[6].did_write(c1)
        do("vector", lambda e: e.tensor_copy(SEGs[0:8, 0:1], PBs[6][0:8, 0:1]),
           reads=[PB[6]], writes=[RSEG])
        do("gpsimd", lambda e: e.iota(KTHs[0:8, 0:16], pattern=[[512, 16]], base=0,
                                      channel_multiplier=0,
                                      allow_small_or_imprecise_dtypes=True),
           writes=[RKTH])
        do("vector", lambda e: e.tensor_scalar(KTHs[0:8, 0:16], KTHs[0:8, 0:16],
                                               SEGs[0:8, 0:1], None, op0=ALU.is_lt),
           reads=[RSEG], writes=[RKTH])
        do("vector", lambda e: e.tensor_reduce(SEGs[0:8, 1:2], KTHs[0:8, 0:16],
                                               axis=AX.X, op=ALU.add),
           reads=[RKTH], writes=[RSEG])
        do("vector", lambda e: e.tensor_scalar(SEGs[0:8, 1:2], SEGs[0:8, 1:2], 512.0,
                                               None, op0=ALU.mult), writes=[RSEG])
        sdep = RSEG.for_read() + RCONST.for_read() + PB[6].for_write()
        smm = p.op("tensor", lambda e: e.matmul(PBs[6][0:8, 4:5], LTs[0:8, 0:8],
                                                SEGs[0:8, 1:2], start=True, stop=True),
                   deps=sdep)
        RSEG.did_read(smm)
        PB[6].did_write(smm)
        do("vector", lambda e: e.tensor_copy(SEGs[0:8, 2:3], PBs[6][0:8, 4:5]),
           reads=[PB[6]], writes=[RSEG])
        do("vector", lambda e: e.tensor_tensor(SEGs[0:8, 3:4], SEGs[0:8, 2:3],
                                               SEGs[0:8, 1:2], op=ALU.add), writes=[RSEG])
        rdep = RSEG.for_read() + RCONST.for_read() + PB[6].for_write()
        rmm = p.op("tensor", lambda e: e.matmul(PBs[6][0:1, 8:16], SEGs[0:8, 2:3],
                                                IDs[0:8, 0:8], start=True, stop=True),
                   deps=rdep)
        RSEG.did_read(rmm)
        PB[6].did_write(rmm)
        do("vector", lambda e: e.tensor_copy(SEGRs[0:1, :], PBs[6][0:1, 8:16]),
           reads=[PB[6]], writes=[RSEGR])
        bdep = RSEGR.for_read() + RCONST.for_read() + PB[6].for_write()
        bmm = p.op("tensor", lambda e: e.matmul(PBs[6][:, 16:24], ONESs[0:1, :],
                                                SEGRs[0:1, :], start=True, stop=True),
                   deps=bdep)
        RSEGR.did_read(bmm)
        PB[6].did_write(bmm)
        do("vector", lambda e: e.tensor_copy(SEGBCs[:, :], PBs[6][:, 16:24]),
           reads=[PB[6]], writes=[RSEGBC])
        do("gpsimd", lambda e: e.iota(KTHs[:, :], pattern=[[512, 24]], base=0,
                                      channel_multiplier=0,
                                      allow_small_or_imprecise_dtypes=True),
           writes=[RKTH])
        do("vector", lambda e: e.tensor_scalar(IDXTs[0:8, 0:24], KTHs[0:8, :],
                                               SEGs[0:8, 2:3], None, op0=ALU.is_ge),
           reads=[RKTH, RSEG], writes=[RIDXT])
        do("vector", lambda e: e.tensor_scalar(KTHs[0:8, :], KTHs[0:8, :],
                                               SEGs[0:8, 3:4], None, op0=ALU.is_lt),
           reads=[RSEG], writes=[RKTH])
        do("vector", lambda e: e.tensor_tensor(IDXTs[0:8, 0:24], IDXTs[0:8, 0:24],
                                               KTHs[0:8, :], op=ALU.mult),
           reads=[RKTH], writes=[RIDXT])
        edep = RIDXT.for_read() + RCONST.for_read() + PB[6].for_write()
        emm = p.op("tensor", lambda e: e.matmul(PBs[6][0:1, 24:48], IOTAPs[0:8, 0:1],
                                                IDXTs[0:8, 0:24], start=True, stop=True),
                   deps=edep)
        RIDXT.did_read(emm)
        PB[6].did_write(emm)
        do("vector", lambda e: e.tensor_copy(EOCRs[0:1, :], PBs[6][0:1, 24:48]),
           reads=[PB[6]], writes=[REOCR])
        for k in range(NCHSP):
            wdep = REOCR.for_read() + RCONST.for_read() + PB[7].for_write()
            wmm = p.op("tensor", lambda e, k=k: e.matmul(
                PBs[7][:, 8:9], ONESs[0:1, :], EOCRs[0:1, k:k + 1],
                start=True, stop=True), deps=wdep)
            REOCR.did_read(wmm)
            PB[7].did_write(wmm)
            for q in range(NQ):
                do("vector", lambda e, q=q: e.tensor_scalar(
                    VALFs[:, 0:1], PBs[7][:, 8:9], 512.0, float(q * 128 + NSH * NQ * 128),
                    op0=ALU.mult, op1=ALU.add), reads=[PB[7]], writes=[RVAL])
                do("vector", lambda e: e.tensor_tensor(
                    VALFs[:, 0:1], VALFs[:, 0:1], IOTAPs[:, :], op=ALU.add),
                   reads=[RCONST], writes=[RVAL])
                do("vector", lambda e, k=k, q=q: e.tensor_copy(
                    WIDXs[:, k * 4 + q:k * 4 + q + 1], VALFs[:, 0:1]),
                   reads=[RVAL], writes=[RWIDX])

        for tile in range(N_TILES):
            t0 = tile * TILE
            dod("scalar", lambda e, tile=tile: e.dma_start(
                out=IDXTs[:, :].rearrange("p (s r) -> p s r", s=SUB),
                in_=idx_scr[tile].rearrange("(s p) r -> p s r", p=128)),
                reads=[RIDXSCR], writes=[RIDXT], ch="ix")
            for s in range(SUB):
                for li in range(2):
                    icol = tile * 8 + s * 2 + li
                    do("vector", lambda e, s=s, li=li: e.tensor_scalar(
                        R1s[:, :], IDX8s[:, :], IDXTs[:, s * 6 + 2 + li:s * 6 + 3 + li],
                        None, op0=ALU.is_equal),
                       reads=[RCONST, RIDXT], writes=[RR1])
                    do("vector", lambda e: e.tensor_tensor(
                        R3s[:, :], R1s[:, :], SEGBCs[:, :], op=ALU.mult),
                       reads=[RR1, RSEGBC], writes=[RR3])
                    do("vector", lambda e: e.tensor_reduce(
                        RSs[:, 0:1], R3s[:, :], axis=AX.X, op=ALU.add),
                       reads=[RR3], writes=[RRS])
                    do("vector", lambda e, s=s, li=li: e.tensor_tensor(
                        VALFs[:, 0:1], RSs[:, 0:1], IDXTs[:, s * 6 + li:s * 6 + 1 + li],
                        op=ALU.add), reads=[RRS, RIDXT], writes=[RVAL])
                    do("vector", lambda e, icol=icol: e.tensor_copy(
                        IDXALLs[:, icol:icol + 1], VALFs[:, 0:1]),
                       reads=[RVAL], writes=[RIDXALL])
                    do("vector", lambda e, t0=t0, s=s: e.tensor_scalar(
                        VALFs[:, 1:2], IOTAPs[:, :], float(t0 + s * 128), None,
                        op0=ALU.add), reads=[RCONST], writes=[RVAL])
                    dod("gpsimd", lambda e, icol=icol: e.indirect_dma_start(
                        out=tok_scr, out_offset=IOA(IDXALLs[:, icol:icol + 1], 0),
                        in_=VALFs[:, 1:2], in_offset=None),
                        reads=[RVAL, RIDXALL], writes=[RTOKSCR], ch="sc")
                    dod("gpsimd", lambda e, icol=icol, s=s, li=li: e.indirect_dma_start(
                        out=cw2_scr, out_offset=IOA(IDXALLs[:, icol:icol + 1], 0),
                        in_=IDXTs[:, s * 6 + 4 + li:s * 6 + 5 + li], in_offset=None),
                        reads=[RIDXT, RIDXALL], writes=[RCW2SCR], ch="sc")



    store_ops = []

    for tile in range(N_TILES):
        t0 = tile * TILE
        pos0 = t0 % 2048

        dod("scalar", lambda e, t0=t0: e.dma_start(
            out=RXs[:, :],
            in_=x_in[t0:t0 + TILE, :].rearrange("(s p) c -> p s c", p=128)),
            writes=[RX], ch="x")

        for s in range(SUB):
            xsl = RXs[:, s * D:(s + 1) * D]
            c0 = 16 * s
            do("vector", lambda e, xsl=xsl, c0=c0: e.tensor_reduce(
                STATs[:, c0:c0 + 8],
                xsl.rearrange("p (a b) -> p a b", a=8), axis=AX.X, op=ALU.add),
               reads=[RX], writes=[RSTAT])
            do("vector", lambda e, c0=c0: e.tensor_reduce(
                STATs[:, c0 + 8:c0 + 9], STATs[:, c0:c0 + 8], axis=AX.X, op=ALU.add),
               writes=[RSTAT])
            do("vector", lambda e, c0=c0: e.tensor_scalar(
                STATs[:, c0 + 9:c0 + 10], STATs[:, c0 + 8:c0 + 9], -1.0 / D, None,
                op0=ALU.mult), writes=[RSTAT])
            do("scalar", lambda e, xsl=xsl, c0=c0: e.activation(
                RDs[:, 0:D], xsl, AF.Square, bias=STATs[:, c0 + 9:c0 + 10]),
               reads=[RX, RSTAT], writes=[RD])
            do("vector", lambda e, c0=c0: e.tensor_reduce(
                STATs[:, c0:c0 + 8],
                RDs[:, 0:D].rearrange("p (a b) -> p a b", a=8), axis=AX.X,
                op=ALU.add), reads=[RD], writes=[RSTAT])
            do("vector", lambda e, c0=c0: e.tensor_reduce(
                STATs[:, c0 + 10:c0 + 11], STATs[:, c0:c0 + 8], axis=AX.X,
                op=ALU.add), writes=[RSTAT])
            do("vector", lambda e, c0=c0: e.tensor_scalar(
                STATs[:, c0 + 11:c0 + 12], STATs[:, c0 + 10:c0 + 11], 1.0 / D, 1e-5,
                op0=ALU.mult, op1=ALU.add), writes=[RSTAT])
            do("scalar", lambda e, c0=c0: e.activation(
                STATs[:, c0 + 12:c0 + 13], STATs[:, c0 + 11:c0 + 12], AF.Sqrt),
               writes=[RSTAT])
            do("vector", lambda e, c0=c0: e.reciprocal(
                STATs[:, c0 + 13:c0 + 14], STATs[:, c0 + 12:c0 + 13]),
               writes=[RSTAT])
            do("vector", lambda e, xsl=xsl, s=s, c0=c0: e.tensor_scalar(
                RAs[:, s * D:(s + 1) * D], xsl, STATs[:, c0 + 9:c0 + 10],
                STATs[:, c0 + 13:c0 + 14], op0=ALU.add, op1=ALU.mult),
               reads=[RX, RSTAT], writes=[RA])

        def xn_3d(h):
            return XNs[:, h * HB:(h + 1) * HB].rearrange(
                "p (c t) -> p c t", c=CC)

        for g4 in range(SUB * CC // 4):
            s, ch4 = divmod(g4, 2)
            cc0 = ch4 * 4
            bk = 6 + (g4 % 2)
            for q in range(4):
                cc = cc0 + q
                do("tensor", lambda e, s=s, cc=cc, bk=bk, q=q: e.transpose(
                    PBs[bk][:, q * 128:(q + 1) * 128],
                    RAs[:, s * D + cc * 128:s * D + cc * 128 + 128], IDs[:, :]),
                    reads=[RA, RCONST], writes=[PB[bk]])
            pview = PBs[bk][:, :].rearrange("p (q c) -> p q c", q=4)
            do("scalar", lambda e, s=s, cc0=cc0, bk=bk, pview=pview: e.activation(
                xn_3d(0)[:, cc0:cc0 + 4, s * 128:(s + 1) * 128], pview, AF.Copy),
                reads=[PB[bk]], writes=[RXN])
            do("vector", lambda e, s=s, cc0=cc0, bk=bk, pview=pview: e.tensor_tensor(
                xn_3d(1)[:, cc0:cc0 + 4, s * 128:(s + 1) * 128], pview,
                xn_3d(0)[:, cc0:cc0 + 4, s * 128:(s + 1) * 128], op=ALU.subtract),
               reads=[PB[bk], RXN], writes=[RXN])
            do("vector", lambda e, bk=bk: e.tensor_copy(
                RSTGs[0][:, :], PBs[bk][:, :]),
               reads=[PB[bk]], writes=[RSTG[0]])
            for q in range(4):
                cc = cc0 + q
                lg_deps = (RCONST.for_read() + RSTG[0].for_read()
                           + (PB[3].for_write() if cc == 0 else []))
                lgmm = p.op("tensor", lambda e, s=s, cc=cc, q=q: e.matmul(
                    PBs[3][0:8, s * 128:(s + 1) * 128], RWs[:, cc * 8:(cc + 1) * 8],
                    RSTGs[0][:, q * 128:(q + 1) * 128],
                    start=(cc == 0), stop=(cc == CC - 1)),
                    deps=lg_deps)
                RSTG[0].did_read(lgmm)
                if cc == CC - 1:
                    PB[3].did_write(lgmm)
        do("vector", lambda e: e.tensor_scalar(
            LGs[:, :], PBs[3][0:8, :], EBs[0:8, 0:1], None, op0=ALU.add),
           reads=[PB[3], RCONST], writes=[RLG])

        dod("scalar", lambda e, t0=t0: e.dma_start(
            out=xn_scr[t0:t0 + TILE, :].rearrange("(s p) c -> p s c", p=128),
            in_=RAs[:, :]), reads=[RA], writes=[RXNSCR], ch="xn")

        for tc in range(SUB):
            do("tensor", lambda e, tc=tc: e.transpose(
                PBs[6][:, tc * 8:(tc + 1) * 8],
                LGs[:, tc * 128:(tc + 1) * 128], IDs[0:8, 0:8]),
               reads=[RLG, RCONST], writes=[PB[6]])
        do("vector", lambda e: e.tensor_copy(
            LGTs[:, :], PBs[6][:, 0:SUB * 8]), reads=[PB[6]], writes=[RLGT])
        if debug and tile == 0:
            dod("scalar", lambda e: e.dma_start(out=dbg["lg_tok"], in_=LGTs[:, :]),
                reads=[RLGT], ch="dbg")
        for tc in range(SUB):
            lg = LGTs[:, tc * 8:(tc + 1) * 8]
            do("vector", lambda e, lg=lg: e.tensor_reduce(
                RSs[:, 0:1], lg, axis=AX.X, op=ALU.max), reads=[RLGT], writes=[RRS])
            do("vector", lambda e, lg=lg: e.tensor_scalar(
                R1s[:, :], lg, RSs[:, 0:1], None, op0=ALU.is_ge),
               reads=[RLGT, RRS], writes=[RR1])
            do("vector", lambda e: e.scalar_tensor_tensor(
                R2s[:, :], IDX8s[:, :], 9.0, R1s[:, :], op0=ALU.subtract, op1=ALU.mult),
               reads=[RCONST, RR1], writes=[RR2])
            do("vector", lambda e: e.tensor_reduce(
                RSs[:, 1:2], R2s[:, :], axis=AX.X, op=ALU.min),
               reads=[RR2], writes=[RRS])
            do("vector", lambda e: e.tensor_scalar(
                RSs[:, 1:2], RSs[:, 1:2], 9.0, None, op0=ALU.add), writes=[RRS])
            do("vector", lambda e: e.tensor_scalar(
                R1s[:, :], IDX8s[:, :], RSs[:, 1:2], None, op0=ALU.is_equal),
               reads=[RCONST, RRS], writes=[RR1])
            do("vector", lambda e, lg=lg: e.scalar_tensor_tensor(
                R3s[:, :], R1s[:, :], -BIG, lg, op0=ALU.mult, op1=ALU.add),
               reads=[RR1, RLGT], writes=[RR3])
            do("vector", lambda e: e.tensor_reduce(
                RSs[:, 2:3], R3s[:, :], axis=AX.X, op=ALU.max),
               reads=[RR3], writes=[RRS])
            do("vector", lambda e: e.tensor_scalar(
                R2s[:, :], R3s[:, :], RSs[:, 2:3], None, op0=ALU.is_ge),
               reads=[RR3, RRS], writes=[RR2])
            do("vector", lambda e: e.scalar_tensor_tensor(
                R2s[:, :], IDX8s[:, :], 9.0, R2s[:, :], op0=ALU.subtract, op1=ALU.mult),
               reads=[RCONST], writes=[RR2])
            do("vector", lambda e: e.tensor_reduce(
                RSs[:, 3:4], R2s[:, :], axis=AX.X, op=ALU.min),
               reads=[RR2], writes=[RRS])
            do("vector", lambda e: e.tensor_scalar(
                RSs[:, 3:4], RSs[:, 3:4], 9.0, None, op0=ALU.add), writes=[RRS])
            do("vector", lambda e: e.tensor_scalar(
                R2s[:, :], IDX8s[:, :], RSs[:, 3:4], None, op0=ALU.is_equal),
               reads=[RCONST, RRS], writes=[RR2])
            do("vector", lambda e: e.tensor_tensor(
                M8s[:, :], R1s[:, :], R2s[:, :], op=ALU.add),
               reads=[RR1, RR2], writes=[RM8])
            pf_deps = (RCONST.for_read() + RM8.for_read() + RRBASE.for_read()
                       + PB[3].for_write())
            pf1 = p.op("tensor", lambda e, tc=tc: e.matmul(
                PBs[3][:, tc * 8:(tc + 1) * 8], LTs[:, :], M8s[:, :],
                start=True, stop=False), deps=pf_deps)
            pf2 = p.op("tensor", lambda e, tc=tc: e.matmul(
                PBs[3][:, tc * 8:(tc + 1) * 8], ONESs[0:1, :], RBASEs[0:1, :],
                start=False, stop=True), deps=[])
            RM8.did_read(pf2)
            RRBASE.did_read(pf2)
            PB[3].did_write(pf2)
            cnt_deps = RM8.for_read() + RCONST.for_read() + PB[3].for_write()
            cntmm = p.op("tensor", lambda e, tc=tc: e.matmul(
                PBs[3][0:1, 32 + tc * 8:40 + tc * 8], ONESs[:, 0:1], M8s[:, :],
                start=True, stop=True), deps=cnt_deps)
            RM8.did_read(cntmm)
            PB[3].did_write(cntmm)
            do("vector", lambda e, tc=tc: e.tensor_copy(
                PRKs[:, :], PBs[3][:, tc * 8:(tc + 1) * 8]),
               reads=[PB[3]], writes=[RPRK])
            do("vector", lambda e: e.tensor_tensor(
                R3s[:, :], R1s[:, :], PRKs[:, :], op=ALU.mult),
               reads=[RR1, RPRK], writes=[RR3])
            do("vector", lambda e, tc=tc: e.tensor_reduce(
                IDXTs[:, tc * 6:tc * 6 + 1], R3s[:, :], axis=AX.X, op=ALU.add),
               reads=[RR3], writes=[RIDXT])
            do("vector", lambda e: e.tensor_tensor(
                R3s[:, :], R2s[:, :], PRKs[:, :], op=ALU.mult),
               reads=[RR2, RPRK], writes=[RR3])
            do("vector", lambda e, tc=tc: e.tensor_reduce(
                IDXTs[:, tc * 6 + 1:tc * 6 + 2], R3s[:, :], axis=AX.X, op=ALU.add),
               reads=[RR3], writes=[RIDXT])
            do("vector", lambda e, tc=tc: e.tensor_tensor(
                RBASEs[0:1, :], RBASEs[0:1, :], PBs[3][0:1, 32 + tc * 8:40 + tc * 8],
                op=ALU.add), reads=[PB[3]], writes=[RRBASE])
            do("vector", lambda e, tc=tc: e.tensor_copy(
                IDXTs[:, tc * 6 + 2:tc * 6 + 3], RSs[:, 1:2]),
               reads=[RRS], writes=[RIDXT])
            do("vector", lambda e, tc=tc: e.tensor_copy(
                IDXTs[:, tc * 6 + 3:tc * 6 + 4], RSs[:, 3:4]),
               reads=[RRS], writes=[RIDXT])
            do("vector", lambda e: e.tensor_tensor(
                RSs[:, 4:5], RSs[:, 0:1], RSs[:, 2:3], op=ALU.subtract), writes=[RRS])
            do("scalar", lambda e: e.activation(
                RSs[:, 4:5], RSs[:, 4:5], AF.Sigmoid), writes=[RRS])
            do("vector", lambda e: e.tensor_scalar(
                RSs[:, 5:6], RSs[:, 4:5], -1.0, 1.0, op0=ALU.mult, op1=ALU.add),
               writes=[RRS])
            do("vector", lambda e: e.tensor_scalar(
                R1s[:, :], R1s[:, :], RSs[:, 4:5], None, op0=ALU.mult),
               reads=[RRS], writes=[RR1])
            do("vector", lambda e, tc=tc: e.scalar_tensor_tensor(
                CWTs[:, tc * 8:(tc + 1) * 8], R2s[:, :], RSs[:, 5:6], R1s[:, :],
                op0=ALU.mult, op1=ALU.add),
               reads=[RR2, RRS, RR1], writes=[RCWT])
            do("vector", lambda e, tc=tc: e.tensor_copy(
                IDXTs[:, tc * 6 + 4:tc * 6 + 5], RSs[:, 4:5]),
               reads=[RRS], writes=[RIDXT])
            do("vector", lambda e, tc=tc: e.tensor_copy(
                IDXTs[:, tc * 6 + 5:tc * 6 + 6], RSs[:, 5:6]),
               reads=[RRS], writes=[RIDXT])
        for tc in range(SUB):
            do("tensor", lambda e, tc=tc: e.transpose(
                PBs[7][0:8, tc * 128:(tc + 1) * 128],
                CWTs[:, tc * 8:(tc + 1) * 8], IDs[:, :]),
               reads=[RCWT, RCONST], writes=[PB[7]])
        do("vector", lambda e: e.tensor_copy(CWFs[:, :], PBs[7][0:8, 0:TILE]),
           reads=[PB[7]], writes=[RCWF])
        dod("scalar", lambda e: e.dma_start(out=cw_scr, in_=CWFs[:, :]),
            reads=[RCWF], writes=[RCWSCR], ch="sm")
        dod("scalar", lambda e, tile=tile: e.dma_start(
            out=idx_scr[tile].rearrange("(s p) r -> p s r", p=128),
            in_=IDXTs[:, :].rearrange("p (s r) -> p s r", s=SUB)),
            reads=[RIDXT], writes=[RIDXSCR], ch="sm2")
        if tile == N_TILES - 1:
            record_index_finalize()
        if debug and tile == 0:
            dod("scalar", lambda e: e.dma_start(out=dbg["cw_fm"], in_=CWFs[:, :]),
                reads=[RCWF], ch="dbg")

        def mm3(bk, lhs_pairs, rhs_pairs, lreads, out_ap=None):
            n = len(lhs_pairs)
            first_deps = ([d for r in lreads for d in r.for_read()]
                          + PB[bk].for_write())
            last = None
            for j in range(n):
                for pi, (lh, rh) in enumerate([(0, 0), (0, 1), (1, 0)]):
                    st_ = (j == 0 and pi == 0)
                    sp_ = (j == n - 1 and pi == 2)
                    last = p.op("tensor",
                                lambda e, j=j, lh=lh, rh=rh, st_=st_, sp_=sp_: e.matmul(
                                    (out_ap or PBs[bk][:, :]),
                                    lhs_pairs[j](lh), rhs_pairs[j](rh),
                                    start=st_, stop=sp_),
                                deps=first_deps if st_ else [])
            for r in lreads:
                r.did_read(last)
            PB[bk].did_write(last)
            return last

        def xn_half(cc, h):
            return XNs[:, h * HB + cc * TILE:h * HB + cc * TILE + TILE]

        def xn_half_tc(cc, tc, h):
            o = h * HB + cc * TILE + tc * 128
            return XNs[:, o:o + 128]

        for ms in range(6):
            sl_i = next_slot()
            dod("sync", lambda e, ms=ms, sl_i=sl_i: e.dma_start(
                out=SLOTs[sl_i][:, :], in_=wqkv_s[ms]), writes=[SL[sl_i]],
                reads=[RWSPLIT] if tile == 0 else (), ch=f"sl{sl_i}")
            if ms < 4:
                for mc in range(4):
                    mg = ms * 4 + mc
                    bk = next_bank(qkv_bank_rr, "qkv")
                    lhs = [(lambda e_cc: (lambda h, sl_i=sl_i, e_cc=e_cc, mc=mc:
                            SLOTs[sl_i][:, h * HB + e_cc * 512 + mc * 128:
                                         h * HB + e_cc * 512 + mc * 128 + 128]))(cc)
                           for cc in range(CC)]
                    rhs = [(lambda e_cc: (lambda h, e_cc=e_cc: xn_half(e_cc, h)))(cc)
                           for cc in range(CC)]
                    mm3(bk, lhs, rhs, [SL[sl_i], RXN])
                    if mg < 8:
                        hp = mg
                        do("scalar", lambda e, hp=hp, bk=bk: e.activation(
                            RAs[:, hp * TILE:(hp + 1) * TILE], PBs[bk][:, :],
                            AF.Copy, scale=0.125),
                           reads=[PB[bk]], writes=[RA])
                    else:
                        hp = mg - 8
                        lsl = 0
                        for half in range(2):
                            dod("scalar", lambda e, hp=hp, lsl=lsl, pos0=pos0, half=half: e.dma_start(
                                out=LAMs[lsl][half * 64:(half + 1) * 64, :],
                                in_=lam_scr[2 * hp + half:2 * hp + half + 1,
                                            pos0:pos0 + TILE].broadcast_to((64, TILE))),
                                reads=[RLAMSCR], writes=[RLAM[lsl]], ch=f"lam{lsl}")
                        do("vector", lambda e, hp=hp, lsl=lsl, bk=bk: e.tensor_tensor(
                            RBs[:, hp * TILE:(hp + 1) * TILE], PBs[bk][:, :],
                            LAMs[lsl][:, :], op=ALU.mult),
                           reads=[PB[bk], RLAM[lsl]], writes=[RB])
            else:
                vs = ms - 4
                for tc in range(SUB):
                    bk = next_bank(qkv_bank_rr, "qkv")
                    lhs = [(lambda e_cc: (lambda h, e_cc=e_cc, tc=tc:
                            xn_half_tc(e_cc, tc, h)))(cc) for cc in range(CC)]
                    rhs = [(lambda e_cc: (lambda h, sl_i=sl_i, e_cc=e_cc:
                            SLOTs[sl_i][:, h * HB + e_cc * 512:
                                         h * HB + e_cc * 512 + 512]))(cc)
                           for cc in range(CC)]
                    mm3(bk, lhs, rhs, [SL[sl_i], RXN])
                    do("vector", lambda e, tc=tc, vs=vs, bk=bk: e.tensor_copy(
                        RXs[:, tc * D + vs * 512:tc * D + vs * 512 + 512],
                        PBs[bk][:, :]), reads=[PB[bk]], writes=[RX])
        if debug and tile == 0:
            dod("scalar", lambda e: e.dma_start(out=dbg["q_t"], in_=RAs[:, :]),
                reads=[RA], ch="dbg")
            dod("scalar", lambda e: e.dma_start(out=dbg["k_t"], in_=RBs[:, :]),
                reads=[RB], ch="dbg")
            dod("scalar", lambda e: e.dma_start(out=dbg["v_tok"], in_=RXs[:, :]),
                reads=[RX], ch="dbg")

        def att_ap(pr, a, b):
            return RYs[:, pr * 1024 + a:pr * 1024 + b]

        def atn_ap(pr, a, b):
            return RYs[:, pr * 1024 + 512 + a:pr * 1024 + 512 + b]

        att_reg = [RY, RY, RY]
        atn_reg = [RY, RY, RY]

        def emit_scores_softmax(it):
            ci, h = divmod(it, HEADS)
            hp, off = h // 2, 64 * (h % 2)
            pr = it % 3
            par = pr * 8
            bkS = next_bank(att_bank_rr, "att")
            for qs in range(2):
                do("tensor", lambda e, qs=qs, ci=ci, hp=hp, off=off, bkS=bkS: e.matmul(
                    PBs[bkS][:, qs * CS:(qs + 1) * CS],
                    RAs[off:off + 64,
                        hp * TILE + ci * CS + qs * 128:hp * TILE + ci * CS + qs * 128 + 128],
                    RBs[off:off + 64, hp * TILE + ci * CS:hp * TILE + ci * CS + CS],
                    start=True, stop=True),
                   reads=[RA, RB], writes=[PB[bkS]])
            do("vector", lambda e, bkS=bkS, par=par, pr=pr: e.tensor_reduce(
                STATAs[:, par:par + 2],
                PBs[bkS][:, :].rearrange("p (a b) -> p a b", a=2),
                axis=AX.X, op=ALU.max, negate=True),
               reads=[PB[bkS]], writes=[RSTATA3[pr]])
            for qs in range(2):
                do("scalar", lambda e, qs=qs, bkS=bkS, par=par, pr=pr: e.activation(
                    att_ap(pr, qs * CS, (qs + 1) * CS),
                    PBs[bkS][:, qs * CS:(qs + 1) * CS],
                    AF.Exp, bias=STATAs[:, par + qs:par + qs + 1],
                    accum_out=STATAs[:, par + 2 + qs:par + 3 + qs]),
                   reads=[PB[bkS], RSTATA3[pr]], writes=[att_reg[pr], RSTATA3[pr]])
            do("vector", lambda e, par=par: e.reciprocal(
                STATAs[:, par + 4:par + 6], STATAs[:, par + 2:par + 4]),
               writes=[RSTATA3[pr]])
            for qs in range(2):
                do("vector", lambda e, qs=qs, par=par, pr=pr: e.tensor_scalar(
                    atn_ap(pr, qs * CS, (qs + 1) * CS),
                    att_ap(pr, qs * CS, (qs + 1) * CS),
                    STATAs[:, par + 4 + qs:par + 5 + qs], None, op0=ALU.mult),
                   reads=[att_reg[pr], RSTATA3[pr]], writes=[atn_reg[pr]])

        def emit_transpose_av(it):
            ci, h = divmod(it, HEADS)
            hp, off = h // 2, 64 * (h % 2)
            pr = it % 3
            bkT = next_bank(att_bank_rr, "att")
            for qs in range(2):
                for kc in range(2):
                    do("tensor", lambda e, qs=qs, kc=kc, bkT=bkT, pr=pr: e.transpose(
                        PBs[bkT][:, (qs * 2 + kc) * 128:(qs * 2 + kc) * 128 + 128],
                        atn_ap(pr, qs * CS + kc * 128, qs * CS + kc * 128 + 128),
                        IDs[:, :]),
                       reads=[atn_reg[pr], RCONST], writes=[PB[bkT]])
            for qs in range(2):
                do("vector", lambda e, qs=qs, bkT=bkT, pr=pr: e.tensor_copy(
                    att_ap(pr, 0, 512).rearrange("p (k c) -> p k c", k=2)
                    [:, :, qs * 128:(qs + 1) * 128],
                    PBs[bkT][:, qs * 256:(qs + 1) * 256].rearrange(
                        "p (k c) -> p k c", k=2)),
                   reads=[PB[bkT]], writes=[att_reg[pr]])
            bkA = next_bank(att_bank_rr, "att")
            av_deps = (RX.for_read() + att_reg[pr].for_read()
                       + PB[bkA].for_write())
            avmm = None
            for kc in range(2):
                s = ci * 2 + kc
                avmm = p.op("tensor", lambda e, s=s, h=h, kc=kc, bkA=bkA, pr=pr: e.matmul(
                    PBs[bkA][0:64, 0:CS],
                    RXs[:, s * D + h * DH:s * D + h * DH + DH],
                    att_ap(pr, kc * CS, kc * CS + CS),
                    start=(kc == 0), stop=(kc == 1)),
                    deps=av_deps if kc == 0 else [])
            RX.did_read(avmm)
            att_reg[pr].did_read(avmm)
            PB[bkA].did_write(avmm)
            do("vector", lambda e, off=off, hp=hp, ci=ci, bkA=bkA: e.tensor_copy(
                RDs[off:off + 64, hp * TILE + ci * CS:hp * TILE + ci * CS + CS],
                PBs[bkA][0:64, 0:CS]),
               reads=[PB[bkA]], writes=[RD])

        for it in range(NCH * HEADS):
            emit_scores_softmax(it)
            if it > 0:
                emit_transpose_av(it - 1)
        emit_transpose_av(NCH * HEADS - 1)
        if debug and tile == 0:
            dod("scalar", lambda e: e.dma_start(out=dbg["attn_out"], in_=RDs[:, :]),
                reads=[RD], ch="dbg")

        for os_ in range(2):
            sl_i = next_slot()
            dod("sync", lambda e, os_=os_, sl_i=sl_i: e.dma_start(
                out=slotf(sl_i), in_=woutp[os_]), writes=[SL[sl_i]],
                ch=f"sl{sl_i}")
            for oc4 in range(4):
                oc = os_ * 4 + oc4
                bk = next_bank(qkv_bank_rr, "qkv")
                pr_deps = (SL[sl_i].for_read() + RD.for_read()
                           + PB[bk].for_write())
                prmm = None
                for cc in range(CC):
                    prmm = p.op("tensor", lambda e, sl_i=sl_i, cc=cc, oc4=oc4, bk=bk: e.matmul(
                        PBs[bk][:, :],
                        slotf(sl_i)[:, cc * 512 + oc4 * 128:cc * 512 + oc4 * 128 + 128],
                        RDs[:, cc * TILE:cc * TILE + TILE],
                        start=(cc == 0), stop=(cc == CC - 1)),
                        deps=pr_deps if cc == 0 else [])
                SL[sl_i].did_read(prmm)
                RD.did_read(prmm)
                PB[bk].did_write(prmm)
                do("vector", lambda e, oc=oc, bk=bk: e.tensor_scalar(
                    RYs[:, oc * TILE:(oc + 1) * TILE], PBs[bk][:, :],
                    BOUTs[:, oc:oc + 1], None, op0=ALU.add),
                   reads=[PB[bk], RCONST], writes=[RY])
        if debug and tile == 0:
            dod("scalar", lambda e: e.dma_start(out=dbg["y_proj"], in_=RYs[:, :]),
                reads=[RY], ch="dbg")

        for ei in range(NSH):
            routed = False
            csl = 0
            for q in range(NQ):
                sl1, sl3, sl2 = next_slot(), next_slot(), next_slot()
                dod("sync", lambda e, ei=ei, q=q, sl1=sl1: e.dma_start(
                    out=SLOTs[sl1][:, :], in_=w1_s[ei, q]), writes=[SL[sl1]],
                    reads=[RWSPLIT] if tile == 0 else (), ch=f"sl{sl1}")
                dod("sync", lambda e, ei=ei, q=q, sl3=sl3: e.dma_start(
                    out=SLOTs[sl3][:, :], in_=w3_s[ei, q]), writes=[SL[sl3]],
                    reads=[RWSPLIT] if tile == 0 else (), ch=f"sl{sl3}")
                dod("sync", lambda e, ei=ei, q=q, sl2=sl2: e.dma_start(
                    out=SLOTs[sl2][:, :], in_=w2_s[ei, q]), writes=[SL[sl2]],
                    reads=[RWSPLIT] if tile == 0 else (), ch=f"sl{sl2}")
                for hc in range(4):
                    bkG = [0, 2][rr_state["gu"] % 2]
                    bkU = bkG + 1
                    rr_state["gu"] += 1

                    lhs1 = [(lambda e_cc: (lambda h, sl1=sl1, e_cc=e_cc, hc=hc:
                             SLOTs[sl1][:, h * HB + e_cc * HQ + hc * 128:
                                        h * HB + e_cc * HQ + hc * 128 + 128]))(cc)
                            for cc in range(CC)]
                    lhs3 = [(lambda e_cc: (lambda h, sl3=sl3, e_cc=e_cc, hc=hc:
                             SLOTs[sl3][:, h * HB + e_cc * HQ + hc * 128:
                                        h * HB + e_cc * HQ + hc * 128 + 128]))(cc)
                            for cc in range(CC)]
                    rhs = [(lambda e_cc: (lambda h, e_cc=e_cc: xn_half(e_cc, h)))(cc)
                           for cc in range(CC)]
                    mm3(bkG, lhs1, rhs, [SL[sl1], RXN])
                    mm3(bkU, lhs3, rhs, [SL[sl3], RXN])
                    do("scalar", lambda e, bkG=bkG: e.activation(
                        SGs[0][:, :], PBs[bkG][:, :], AF.Silu),
                       reads=[PB[bkG]], writes=[RSG[0]])
                    do("vector", lambda e, bkU=bkU: e.tensor_tensor(
                        SGs[1][:, :], PBs[bkU][:, :], SGs[0][:, :], op=ALU.mult),
                       reads=[PB[bkU], RSG[0]], writes=[RSG[1]])
                    if routed:
                        do("vector", lambda e, csl=csl: e.tensor_tensor(
                            SGs[0][:, :], SGs[1][:, :], CWs[csl][:, :], op=ALU.mult),
                           reads=[RSG[1], RCW[csl]], writes=[RSG[0]])
                        fin, rfin = SGs[0], RSG[0]
                    else:
                        fin, rfin = SGs[1], RSG[1]
                    do("scalar", lambda e, hc=hc, fin=fin: e.activation(
                        RHs[:, hc * 512:hc * 512 + 512], fin[:, :], AF.Copy),
                       reads=[rfin], writes=[RH])
                    do("vector", lambda e, hc=hc, fin=fin: e.tensor_tensor(
                        RHs[:, 2048 + hc * 512:2048 + hc * 512 + 512], fin[:, :],
                        RHs[:, hc * 512:hc * 512 + 512], op=ALU.subtract),
                       reads=[rfin], writes=[RH])
                for oc in range(CC):
                    bkW = [4, 5][rr_state["w2"] % 2]
                    rr_state["w2"] += 1
                    lhs2 = [(lambda e_hc: (lambda h, sl2=sl2, e_hc=e_hc, oc=oc:
                             SLOTs[sl2][:, h * HB + e_hc * D + oc * 128:
                                        h * HB + e_hc * D + oc * 128 + 128]))(hc)
                            for hc in range(4)]
                    rhsh = [(lambda e_hc: (lambda h, e_hc=e_hc:
                             RHs[:, h * 2048 + e_hc * 512:h * 2048 + e_hc * 512 + 512]))(hc)
                            for hc in range(4)]
                    mm3(bkW, lhs2, rhsh, [SL[sl2], RH])
                    do("vector", lambda e, oc=oc, bkW=bkW: e.tensor_tensor(
                        RYs[:, oc * TILE:(oc + 1) * TILE],
                        RYs[:, oc * TILE:(oc + 1) * TILE], PBs[bkW][:, :],
                        op=ALU.add), reads=[PB[bkW]], writes=[RY])
        if debug and tile == 0:
            dod("scalar", lambda e: e.dma_start(out=dbg["y_ffn"], in_=RYs[:, :]),
                reads=[RY], ch="dbg")

        dod("scalar", lambda e, tile=tile: e.dma_start(
            out=ysh_scr[tile], in_=RYs[:, :]), reads=[RY], writes=[RYSH], ch="ysh")


    w1flat = w1p.rearrange("a b p c -> (a b p) c")
    w3flat = w3p.rearrange("a b p c -> (a b p) c")
    w2flat = w2p.rearrange("a b p c -> (a b p) c")
    for k in range(NCHSP):
        xnb = XNs if k % 2 == 0 else RBs[:, :].bitcast(BF16)
        rxnb = RXN if k % 2 == 0 else RB
        dod("scalar", lambda e, k=k: e.dma_start(
            out=LGs[0:1, :],
            in_=tok_scr[512 * k:512 * (k + 1), :].rearrange("a b -> (a b)").unsqueeze(0)),
            reads=[RTOKSCR], writes=[RTOKROW], ch="tk")
        for s in range(SUB):
            tdep = RTOKROW.for_read() + RCONST.for_read() + PB[7].for_write()
            tmm = p.op("tensor", lambda e, s=s: e.transpose(
                PBs[7][:, s:s + 1], LGs[0:1, s * 128:(s + 1) * 128],
                IDs[0:1, 0:1]), deps=tdep)
            RTOKROW.did_read(tmm)
            PB[7].did_write(tmm)
            do("vector", lambda e, s=s: e.tensor_copy(
                TOKSBs[:, s:s + 1], PBs[7][:, s:s + 1]),
               reads=[PB[7]], writes=[RTOKSB])
            dod("gpsimd", lambda e, s=s: e.indirect_dma_start(
                out=RXs[:, s * D:(s + 1) * D], out_offset=None,
                in_=xn_scr, in_offset=IOA(TOKSBs[:, s:s + 1], 0)),
                reads=[RTOKSB, RXNSCR], writes=[RX], ch="xg")
        for g4 in range(SUB * CC // 4):
            s, ch4 = divmod(g4, 2)
            cc0 = ch4 * 4
            bk = 6 + (g4 % 2)
            for q in range(4):
                cc = cc0 + q
                do("tensor", lambda e, s=s, cc=cc, bk=bk, q=q: e.transpose(
                    PBs[bk][:, q * 128:(q + 1) * 128],
                    RXs[:, s * D + cc * 128:s * D + cc * 128 + 128], IDs[:, :]),
                   reads=[RX, RCONST], writes=[PB[bk]])
            pview = PBs[bk][:, :].rearrange("p (q c) -> p q c", q=4)

            def xnb_3d(h, xnb=xnb):
                return xnb[:, h * HB:(h + 1) * HB].rearrange(
                    "p (c t) -> p c t", c=CC)
            do("scalar", lambda e, s=s, cc0=cc0, pview=pview, xnb_3d=xnb_3d: e.activation(
                xnb_3d(0)[:, cc0:cc0 + 4, s * 128:(s + 1) * 128], pview, AF.Copy),
               reads=[PB[bk]], writes=[rxnb])
            do("vector", lambda e, s=s, cc0=cc0, pview=pview, xnb_3d=xnb_3d: e.tensor_tensor(
                xnb_3d(1)[:, cc0:cc0 + 4, s * 128:(s + 1) * 128], pview,
                xnb_3d(0)[:, cc0:cc0 + 4, s * 128:(s + 1) * 128], op=ALU.subtract),
               reads=[PB[bk]], writes=[rxnb])
        csl = k % 2
        dod("scalar", lambda e, k=k, csl=csl: e.dma_start(
            out=CWs[csl][:, :],
            in_=cw2_scr[512 * k:512 * (k + 1), :].rearrange("a b -> (a b)")
            .unsqueeze(0).broadcast_to((128, TILE))),
            reads=[RCW2SCR], writes=[RCW[csl]], ch=f"cw{csl}")
        for q in range(NQ):
            kq = k * 4 + q
            sraw = next_slot()
            sl1, sl3, sl2 = next_slot(), next_slot(), next_slot()
            for sl_i, wflat in [(sl1, w1flat), (sl3, w3flat), (sl2, w2flat)]:
                dod("gpsimd", lambda e, sraw=sraw, wflat=wflat, kq=kq: e.indirect_dma_start(
                    out=slotf(sraw), out_offset=None,
                    in_=wflat, in_offset=IOA(WIDXs[:, kq:kq + 1], 0)),
                    reads=[RWIDX], writes=[SL[sraw]], ch="wg")
                do("scalar", lambda e, sraw=sraw, sl_i=sl_i: e.activation(
                    SLOTs[sl_i][:, 0:HB], slotf(sraw), AF.Copy),
                   reads=[SL[sraw]], writes=[SL[sl_i]])
                do("vector", lambda e, sraw=sraw, sl_i=sl_i: e.tensor_tensor(
                    SLOTs[sl_i][:, HB:2 * HB], slotf(sraw), SLOTs[sl_i][:, 0:HB],
                    op=ALU.subtract), reads=[SL[sraw]], writes=[SL[sl_i]])
            for hc in range(4):
                bkG = [0, 2][rr_state["gu"] % 2]
                bkU = bkG + 1
                rr_state["gu"] += 1
                lhs1 = [(lambda e_cc: (lambda h, sl1=sl1, e_cc=e_cc, hc=hc:
                         SLOTs[sl1][:, h * HB + e_cc * HQ + hc * 128:
                                    h * HB + e_cc * HQ + hc * 128 + 128]))(cc)
                        for cc in range(CC)]
                lhs3 = [(lambda e_cc: (lambda h, sl3=sl3, e_cc=e_cc, hc=hc:
                         SLOTs[sl3][:, h * HB + e_cc * HQ + hc * 128:
                                    h * HB + e_cc * HQ + hc * 128 + 128]))(cc)
                        for cc in range(CC)]
                rhs = [(lambda e_cc: (lambda h, e_cc=e_cc, xnb=xnb:
                        xnb[:, h * HB + e_cc * TILE:h * HB + e_cc * TILE + TILE]))(cc)
                       for cc in range(CC)]
                mm3(bkG, lhs1, rhs, [SL[sl1], rxnb])
                mm3(bkU, lhs3, rhs, [SL[sl3], rxnb])
                do("scalar", lambda e, bkG=bkG: e.activation(
                    SGs[0][:, :], PBs[bkG][:, :], AF.Silu),
                   reads=[PB[bkG]], writes=[RSG[0]])
                do("vector", lambda e, bkU=bkU: e.tensor_tensor(
                    SGs[1][:, :], PBs[bkU][:, :], SGs[0][:, :], op=ALU.mult),
                   reads=[PB[bkU], RSG[0]], writes=[RSG[1]])
                do("vector", lambda e, csl=csl: e.tensor_tensor(
                    SGs[0][:, :], SGs[1][:, :], CWs[csl][:, :], op=ALU.mult),
                   reads=[RSG[1], RCW[csl]], writes=[RSG[0]])
                do("scalar", lambda e, hc=hc: e.activation(
                    RHs[:, hc * 512:hc * 512 + 512], SGs[0][:, :], AF.Copy),
                   reads=[RSG[0]], writes=[RH])
                do("vector", lambda e, hc=hc: e.tensor_tensor(
                    RHs[:, 2048 + hc * 512:2048 + hc * 512 + 512], SGs[0][:, :],
                    RHs[:, hc * 512:hc * 512 + 512], op=ALU.subtract),
                   reads=[RSG[0]], writes=[RH])
            for oc in range(CC):
                bkW = [4, 5][rr_state["w2"] % 2]
                rr_state["w2"] += 1
                lhs2 = [(lambda e_hc: (lambda h, sl2=sl2, e_hc=e_hc, oc=oc:
                         SLOTs[sl2][:, h * HB + e_hc * D + oc * 128:
                                    h * HB + e_hc * D + oc * 128 + 128]))(hc)
                        for hc in range(4)]
                rhsh = [(lambda e_hc: (lambda h, e_hc=e_hc:
                         RHs[:, h * 2048 + e_hc * 512:h * 2048 + e_hc * 512 + 512]))(hc)
                        for hc in range(4)]
                mm3(bkW, lhs2, rhsh, [SL[sl2], RH])
                if q == 0:
                    do("vector", lambda e, oc=oc, bkW=bkW: e.tensor_copy(
                        RYs[:, oc * TILE:(oc + 1) * TILE], PBs[bkW][:, :]),
                       reads=[PB[bkW]], writes=[RY])
                else:
                    do("vector", lambda e, oc=oc, bkW=bkW: e.tensor_tensor(
                        RYs[:, oc * TILE:(oc + 1) * TILE],
                        RYs[:, oc * TILE:(oc + 1) * TILE], PBs[bkW][:, :],
                        op=ALU.add), reads=[PB[bkW]], writes=[RY])
        for oc in range(CC):
            bk = 6 + oc % 2
            for s in range(SUB):
                do("tensor", lambda e, oc=oc, s=s, bk=bk: e.transpose(
                    PBs[bk][:, s * 128:(s + 1) * 128],
                    RYs[:, oc * TILE + s * 128:oc * TILE + s * 128 + 128], IDs[:, :]),
                   reads=[RY, RCONST], writes=[PB[bk]])
            do("vector", lambda e, oc=oc, bk=bk: e.tensor_copy(
                RAs[:, :].rearrange("p (s c) -> p s c", s=SUB)
                [:, :, oc * 128:(oc + 1) * 128],
                PBs[bk][:, :].rearrange("p (q c) -> p q c", q=4)),
               reads=[PB[bk]], writes=[RA])
        dod("scalar", lambda e, k=k: e.dma_start(
            out=og_scr[512 * k:512 * (k + 1), :].rearrange("(s p) c -> p s c", p=128),
            in_=RAs[:, :]), reads=[RA], writes=[ROG], ch="og")

    for tile in range(N_TILES):
        t0 = tile * TILE
        if tile % 2 == 0:
            fD, fDr = RDs[:, :], RD
            fX, fXr = RXs[:, :], RX
            fA, fAr = RAs[:, :], RA
            fB, fBr = RBs[:, :], RB
        else:
            fD, fDr = RYs[:, :], RY
            fX, fXr = XNs[:, :].bitcast(F32), RXN
            fA, fAr = slotf(0), SL[0]
            fB, fBr = slotf(1), SL[1]
        dod("scalar", lambda e, tile=tile, fD=fD: e.dma_start(
            out=fD, in_=ysh_scr[tile]), reads=[RYSH], writes=[fDr], ch="ysh2")
        dod("scalar", lambda e, t0=t0, fX=fX: e.dma_start(
            out=fX.rearrange("p (s c) -> p s c", s=SUB),
            in_=x_in[t0:t0 + TILE, :].rearrange("(s p) c -> p s c", p=128)),
            writes=[fXr], ch="x")
        for s in range(SUB):
            for li, (buf, reg) in enumerate([(fA, fAr), (fB, fBr)]):
                icol = tile * 8 + s * 2 + li
                dod("gpsimd", lambda e, s=s, buf=buf, icol=icol: e.indirect_dma_start(
                    out=buf[:, s * D:(s + 1) * D], out_offset=None,
                    in_=og_scr, in_offset=IOA(IDXALLs[:, icol:icol + 1], 0)),
                    reads=[RIDXALL, ROG], writes=[reg], ch="fg")
        for oc in range(CC):
            bk = 6 + oc % 2
            for s in range(SUB):
                do("tensor", lambda e, oc=oc, s=s, bk=bk, fD=fD: e.transpose(
                    PBs[bk][:, s * 128:(s + 1) * 128],
                    fD[:, oc * TILE + s * 128:oc * TILE + s * 128 + 128], IDs[:, :]),
                   reads=[fDr, RCONST], writes=[PB[bk]])
            do("vector", lambda e, oc=oc, bk=bk, fX=fX: e.tensor_tensor(
                fX.rearrange("p (s c) -> p s c", s=SUB)
                [:, :, oc * 128:(oc + 1) * 128],
                PBs[bk][:, :].rearrange("p (q c) -> p q c", q=4),
                fX.rearrange("p (s c) -> p s c", s=SUB)
                [:, :, oc * 128:(oc + 1) * 128], op=ALU.add),
               reads=[PB[bk]], writes=[fXr])
        do("vector", lambda e, fX=fX, fA=fA: e.tensor_tensor(
            fX, fX, fA, op=ALU.add), reads=[fAr], writes=[fXr])
        do("vector", lambda e, fX=fX, fB=fB: e.tensor_tensor(
            fX, fX, fB, op=ALU.add), reads=[fBr], writes=[fXr])
        st = dod("scalar", lambda e, t0=t0, fX=fX: e.dma_start(
            out=y_out[t0:t0 + TILE, :].rearrange("(s p) c -> p s c", p=128),
            in_=fX.rearrange("p (s c) -> p s c", s=SUB)),
            reads=[fXr], writes=[], ch="st")
        store_ops.append(st)

    p.op("vector", lambda e: e.memset(R3s[0:1, 0:1], 0.0), deps=store_ops)

    p.emit(nc, stack)
    return nc, stack, dbg


def _pack(inputs):
    f32 = np.float32
    Wqkv = np.asarray(inputs["L0_attn_Wqkv"], f32)
    Wout = np.asarray(inputs["L0_attn_Wout"], f32)
    w1 = np.asarray(inputs["L0_ffn_w1"], f32)
    w3 = np.asarray(inputs["L0_ffn_w3"], f32)
    w2 = np.asarray(inputs["L0_ffn_w2"], f32)
    sw1 = np.asarray(inputs["L0_ffn_sw1"], f32)
    sw3 = np.asarray(inputs["L0_ffn_sw3"], f32)
    sw2 = np.asarray(inputs["L0_ffn_sw2"], f32)

    wqkvp = np.ascontiguousarray(
        Wqkv.reshape(6, 512, 8, 128).transpose(0, 3, 2, 1)).reshape(6, 128, 4096)
    woutp = np.ascontiguousarray(
        Wout.reshape(2, 512, 8, 128).transpose(0, 3, 2, 1)).reshape(2, 128, 4096)

    def pack13(w):
        e = w.shape[0]
        return np.ascontiguousarray(
            w.reshape(e, NQ, HQ, 8, 128).transpose(0, 1, 4, 3, 2)).reshape(
                e, NQ, 128, 4096)

    def pack2(w):
        e = w.shape[0]
        return np.ascontiguousarray(
            w.reshape(e, D, NQ, 4, 128).transpose(0, 2, 4, 3, 1)).reshape(
                e, NQ, 128, 4096)

    w1p = np.concatenate([pack13(sw1), pack13(w1)], axis=0)
    w3p = np.concatenate([pack13(sw3), pack13(w3)], axis=0)
    w2p = np.concatenate([pack2(sw2), pack2(w2)], axis=0)

    router_w = np.asarray(inputs["L0_ffn_router_w"], f32)
    rwp = np.ascontiguousarray(
        router_w.reshape(8, 8, 128).transpose(2, 1, 0)).reshape(128, 64)
    boutp = np.ascontiguousarray(
        np.asarray(inputs["L0_attn_bout"], f32).reshape(8, 128).T)

    common = {
        "wqkvp": wqkvp, "woutp": woutp,
        "w1p": w1p, "w3p": w3p, "w2p": w2p,
        "rwp": rwp, "boutp": boutp,
        "ebias": np.pad(np.asarray(inputs["L0_ffn_expert_bias"], f32).reshape(8, 1),
                        ((0, 120), (0, 0))),
        "bl1": np.asarray(inputs["L0_attn_bl1"], f32).reshape(64, 1),
        "bl2": np.asarray(inputs["L0_attn_bl2"], f32).reshape(16, 1),
        "wl1row": np.pad(np.asarray(inputs["L0_attn_Wl1"], f32).T, ((0, 15), (0, 0))),
        "wl2t": np.ascontiguousarray(np.asarray(inputs["L0_attn_Wl2"], f32).T),
        "idx8": np.tile(np.arange(8, dtype=f32), (128, 1)),
    }
    x_all = np.ascontiguousarray(np.asarray(inputs["L0_attn_x"], f32)).reshape(
        16 * 2048, D)
    return common, x_all


def kernel(**inputs):
    common, x_all = _pack(inputs)
    nc, stack, _ = build_program(debug=False)
    in_maps = []
    for c in range(NCORES):
        m = dict(common)
        m["x"] = np.ascontiguousarray(x_all[c * T_CORE:(c + 1) * T_CORE])
        in_maps.append(m)
    res = run_bass_kernel_spmd(nc, in_maps, core_ids=list(range(NCORES)))
    out = np.concatenate([r["y"] for r in res.results], axis=0)
    return out.reshape(16, 2048, D)


# version 13
# speedup vs baseline: 1.2980x; 1.0798x over previous
import contextlib
import numpy as np
import concourse.bass as bass
import concourse.mybir as mybir
from concourse.bass_utils import run_bass_kernel_spmd

F32 = mybir.dt.float32
BF16 = mybir.dt.bfloat16
AF = mybir.ActivationFunctionType
ALU = mybir.AluOpType
AX = mybir.AxisListType

NCORES = 8
D = 1024
T_CORE = 4096
TILE = 512
N_TILES = T_CORE // TILE
SUB = TILE // 128
CC = D // 128
HID = 2048
NQ = 4
HQ = HID // NQ
HEADS = 16
DH = 64
CS = 256
NCH = TILE // CS
NEXP = 10
NSH = 2
NCHSP = 23
NSLSP = NCHSP * 512
BIG = 1.0e30
HB = 4096


class _Op:
    __slots__ = ("eng", "kind", "ch", "fn", "deps", "need_inc", "val", "seq")

    def __init__(self, eng, kind, ch, fn, deps, seq):
        self.eng, self.kind, self.ch, self.fn = eng, kind, ch, fn
        self.deps = deps
        self.need_inc = kind == "d"
        self.val = None
        self.seq = seq


class Prog:
    ENGINES = ("tensor", "vector", "scalar", "gpsimd", "sync")

    def __init__(self):
        self.ops = []

    def op(self, eng, fn, deps=()):
        o = _Op(eng, "c", "", fn, [d for d in deps if d is not None], len(self.ops))
        self.ops.append(o)
        return o

    def dma(self, eng, fn, deps=(), ch=""):
        o = _Op(eng, "d", ch, fn, [d for d in deps if d is not None], len(self.ops))
        self.ops.append(o)
        return o

    def emit(self, nc, stack):
        for o in self.ops:
            for d in o.deps:
                d.need_inc = True
        cnt = {}
        for o in self.ops:
            if o.need_inc:
                key = (o.eng, o.kind, o.ch)
                step = 16 if o.kind == "d" else 1
                cnt[key] = cnt.get(key, 0) + step
                o.val = (key, cnt[key])
        sems = {k: stack.enter_context(nc.semaphore(f"s_{k[0]}_{k[1]}{k[2]}"))
                for k in cnt}
        block = stack.enter_context(nc.Block())
        per_eng = {e: [o for o in self.ops if o.eng == e] for e in self.ENGINES}
        for eng_name in self.ENGINES:
            if not per_eng[eng_name]:
                continue

            def make(engn):
                ops = per_eng[engn]

                def emitter(eng):
                    waited = {}
                    for o in ops:
                        need = {}
                        for d in o.deps:
                            if d.val is None:
                                continue
                            key, v = d.val
                            if v > need.get(key, 0):
                                need[key] = v
                        for key, v in need.items():
                            if waited.get(key, 0) >= v:
                                continue
                            eng.wait_ge(sems[key], v)
                            waited[key] = v
                        instr = o.fn(eng)
                        if o.need_inc:
                            key, _ = o.val
                            instr.then_inc(sems[key], 16 if o.kind == "d" else 1)
                return emitter
            getattr(block, eng_name)(make(eng_name))


class Reg:
    __slots__ = ("w", "r")

    def __init__(self):
        self.w = []
        self.r = []

    def for_read(self):
        return list(self.w)

    def for_write(self):
        return list(self.w) + list(self.r)

    def did_read(self, op):
        self.r.append(op)

    def did_write(self, op):
        if self.r:
            self.w = [op]
            self.r = []
        else:
            self.w.append(op)


def build_program(debug=False):
    nc = bass.Bass("TRN2", target_bir_lowering=False)
    p = Prog()
    stack = contextlib.ExitStack()
    E = stack.enter_context

    def din(name, shape):
        return nc.dram_tensor(name, list(shape), F32, kind="ExternalInput").ap()

    x_in = din("x", (T_CORE, D))
    wqkvp = din("wqkvp", (6, 128, CC * 512))
    woutp = din("woutp", (2, 128, CC * 512))
    w1p = din("w1p", (NEXP, NQ, 128, CC * HQ))
    w3p = din("w3p", (NEXP, NQ, 128, CC * HQ))
    w2p = din("w2p", (NEXP, NQ, 128, NQ * D))
    rwp = din("rwp", (128, CC * 8))
    boutp = din("boutp", (128, CC))
    ebias_in = din("ebias", (128, 1))
    bl1_in = din("bl1", (64, 1))
    bl2_in = din("bl2", (16, 1))
    wl1row = din("wl1row", (16, 64))
    wl2t = din("wl2t", (64, 16))
    idx8_in = din("idx8", (128, 8))

    y_out = nc.dram_tensor("y", [T_CORE, D], F32, kind="ExternalOutput").ap()
    lam_scr = nc.dram_tensor("lam_scr", [HEADS, 2048], F32, kind="Internal").ap()
    cw_scr = nc.dram_tensor("cw_scr", [8, TILE], F32, kind="Internal").ap()
    wqkv_s = nc.dram_tensor("wqkv_s", [6, 128, 2 * HB], BF16, kind="Internal").ap()
    w1_s = nc.dram_tensor("w1_s", [NEXP, NQ, 128, 2 * HB], BF16, kind="Internal").ap()
    w3_s = nc.dram_tensor("w3_s", [NEXP, NQ, 128, 2 * HB], BF16, kind="Internal").ap()
    w2_s = nc.dram_tensor("w2_s", [NEXP, NQ, 128, 2 * HB], BF16, kind="Internal").ap()
    xn_scr = nc.dram_tensor("xn_scr", [T_CORE, D], F32, kind="Internal").ap()
    ysh_scr = nc.dram_tensor("ysh_scr", [N_TILES, 128, CC * TILE], F32, kind="Internal").ap()
    og_scr = nc.dram_tensor("og_scr", [NSLSP, D], F32, kind="Internal").ap()
    tok_scr = nc.dram_tensor("tok_scr", [NSLSP, 1], F32, kind="Internal").ap()
    cw2_scr = nc.dram_tensor("cw2_scr", [NSLSP, 1], F32, kind="Internal").ap()
    idx_scr = nc.dram_tensor("idx_scr", [N_TILES, TILE, 6], F32, kind="Internal").ap()

    dbg = {}
    if debug:
        def dout(name, shape):
            dbg[name] = nc.dram_tensor("dbg_" + name, list(shape), F32,
                                       kind="ExternalOutput").ap()
        dout("q_t", (128, CC * TILE))
        dout("k_t", (128, CC * TILE))
        dout("v_tok", (128, SUB * D))
        dout("attn_out", (128, CC * TILE))
        dout("y_proj", (128, CC * TILE))
        dout("lg_tok", (128, SUB * 8))
        dout("cw_fm", (8, TILE))
        dout("lam_fm", (HEADS, 2048))
        dout("y_ffn", (128, CC * TILE))

    def S(name, shape, dtype=F32):
        return E(nc.sbuf_tensor(name, list(shape), dtype))

    RXs = S("RX", (128, SUB * D))
    RAs = S("RA", (128, SUB * D))
    RBs = S("RB", (128, SUB * D))
    RDs = S("RD", (128, SUB * D))
    XNs = S("XN", (128, 2 * HB), BF16)
    RYs = S("RY", (128, CC * TILE))
    RHs = S("RH", (128, 4096), BF16)
    SGs = [S(f"SG{i}", (128, TILE)) for i in range(2)]
    LAMs = [S("LAM0", (128, TILE))]
    CWs = [S(f"CW{i}", (128, TILE)) for i in range(2)]
    RSTGs = [S("RSTG0", (128, 512))]
    STATs = S("STAT", (128, 64))
    STATAs = S("STATA", (128, 24))
    LGs = S("LG", (8, TILE))
    LGTs = S("LGT", (128, SUB * 8))
    R1s = S("R1", (128, 8))
    R2s = S("R2", (128, 8))
    R3s = S("R3", (128, 8))
    RSs = S("RS", (128, 16))
    CWTs = S("CWT", (128, SUB * 8))
    CWFs = S("CWF", (8, TILE))
    IDs = S("ID", (128, 128))
    IDX8s = S("IDX8", (128, 8))
    EBs = S("EB", (128, 1))
    BL1s = S("BL1", (64, 1))
    BL2s = S("BL2", (16, 1))
    WL1s = S("WL1", (16, 64))
    WL2s = S("WL2", (64, 16))
    RWs = S("RW", (128, CC * 8))
    BOUTs = S("BOUT", (128, CC))
    LTs = S("LT", (128, 128))
    ONESs = S("ONES", (128, 128))
    IOTAPs = S("IOTAP", (128, 1))
    M8s = S("M8", (128, 8))
    PRKs = S("PRK", (128, 8))
    IDXTs = S("IDXT", (128, SUB * 6))
    RBASEs = S("RBASE", (1, 8))
    SEGs = S("SEG", (8, 4))
    SEGRs = S("SEGR", (1, 8))
    SEGBCs = S("SEGBC", (128, 8))
    KTHs = S("KTH", (8, 24))
    EOCRs = S("EOCR", (1, 24))
    IDXALLs = S("IDXALL", (128, 8 * 8), mybir.dt.uint32)
    TOKSBs = S("TOKSB", (128, 8), mybir.dt.uint32)
    WIDXs = S("WIDX", (128, 96), mybir.dt.uint32)
    VALFs = S("VALF", (128, 2))
    NSLOT = 5
    SLOTs = [S(f"SLOT{i}", (128, 2 * HB), BF16) for i in range(NSLOT)]

    def slotf(i):
        return SLOTs[i][:, :].bitcast(F32)

    PBs = [E(nc.psum_tensor(f"pb{i}", [128, 512], F32)) for i in range(8)]

    RX, RA, RB, RD, RXN, RY, RH = (Reg() for _ in range(7))
    RSG = [Reg(), Reg()]
    RLAM = [Reg()]
    RCW = [Reg(), Reg()]
    RSTG = [Reg()]
    RSTAT = Reg()
    RSTATA3 = [Reg(), Reg(), Reg()]
    RLG, RLGT, RR1, RR2, RR3, RRS, RCWT, RCWF = (Reg() for _ in range(8))
    RCONST = Reg()
    RLAMSCR, RCWSCR, RWSPLIT, RWSPLITR, RWSPLITQ = (Reg() for _ in range(5))
    RXNSCR, RYSH, ROG, RTOKSCR, RCW2SCR, RIDXSCR = (Reg() for _ in range(6))
    RM8, RPRK, RIDXT, RRBASE, RSEG, RSEGR, RSEGBC = (Reg() for _ in range(7))
    RTHR, RKTH, REOCR, RIDXALL, RWIDX, RVAL = (Reg() for _ in range(6))
    RTOKSB2 = [Reg(), Reg()]
    RTOKROW2 = [Reg(), Reg()]
    PB = [Reg() for _ in range(8)]
    SL = [Reg() for _ in range(NSLOT)]

    def do(eng, fn, reads=(), writes=(), extra=()):
        deps = [d for r in reads for d in r.for_read()]
        deps += [d for w in writes for d in w.for_write()]
        deps += [d for d in extra if d is not None]
        o = p.op(eng, fn, deps)
        for r in reads:
            r.did_read(o)
        for w in writes:
            w.did_write(o)
        return o

    def dod(eng, fn, reads=(), writes=(), extra=(), ch=""):
        deps = [d for r in reads for d in r.for_read()]
        deps += [d for w in writes for d in w.for_write()]
        deps += [d for d in extra if d is not None]
        o = p.dma(eng, fn, deps, ch=ch)
        for r in reads:
            r.did_read(o)
        for w in writes:
            w.did_write(o)
        return o

    dod("scalar", lambda e: e.dma_start(
        out=RXs[:, :],
        in_=x_in[0:TILE, :].rearrange("(s p) c -> p s c", p=128)),
        writes=[RX], ch="x")
    for t, ap_ in [(IDX8s, idx8_in), (EBs, ebias_in),
                   (BL1s, bl1_in), (BL2s, bl2_in),
                   (WL1s, wl1row), (WL2s, wl2t),
                   (RWs, rwp), (BOUTs, boutp)]:
        o = dod("sync", lambda e, t=t, a=ap_: e.dma_start(out=t[:, :], in_=a),
                ch="c0")
        RCONST.w.append(o)

    o = do("gpsimd", lambda e: e.iota(IDs[:, :], pattern=[[1, 128]], base=0,
                                      channel_multiplier=-1,
                                      allow_small_or_imprecise_dtypes=True),
           writes=[RCONST])
    do("vector", lambda e: e.tensor_scalar(IDs[:, :], IDs[:, :], 0.0, None,
                                           op0=ALU.is_equal), writes=[RCONST])
    o = do("gpsimd", lambda e: e.iota(LTs[:, :], pattern=[[1, 128]], base=0,
                                      channel_multiplier=-1,
                                      allow_small_or_imprecise_dtypes=True),
           writes=[RCONST])
    do("vector", lambda e: e.tensor_scalar(LTs[:, :], LTs[:, :], 0.0, None,
                                           op0=ALU.is_gt), writes=[RCONST])
    do("vector", lambda e: e.memset(ONESs[:, :], 1.0), writes=[RCONST])
    do("gpsimd", lambda e: e.iota(IOTAPs[:, :], pattern=[[0, 1]], base=0,
                                  channel_multiplier=1,
                                  allow_small_or_imprecise_dtypes=True),
       writes=[RCONST])
    do("vector", lambda e: e.memset(RBASEs[:, :], 0.0), writes=[RRBASE])
    zop = do("vector", lambda e: e.memset(M8s[:, :], 0.0), writes=[RM8])
    for zi in range(NCHSP):
        dod("gpsimd", lambda e, zi=zi: e.dma_start(
            out=tok_scr[zi * 512:(zi + 1) * 512].rearrange("(p a) b -> p (a b)", p=128),
            in_=M8s[:, 0:4]), reads=[RM8], writes=[RTOKSCR], ch="z")
        dod("gpsimd", lambda e, zi=zi: e.dma_start(
            out=cw2_scr[zi * 512:(zi + 1) * 512].rearrange("(p a) b -> p (a b)", p=128),
            in_=M8s[:, 0:4]), reads=[RM8], writes=[RCW2SCR], ch="z")

    h1v = RYs[0:64, 0:2048]
    lamv = RYs[0:16, 2048:4096]
    lp_hi = RDs[64:65, 0:2048]
    lpv = RDs[0:1, 0:2048]
    o = do("gpsimd", lambda e: e.iota(lp_hi, pattern=[[1, 2048]], base=1,
                                      channel_multiplier=0,
                                      allow_small_or_imprecise_dtypes=True),
           writes=[RD])
    do("scalar", lambda e: e.activation(lpv, lp_hi, AF.Ln),
       reads=[RD], writes=[RD])
    for sl in range(4):
        mm = do("tensor", lambda e, sl=sl: e.matmul(
            PBs[0][0:64, :], WL1s[0:1, :], lpv[0:1, sl * 512:(sl + 1) * 512],
            start=True, stop=True), reads=[RCONST, RD], writes=[PB[0]])
        do("scalar", lambda e, sl=sl: e.activation(
            h1v[:, sl * 512:(sl + 1) * 512], PBs[0][0:64, :], AF.Relu,
            bias=BL1s[:, 0:1]), reads=[PB[0], RCONST], writes=[RY])
    for sl in range(4):
        mm = do("tensor", lambda e, sl=sl: e.matmul(
            PBs[1][0:16, :], WL2s[:, :], h1v[:, sl * 512:(sl + 1) * 512],
            start=True, stop=True), reads=[RCONST, RY], writes=[PB[1]])
        do("scalar", lambda e, sl=sl: e.activation(
            lamv[:, sl * 512:(sl + 1) * 512], PBs[1][0:16, :], AF.Sigmoid,
            bias=BL2s[:, 0:1]), reads=[PB[1], RCONST], writes=[RY])
    dod("scalar", lambda e: e.dma_start(out=lam_scr, in_=lamv[:, :]),
        reads=[RY], writes=[RLAMSCR], ch="lsm")
    if debug:
        dod("scalar", lambda e: e.dma_start(out=dbg["lam_fm"], in_=lamv[:, :]),
            reads=[RY], ch="dbg")

    units = [(wqkvp[ms], wqkv_s[ms]) for ms in range(6)]
    for ei in range(NSH):
        for q in range(NQ):
            units.append((w1p[ei, q], w1_s[ei, q]))
            units.append((w3p[ei, q], w3_s[ei, q]))
            units.append((w2p[ei, q], w2_s[ei, q]))
    for ei in range(NSH, NEXP):
        for q in range(NQ):
            units.append((w1p[ei, q], w1_s[ei, q]))
            units.append((w3p[ei, q], w3_s[ei, q]))
            units.append((w2p[ei, q], w2_s[ei, q]))

    def record_split_unit(u, src_, dst, routed):
        a, b = 3, 4
        dod("gpsimd", lambda e, src_=src_, a=a: e.dma_start(out=slotf(a), in_=src_),
            writes=[SL[a]], ch=f"sl{a}")
        do("gpsimd", lambda e, a=a, b=b: e.tensor_copy(
            SLOTs[b][:, 0:HB], slotf(a)),
           reads=[SL[a]], writes=[SL[b]])
        do("gpsimd", lambda e, a=a, b=b: e.tensor_tensor(
            SLOTs[b][:, HB:2 * HB], slotf(a), SLOTs[b][:, 0:HB], op=ALU.subtract),
           reads=[SL[a]], writes=[SL[b]])
        wreg = RWSPLITQ if u < 6 else (RWSPLITR if routed else RWSPLIT)
        dod("gpsimd", lambda e, b=b, dst=dst: e.dma_start(out=dst, in_=SLOTs[b][:, :]),
            reads=[SL[b]], writes=[wreg], ch=f"ws{b}")

    for u, (src_, dst) in enumerate(units):
        record_split_unit(u, src_, dst, routed=(u >= 30))

    qkv_bank_rr = [4, 5]
    att_bank_rr = [3, 4, 5, 6, 7]
    rr_state = {"qkv": 0, "att": 0, "slot": 0, "slot5": 0, "gu": 0, "w2": 0,
                "stg": 0}

    def next_bank(pool, key):
        i = pool[rr_state[key] % len(pool)]
        rr_state[key] += 1
        return i

    def next_slot():
        i = rr_state["slot"] % 3
        rr_state["slot"] += 1
        return i

    def next_slot5():
        i = rr_state["slot5"] % NSLOT
        rr_state["slot5"] += 1
        return i

    from concourse.bass import IndirectOffsetOnAxis as IOA

    def record_index_finalize():
        cdep = RRBASE.for_read() + RCONST.for_read() + PB[6].for_write()
        c1 = p.op("tensor", lambda e: e.matmul(PBs[6][0:8, 0:1], RBASEs[0:1, :],
                                               ONESs[0:1, 0:1], start=True, stop=True),
                  deps=cdep)
        RRBASE.did_read(c1)
        PB[6].did_write(c1)
        do("vector", lambda e: e.tensor_copy(SEGs[0:8, 0:1], PBs[6][0:8, 0:1]),
           reads=[PB[6]], writes=[RSEG])
        do("gpsimd", lambda e: e.iota(KTHs[0:8, 0:16], pattern=[[512, 16]], base=0,
                                      channel_multiplier=0,
                                      allow_small_or_imprecise_dtypes=True),
           writes=[RKTH])
        do("vector", lambda e: e.tensor_scalar(KTHs[0:8, 0:16], KTHs[0:8, 0:16],
                                               SEGs[0:8, 0:1], None, op0=ALU.is_lt),
           reads=[RSEG], writes=[RKTH])
        do("vector", lambda e: e.tensor_reduce(SEGs[0:8, 1:2], KTHs[0:8, 0:16],
                                               axis=AX.X, op=ALU.add),
           reads=[RKTH], writes=[RSEG])
        do("vector", lambda e: e.tensor_scalar(SEGs[0:8, 1:2], SEGs[0:8, 1:2], 512.0,
                                               None, op0=ALU.mult), writes=[RSEG])
        sdep = RSEG.for_read() + RCONST.for_read() + PB[6].for_write()
        smm = p.op("tensor", lambda e: e.matmul(PBs[6][0:8, 4:5], LTs[0:8, 0:8],
                                                SEGs[0:8, 1:2], start=True, stop=True),
                   deps=sdep)
        RSEG.did_read(smm)
        PB[6].did_write(smm)
        do("vector", lambda e: e.tensor_copy(SEGs[0:8, 2:3], PBs[6][0:8, 4:5]),
           reads=[PB[6]], writes=[RSEG])
        do("vector", lambda e: e.tensor_tensor(SEGs[0:8, 3:4], SEGs[0:8, 2:3],
                                               SEGs[0:8, 1:2], op=ALU.add), writes=[RSEG])
        rdep = RSEG.for_read() + RCONST.for_read() + PB[6].for_write()
        rmm = p.op("tensor", lambda e: e.matmul(PBs[6][0:1, 8:16], SEGs[0:8, 2:3],
                                                IDs[0:8, 0:8], start=True, stop=True),
                   deps=rdep)
        RSEG.did_read(rmm)
        PB[6].did_write(rmm)
        do("vector", lambda e: e.tensor_copy(SEGRs[0:1, :], PBs[6][0:1, 8:16]),
           reads=[PB[6]], writes=[RSEGR])
        bdep = RSEGR.for_read() + RCONST.for_read() + PB[6].for_write()
        bmm = p.op("tensor", lambda e: e.matmul(PBs[6][:, 16:24], ONESs[0:1, :],
                                                SEGRs[0:1, :], start=True, stop=True),
                   deps=bdep)
        RSEGR.did_read(bmm)
        PB[6].did_write(bmm)
        do("vector", lambda e: e.tensor_copy(SEGBCs[:, :], PBs[6][:, 16:24]),
           reads=[PB[6]], writes=[RSEGBC])
        do("gpsimd", lambda e: e.iota(KTHs[:, :], pattern=[[512, 24]], base=0,
                                      channel_multiplier=0,
                                      allow_small_or_imprecise_dtypes=True),
           writes=[RKTH])
        do("vector", lambda e: e.tensor_scalar(IDXTs[0:8, 0:24], KTHs[0:8, :],
                                               SEGs[0:8, 2:3], None, op0=ALU.is_ge),
           reads=[RKTH, RSEG], writes=[RIDXT])
        do("vector", lambda e: e.tensor_scalar(KTHs[0:8, :], KTHs[0:8, :],
                                               SEGs[0:8, 3:4], None, op0=ALU.is_lt),
           reads=[RSEG], writes=[RKTH])
        do("vector", lambda e: e.tensor_tensor(IDXTs[0:8, 0:24], IDXTs[0:8, 0:24],
                                               KTHs[0:8, :], op=ALU.mult),
           reads=[RKTH], writes=[RIDXT])
        edep = RIDXT.for_read() + RCONST.for_read() + PB[6].for_write()
        emm = p.op("tensor", lambda e: e.matmul(PBs[6][0:1, 24:48], IOTAPs[0:8, 0:1],
                                                IDXTs[0:8, 0:24], start=True, stop=True),
                   deps=edep)
        RIDXT.did_read(emm)
        PB[6].did_write(emm)
        do("vector", lambda e: e.tensor_copy(EOCRs[0:1, :], PBs[6][0:1, 24:48]),
           reads=[PB[6]], writes=[REOCR])
        for k in range(NCHSP):
            wdep = REOCR.for_read() + RCONST.for_read() + PB[7].for_write()
            wmm = p.op("tensor", lambda e, k=k: e.matmul(
                PBs[7][:, 8:9], ONESs[0:1, :], EOCRs[0:1, k:k + 1],
                start=True, stop=True), deps=wdep)
            REOCR.did_read(wmm)
            PB[7].did_write(wmm)
            for q in range(NQ):
                do("vector", lambda e, q=q: e.tensor_scalar(
                    VALFs[:, 0:1], PBs[7][:, 8:9], 512.0, float(q * 128 + NSH * NQ * 128),
                    op0=ALU.mult, op1=ALU.add), reads=[PB[7]], writes=[RVAL])
                do("vector", lambda e: e.tensor_tensor(
                    VALFs[:, 0:1], VALFs[:, 0:1], IOTAPs[:, :], op=ALU.add),
                   reads=[RCONST], writes=[RVAL])
                do("vector", lambda e, k=k, q=q: e.tensor_copy(
                    WIDXs[:, k * 4 + q:k * 4 + q + 1], VALFs[:, 0:1]),
                   reads=[RVAL], writes=[RWIDX])

        for tile in range(N_TILES):
            t0 = tile * TILE
            dod("scalar", lambda e, tile=tile: e.dma_start(
                out=IDXTs[:, :].rearrange("p (s r) -> p s r", s=SUB),
                in_=idx_scr[tile].rearrange("(s p) r -> p s r", p=128)),
                reads=[RIDXSCR], writes=[RIDXT], ch="ix")
            for s in range(SUB):
                for li in range(2):
                    icol = tile * 8 + s * 2 + li
                    do("vector", lambda e, s=s, li=li: e.tensor_scalar(
                        R1s[:, :], IDX8s[:, :], IDXTs[:, s * 6 + 2 + li:s * 6 + 3 + li],
                        None, op0=ALU.is_equal),
                       reads=[RCONST, RIDXT], writes=[RR1])
                    do("vector", lambda e: e.tensor_tensor(
                        R3s[:, :], R1s[:, :], SEGBCs[:, :], op=ALU.mult),
                       reads=[RR1, RSEGBC], writes=[RR3])
                    do("vector", lambda e: e.tensor_reduce(
                        RSs[:, 0:1], R3s[:, :], axis=AX.X, op=ALU.add),
                       reads=[RR3], writes=[RRS])
                    do("vector", lambda e, s=s, li=li: e.tensor_tensor(
                        VALFs[:, 0:1], RSs[:, 0:1], IDXTs[:, s * 6 + li:s * 6 + 1 + li],
                        op=ALU.add), reads=[RRS, RIDXT], writes=[RVAL])
                    do("vector", lambda e, icol=icol: e.tensor_copy(
                        IDXALLs[:, icol:icol + 1], VALFs[:, 0:1]),
                       reads=[RVAL], writes=[RIDXALL])
                    do("vector", lambda e, t0=t0, s=s: e.tensor_scalar(
                        VALFs[:, 1:2], IOTAPs[:, :], float(t0 + s * 128), None,
                        op0=ALU.add), reads=[RCONST], writes=[RVAL])
                    dod("gpsimd", lambda e, icol=icol: e.indirect_dma_start(
                        out=tok_scr, out_offset=IOA(IDXALLs[:, icol:icol + 1], 0),
                        in_=VALFs[:, 1:2], in_offset=None),
                        reads=[RVAL, RIDXALL], writes=[RTOKSCR], ch="sc")
                    dod("gpsimd", lambda e, icol=icol, s=s, li=li: e.indirect_dma_start(
                        out=cw2_scr, out_offset=IOA(IDXALLs[:, icol:icol + 1], 0),
                        in_=IDXTs[:, s * 6 + 4 + li:s * 6 + 5 + li], in_offset=None),
                        reads=[RIDXT, RIDXALL], writes=[RCW2SCR], ch="sc")



    store_ops = []

    for tile in range(N_TILES):
        t0 = tile * TILE
        pos0 = t0 % 2048

        if tile > 0:
            dod("scalar", lambda e, t0=t0: e.dma_start(
                out=RXs[:, :],
                in_=x_in[t0:t0 + TILE, :].rearrange("(s p) c -> p s c", p=128)),
                writes=[RX], ch="x")

        for s in range(SUB):
            xsl = RXs[:, s * D:(s + 1) * D]
            c0 = 16 * s
            do("vector", lambda e, xsl=xsl, c0=c0: e.tensor_reduce(
                STATs[:, c0:c0 + 8],
                xsl.rearrange("p (a b) -> p a b", a=8), axis=AX.X, op=ALU.add),
               reads=[RX], writes=[RSTAT])
            do("vector", lambda e, c0=c0: e.tensor_reduce(
                STATs[:, c0 + 8:c0 + 9], STATs[:, c0:c0 + 8], axis=AX.X, op=ALU.add),
               writes=[RSTAT])
            do("vector", lambda e, c0=c0: e.tensor_scalar(
                STATs[:, c0 + 9:c0 + 10], STATs[:, c0 + 8:c0 + 9], -1.0 / D, None,
                op0=ALU.mult), writes=[RSTAT])
            do("scalar", lambda e, xsl=xsl, c0=c0: e.activation(
                RDs[:, 0:D], xsl, AF.Square, bias=STATs[:, c0 + 9:c0 + 10]),
               reads=[RX, RSTAT], writes=[RD])
            do("vector", lambda e, c0=c0: e.tensor_reduce(
                STATs[:, c0:c0 + 8],
                RDs[:, 0:D].rearrange("p (a b) -> p a b", a=8), axis=AX.X,
                op=ALU.add), reads=[RD], writes=[RSTAT])
            do("vector", lambda e, c0=c0: e.tensor_reduce(
                STATs[:, c0 + 10:c0 + 11], STATs[:, c0:c0 + 8], axis=AX.X,
                op=ALU.add), writes=[RSTAT])
            do("vector", lambda e, c0=c0: e.tensor_scalar(
                STATs[:, c0 + 11:c0 + 12], STATs[:, c0 + 10:c0 + 11], 1.0 / D, 1e-5,
                op0=ALU.mult, op1=ALU.add), writes=[RSTAT])
            do("scalar", lambda e, c0=c0: e.activation(
                STATs[:, c0 + 12:c0 + 13], STATs[:, c0 + 11:c0 + 12], AF.Sqrt),
               writes=[RSTAT])
            do("vector", lambda e, c0=c0: e.reciprocal(
                STATs[:, c0 + 13:c0 + 14], STATs[:, c0 + 12:c0 + 13]),
               writes=[RSTAT])
            do("vector", lambda e, xsl=xsl, s=s, c0=c0: e.tensor_scalar(
                RAs[:, s * D:(s + 1) * D], xsl, STATs[:, c0 + 9:c0 + 10],
                STATs[:, c0 + 13:c0 + 14], op0=ALU.add, op1=ALU.mult),
               reads=[RX, RSTAT], writes=[RA])

        def xn_3d(h):
            return XNs[:, h * HB:(h + 1) * HB].rearrange(
                "p (c t) -> p c t", c=CC)

        for g4 in range(SUB * CC // 4):
            s, ch4 = divmod(g4, 2)
            cc0 = ch4 * 4
            bk = 6 + (g4 % 2)
            for q in range(4):
                cc = cc0 + q
                do("tensor", lambda e, s=s, cc=cc, bk=bk, q=q: e.transpose(
                    PBs[bk][:, q * 128:(q + 1) * 128],
                    RAs[:, s * D + cc * 128:s * D + cc * 128 + 128], IDs[:, :]),
                    reads=[RA, RCONST], writes=[PB[bk]])
            pview = PBs[bk][:, :].rearrange("p (q c) -> p q c", q=4)
            do("scalar", lambda e, s=s, cc0=cc0, bk=bk, pview=pview: e.activation(
                xn_3d(0)[:, cc0:cc0 + 4, s * 128:(s + 1) * 128], pview, AF.Copy),
                reads=[PB[bk]], writes=[RXN])
            do("vector", lambda e, s=s, cc0=cc0, bk=bk, pview=pview: e.tensor_tensor(
                xn_3d(1)[:, cc0:cc0 + 4, s * 128:(s + 1) * 128], pview,
                xn_3d(0)[:, cc0:cc0 + 4, s * 128:(s + 1) * 128], op=ALU.subtract),
               reads=[PB[bk], RXN], writes=[RXN])
            do("vector", lambda e, bk=bk: e.tensor_copy(
                RSTGs[0][:, :], PBs[bk][:, :]),
               reads=[PB[bk]], writes=[RSTG[0]])
            for q in range(4):
                cc = cc0 + q
                lg_deps = (RCONST.for_read() + RSTG[0].for_read()
                           + (PB[3].for_write() if cc == 0 else []))
                lgmm = p.op("tensor", lambda e, s=s, cc=cc, q=q: e.matmul(
                    PBs[3][0:8, s * 128:(s + 1) * 128], RWs[:, cc * 8:(cc + 1) * 8],
                    RSTGs[0][:, q * 128:(q + 1) * 128],
                    start=(cc == 0), stop=(cc == CC - 1)),
                    deps=lg_deps)
                RSTG[0].did_read(lgmm)
                if cc == CC - 1:
                    PB[3].did_write(lgmm)
        do("vector", lambda e: e.tensor_scalar(
            LGs[:, :], PBs[3][0:8, :], EBs[0:8, 0:1], None, op0=ALU.add),
           reads=[PB[3], RCONST], writes=[RLG])

        dod("scalar", lambda e, t0=t0: e.dma_start(
            out=xn_scr[t0:t0 + TILE, :].rearrange("(s p) c -> p s c", p=128),
            in_=RAs[:, :]), reads=[RA], writes=[RXNSCR], ch="xn")

        for tc in range(SUB):
            do("tensor", lambda e, tc=tc: e.transpose(
                PBs[6][:, tc * 8:(tc + 1) * 8],
                LGs[:, tc * 128:(tc + 1) * 128], IDs[0:8, 0:8]),
               reads=[RLG, RCONST], writes=[PB[6]])
        do("vector", lambda e: e.tensor_copy(
            LGTs[:, :], PBs[6][:, 0:SUB * 8]), reads=[PB[6]], writes=[RLGT])
        if debug and tile == 0:
            dod("scalar", lambda e: e.dma_start(out=dbg["lg_tok"], in_=LGTs[:, :]),
                reads=[RLGT], ch="dbg")
        for tc in range(SUB):
            lg = LGTs[:, tc * 8:(tc + 1) * 8]
            do("vector", lambda e, lg=lg: e.tensor_reduce(
                RSs[:, 0:1], lg, axis=AX.X, op=ALU.max), reads=[RLGT], writes=[RRS])
            do("vector", lambda e, lg=lg: e.tensor_scalar(
                R1s[:, :], lg, RSs[:, 0:1], None, op0=ALU.is_ge),
               reads=[RLGT, RRS], writes=[RR1])
            do("vector", lambda e: e.scalar_tensor_tensor(
                R2s[:, :], IDX8s[:, :], 9.0, R1s[:, :], op0=ALU.subtract, op1=ALU.mult),
               reads=[RCONST, RR1], writes=[RR2])
            do("vector", lambda e: e.tensor_reduce(
                RSs[:, 1:2], R2s[:, :], axis=AX.X, op=ALU.min),
               reads=[RR2], writes=[RRS])
            do("vector", lambda e: e.tensor_scalar(
                RSs[:, 1:2], RSs[:, 1:2], 9.0, None, op0=ALU.add), writes=[RRS])
            do("vector", lambda e: e.tensor_scalar(
                R1s[:, :], IDX8s[:, :], RSs[:, 1:2], None, op0=ALU.is_equal),
               reads=[RCONST, RRS], writes=[RR1])
            do("vector", lambda e, lg=lg: e.scalar_tensor_tensor(
                R3s[:, :], R1s[:, :], -BIG, lg, op0=ALU.mult, op1=ALU.add),
               reads=[RR1, RLGT], writes=[RR3])
            do("vector", lambda e: e.tensor_reduce(
                RSs[:, 2:3], R3s[:, :], axis=AX.X, op=ALU.max),
               reads=[RR3], writes=[RRS])
            do("vector", lambda e: e.tensor_scalar(
                R2s[:, :], R3s[:, :], RSs[:, 2:3], None, op0=ALU.is_ge),
               reads=[RR3, RRS], writes=[RR2])
            do("vector", lambda e: e.scalar_tensor_tensor(
                R2s[:, :], IDX8s[:, :], 9.0, R2s[:, :], op0=ALU.subtract, op1=ALU.mult),
               reads=[RCONST], writes=[RR2])
            do("vector", lambda e: e.tensor_reduce(
                RSs[:, 3:4], R2s[:, :], axis=AX.X, op=ALU.min),
               reads=[RR2], writes=[RRS])
            do("vector", lambda e: e.tensor_scalar(
                RSs[:, 3:4], RSs[:, 3:4], 9.0, None, op0=ALU.add), writes=[RRS])
            do("vector", lambda e: e.tensor_scalar(
                R2s[:, :], IDX8s[:, :], RSs[:, 3:4], None, op0=ALU.is_equal),
               reads=[RCONST, RRS], writes=[RR2])
            do("vector", lambda e: e.tensor_tensor(
                M8s[:, :], R1s[:, :], R2s[:, :], op=ALU.add),
               reads=[RR1, RR2], writes=[RM8])
            pf_deps = (RCONST.for_read() + RM8.for_read() + RRBASE.for_read()
                       + PB[3].for_write())
            pf1 = p.op("tensor", lambda e, tc=tc: e.matmul(
                PBs[3][:, tc * 8:(tc + 1) * 8], LTs[:, :], M8s[:, :],
                start=True, stop=False), deps=pf_deps)
            pf2 = p.op("tensor", lambda e, tc=tc: e.matmul(
                PBs[3][:, tc * 8:(tc + 1) * 8], ONESs[0:1, :], RBASEs[0:1, :],
                start=False, stop=True), deps=[])
            RM8.did_read(pf2)
            RRBASE.did_read(pf2)
            PB[3].did_write(pf2)
            cnt_deps = RM8.for_read() + RCONST.for_read() + PB[3].for_write()
            cntmm = p.op("tensor", lambda e, tc=tc: e.matmul(
                PBs[3][0:1, 32 + tc * 8:40 + tc * 8], ONESs[:, 0:1], M8s[:, :],
                start=True, stop=True), deps=cnt_deps)
            RM8.did_read(cntmm)
            PB[3].did_write(cntmm)
            do("vector", lambda e, tc=tc: e.tensor_copy(
                PRKs[:, :], PBs[3][:, tc * 8:(tc + 1) * 8]),
               reads=[PB[3]], writes=[RPRK])
            do("vector", lambda e: e.tensor_tensor(
                R3s[:, :], R1s[:, :], PRKs[:, :], op=ALU.mult),
               reads=[RR1, RPRK], writes=[RR3])
            do("vector", lambda e, tc=tc: e.tensor_reduce(
                IDXTs[:, tc * 6:tc * 6 + 1], R3s[:, :], axis=AX.X, op=ALU.add),
               reads=[RR3], writes=[RIDXT])
            do("vector", lambda e: e.tensor_tensor(
                R3s[:, :], R2s[:, :], PRKs[:, :], op=ALU.mult),
               reads=[RR2, RPRK], writes=[RR3])
            do("vector", lambda e, tc=tc: e.tensor_reduce(
                IDXTs[:, tc * 6 + 1:tc * 6 + 2], R3s[:, :], axis=AX.X, op=ALU.add),
               reads=[RR3], writes=[RIDXT])
            do("vector", lambda e, tc=tc: e.tensor_tensor(
                RBASEs[0:1, :], RBASEs[0:1, :], PBs[3][0:1, 32 + tc * 8:40 + tc * 8],
                op=ALU.add), reads=[PB[3]], writes=[RRBASE])
            do("vector", lambda e, tc=tc: e.tensor_copy(
                IDXTs[:, tc * 6 + 2:tc * 6 + 3], RSs[:, 1:2]),
               reads=[RRS], writes=[RIDXT])
            do("vector", lambda e, tc=tc: e.tensor_copy(
                IDXTs[:, tc * 6 + 3:tc * 6 + 4], RSs[:, 3:4]),
               reads=[RRS], writes=[RIDXT])
            do("vector", lambda e: e.tensor_tensor(
                RSs[:, 4:5], RSs[:, 0:1], RSs[:, 2:3], op=ALU.subtract), writes=[RRS])
            do("scalar", lambda e: e.activation(
                RSs[:, 4:5], RSs[:, 4:5], AF.Sigmoid), writes=[RRS])
            do("vector", lambda e: e.tensor_scalar(
                RSs[:, 5:6], RSs[:, 4:5], -1.0, 1.0, op0=ALU.mult, op1=ALU.add),
               writes=[RRS])
            do("vector", lambda e: e.tensor_scalar(
                R1s[:, :], R1s[:, :], RSs[:, 4:5], None, op0=ALU.mult),
               reads=[RRS], writes=[RR1])
            do("vector", lambda e, tc=tc: e.scalar_tensor_tensor(
                CWTs[:, tc * 8:(tc + 1) * 8], R2s[:, :], RSs[:, 5:6], R1s[:, :],
                op0=ALU.mult, op1=ALU.add),
               reads=[RR2, RRS, RR1], writes=[RCWT])
            do("vector", lambda e, tc=tc: e.tensor_copy(
                IDXTs[:, tc * 6 + 4:tc * 6 + 5], RSs[:, 4:5]),
               reads=[RRS], writes=[RIDXT])
            do("vector", lambda e, tc=tc: e.tensor_copy(
                IDXTs[:, tc * 6 + 5:tc * 6 + 6], RSs[:, 5:6]),
               reads=[RRS], writes=[RIDXT])
        for tc in range(SUB):
            do("tensor", lambda e, tc=tc: e.transpose(
                PBs[7][0:8, tc * 128:(tc + 1) * 128],
                CWTs[:, tc * 8:(tc + 1) * 8], IDs[:, :]),
               reads=[RCWT, RCONST], writes=[PB[7]])
        do("vector", lambda e: e.tensor_copy(CWFs[:, :], PBs[7][0:8, 0:TILE]),
           reads=[PB[7]], writes=[RCWF])
        dod("scalar", lambda e: e.dma_start(out=cw_scr, in_=CWFs[:, :]),
            reads=[RCWF], writes=[RCWSCR], ch="sm")
        dod("scalar", lambda e, tile=tile: e.dma_start(
            out=idx_scr[tile].rearrange("(s p) r -> p s r", p=128),
            in_=IDXTs[:, :].rearrange("p (s r) -> p s r", s=SUB)),
            reads=[RIDXT], writes=[RIDXSCR], ch="sm2")
        if tile == N_TILES - 1:
            record_index_finalize()
        if debug and tile == 0:
            dod("scalar", lambda e: e.dma_start(out=dbg["cw_fm"], in_=CWFs[:, :]),
                reads=[RCWF], ch="dbg")

        def mm3(bk, lhs_pairs, rhs_pairs, lreads, out_ap=None):
            n = len(lhs_pairs)
            first_deps = ([d for r in lreads for d in r.for_read()]
                          + PB[bk].for_write())
            last = None
            for j in range(n):
                for pi, (lh, rh) in enumerate([(0, 0), (0, 1), (1, 0)]):
                    st_ = (j == 0 and pi == 0)
                    sp_ = (j == n - 1 and pi == 2)
                    last = p.op("tensor",
                                lambda e, j=j, lh=lh, rh=rh, st_=st_, sp_=sp_: e.matmul(
                                    (out_ap or PBs[bk][:, :]),
                                    lhs_pairs[j](lh), rhs_pairs[j](rh),
                                    start=st_, stop=sp_),
                                deps=first_deps if st_ else [])
            for r in lreads:
                r.did_read(last)
            PB[bk].did_write(last)
            return last

        def xn_half(cc, h):
            return XNs[:, h * HB + cc * TILE:h * HB + cc * TILE + TILE]

        def xn_half_tc(cc, tc, h):
            o = h * HB + cc * TILE + tc * 128
            return XNs[:, o:o + 128]

        for ms in range(6):
            sl_i = next_slot()
            dod("sync", lambda e, ms=ms, sl_i=sl_i: e.dma_start(
                out=SLOTs[sl_i][:, :], in_=wqkv_s[ms]), writes=[SL[sl_i]],
                reads=[RWSPLITQ] if tile == 0 else (), ch=f"sl{sl_i}")
            if ms < 4:
                for mc in range(4):
                    mg = ms * 4 + mc
                    bk = next_bank(qkv_bank_rr, "qkv")
                    lhs = [(lambda e_cc: (lambda h, sl_i=sl_i, e_cc=e_cc, mc=mc:
                            SLOTs[sl_i][:, h * HB + e_cc * 512 + mc * 128:
                                         h * HB + e_cc * 512 + mc * 128 + 128]))(cc)
                           for cc in range(CC)]
                    rhs = [(lambda e_cc: (lambda h, e_cc=e_cc: xn_half(e_cc, h)))(cc)
                           for cc in range(CC)]
                    mm3(bk, lhs, rhs, [SL[sl_i], RXN])
                    if mg < 8:
                        hp = mg
                        do("scalar", lambda e, hp=hp, bk=bk: e.activation(
                            RAs[:, hp * TILE:(hp + 1) * TILE], PBs[bk][:, :],
                            AF.Copy, scale=0.125),
                           reads=[PB[bk]], writes=[RA])
                    else:
                        hp = mg - 8
                        lsl = 0
                        for half in range(2):
                            dod("scalar", lambda e, hp=hp, lsl=lsl, pos0=pos0, half=half: e.dma_start(
                                out=LAMs[lsl][half * 64:(half + 1) * 64, :],
                                in_=lam_scr[2 * hp + half:2 * hp + half + 1,
                                            pos0:pos0 + TILE].broadcast_to((64, TILE))),
                                reads=[RLAMSCR], writes=[RLAM[lsl]], ch=f"lam{lsl}")
                        do("vector", lambda e, hp=hp, lsl=lsl, bk=bk: e.tensor_tensor(
                            RBs[:, hp * TILE:(hp + 1) * TILE], PBs[bk][:, :],
                            LAMs[lsl][:, :], op=ALU.mult),
                           reads=[PB[bk], RLAM[lsl]], writes=[RB])
            else:
                vs = ms - 4
                for tc in range(SUB):
                    bk = next_bank(qkv_bank_rr, "qkv")
                    lhs = [(lambda e_cc: (lambda h, e_cc=e_cc, tc=tc:
                            xn_half_tc(e_cc, tc, h)))(cc) for cc in range(CC)]
                    rhs = [(lambda e_cc: (lambda h, sl_i=sl_i, e_cc=e_cc:
                            SLOTs[sl_i][:, h * HB + e_cc * 512:
                                         h * HB + e_cc * 512 + 512]))(cc)
                           for cc in range(CC)]
                    mm3(bk, lhs, rhs, [SL[sl_i], RXN])
                    do("vector", lambda e, tc=tc, vs=vs, bk=bk: e.tensor_copy(
                        RXs[:, tc * D + vs * 512:tc * D + vs * 512 + 512],
                        PBs[bk][:, :]), reads=[PB[bk]], writes=[RX])
        if debug and tile == 0:
            dod("scalar", lambda e: e.dma_start(out=dbg["q_t"], in_=RAs[:, :]),
                reads=[RA], ch="dbg")
            dod("scalar", lambda e: e.dma_start(out=dbg["k_t"], in_=RBs[:, :]),
                reads=[RB], ch="dbg")
            dod("scalar", lambda e: e.dma_start(out=dbg["v_tok"], in_=RXs[:, :]),
                reads=[RX], ch="dbg")

        def att_ap(pr, a, b):
            return RYs[:, pr * 1024 + a:pr * 1024 + b]

        def atn_ap(pr, a, b):
            return RYs[:, pr * 1024 + 512 + a:pr * 1024 + 512 + b]

        att_reg = [RY, RY, RY]
        atn_reg = [RY, RY, RY]

        def emit_scores_softmax(it):
            ci, h = divmod(it, HEADS)
            hp, off = h // 2, 64 * (h % 2)
            pr = it % 3
            par = pr * 8
            bkS = next_bank(att_bank_rr, "att")
            for qs in range(2):
                do("tensor", lambda e, qs=qs, ci=ci, hp=hp, off=off, bkS=bkS: e.matmul(
                    PBs[bkS][:, qs * CS:(qs + 1) * CS],
                    RAs[off:off + 64,
                        hp * TILE + ci * CS + qs * 128:hp * TILE + ci * CS + qs * 128 + 128],
                    RBs[off:off + 64, hp * TILE + ci * CS:hp * TILE + ci * CS + CS],
                    start=True, stop=True),
                   reads=[RA, RB], writes=[PB[bkS]])
            do("vector", lambda e, bkS=bkS, par=par, pr=pr: e.tensor_reduce(
                STATAs[:, par:par + 2],
                PBs[bkS][:, :].rearrange("p (a b) -> p a b", a=2),
                axis=AX.X, op=ALU.max, negate=True),
               reads=[PB[bkS]], writes=[RSTATA3[pr]])
            for qs in range(2):
                do("scalar", lambda e, qs=qs, bkS=bkS, par=par, pr=pr: e.activation(
                    att_ap(pr, qs * CS, (qs + 1) * CS),
                    PBs[bkS][:, qs * CS:(qs + 1) * CS],
                    AF.Exp, bias=STATAs[:, par + qs:par + qs + 1],
                    accum_out=STATAs[:, par + 2 + qs:par + 3 + qs]),
                   reads=[PB[bkS], RSTATA3[pr]], writes=[att_reg[pr], RSTATA3[pr]])
            do("vector", lambda e, par=par: e.reciprocal(
                STATAs[:, par + 4:par + 6], STATAs[:, par + 2:par + 4]),
               writes=[RSTATA3[pr]])
            for qs in range(2):
                do("vector", lambda e, qs=qs, par=par, pr=pr: e.tensor_scalar(
                    atn_ap(pr, qs * CS, (qs + 1) * CS),
                    att_ap(pr, qs * CS, (qs + 1) * CS),
                    STATAs[:, par + 4 + qs:par + 5 + qs], None, op0=ALU.mult),
                   reads=[att_reg[pr], RSTATA3[pr]], writes=[atn_reg[pr]])

        def emit_transpose_av(it):
            ci, h = divmod(it, HEADS)
            hp, off = h // 2, 64 * (h % 2)
            pr = it % 3
            bkT = next_bank(att_bank_rr, "att")
            for qs in range(2):
                for kc in range(2):
                    do("tensor", lambda e, qs=qs, kc=kc, bkT=bkT, pr=pr: e.transpose(
                        PBs[bkT][:, (qs * 2 + kc) * 128:(qs * 2 + kc) * 128 + 128],
                        atn_ap(pr, qs * CS + kc * 128, qs * CS + kc * 128 + 128),
                        IDs[:, :]),
                       reads=[atn_reg[pr], RCONST], writes=[PB[bkT]])
            for qs in range(2):
                do("vector", lambda e, qs=qs, bkT=bkT, pr=pr: e.tensor_copy(
                    att_ap(pr, 0, 512).rearrange("p (k c) -> p k c", k=2)
                    [:, :, qs * 128:(qs + 1) * 128],
                    PBs[bkT][:, qs * 256:(qs + 1) * 256].rearrange(
                        "p (k c) -> p k c", k=2)),
                   reads=[PB[bkT]], writes=[att_reg[pr]])
            bkA = next_bank(att_bank_rr, "att")
            av_deps = (RX.for_read() + att_reg[pr].for_read()
                       + PB[bkA].for_write())
            avmm = None
            for kc in range(2):
                s = ci * 2 + kc
                avmm = p.op("tensor", lambda e, s=s, h=h, kc=kc, bkA=bkA, pr=pr: e.matmul(
                    PBs[bkA][0:64, 0:CS],
                    RXs[:, s * D + h * DH:s * D + h * DH + DH],
                    att_ap(pr, kc * CS, kc * CS + CS),
                    start=(kc == 0), stop=(kc == 1)),
                    deps=av_deps if kc == 0 else [])
            RX.did_read(avmm)
            att_reg[pr].did_read(avmm)
            PB[bkA].did_write(avmm)
            do("vector", lambda e, off=off, hp=hp, ci=ci, bkA=bkA: e.tensor_copy(
                RDs[off:off + 64, hp * TILE + ci * CS:hp * TILE + ci * CS + CS],
                PBs[bkA][0:64, 0:CS]),
               reads=[PB[bkA]], writes=[RD])

        for it in range(NCH * HEADS):
            emit_scores_softmax(it)
            if it >= 2:
                emit_transpose_av(it - 2)
        emit_transpose_av(NCH * HEADS - 2)
        emit_transpose_av(NCH * HEADS - 1)
        if debug and tile == 0:
            dod("scalar", lambda e: e.dma_start(out=dbg["attn_out"], in_=RDs[:, :]),
                reads=[RD], ch="dbg")

        for os_ in range(2):
            sl_i = next_slot()
            dod("sync", lambda e, os_=os_, sl_i=sl_i: e.dma_start(
                out=slotf(sl_i), in_=woutp[os_]), writes=[SL[sl_i]],
                ch=f"sl{sl_i}")
            for oc4 in range(4):
                oc = os_ * 4 + oc4
                bk = next_bank(qkv_bank_rr, "qkv")
                pr_deps = (SL[sl_i].for_read() + RD.for_read()
                           + PB[bk].for_write())
                prmm = None
                for cc in range(CC):
                    prmm = p.op("tensor", lambda e, sl_i=sl_i, cc=cc, oc4=oc4, bk=bk: e.matmul(
                        PBs[bk][:, :],
                        slotf(sl_i)[:, cc * 512 + oc4 * 128:cc * 512 + oc4 * 128 + 128],
                        RDs[:, cc * TILE:cc * TILE + TILE],
                        start=(cc == 0), stop=(cc == CC - 1)),
                        deps=pr_deps if cc == 0 else [])
                SL[sl_i].did_read(prmm)
                RD.did_read(prmm)
                PB[bk].did_write(prmm)
                do("vector", lambda e, oc=oc, bk=bk: e.tensor_scalar(
                    RYs[:, oc * TILE:(oc + 1) * TILE], PBs[bk][:, :],
                    BOUTs[:, oc:oc + 1], None, op0=ALU.add),
                   reads=[PB[bk], RCONST], writes=[RY])
        if debug and tile == 0:
            dod("scalar", lambda e: e.dma_start(out=dbg["y_proj"], in_=RYs[:, :]),
                reads=[RY], ch="dbg")

        for ei in range(NSH):
            routed = False
            csl = 0
            for q in range(NQ):
                sl1, sl3, sl2 = next_slot(), next_slot(), next_slot()
                dod("sync", lambda e, ei=ei, q=q, sl1=sl1: e.dma_start(
                    out=SLOTs[sl1][:, :], in_=w1_s[ei, q]), writes=[SL[sl1]],
                    reads=[RWSPLIT] if tile == 0 else (), ch=f"sl{sl1}")
                dod("sync", lambda e, ei=ei, q=q, sl3=sl3: e.dma_start(
                    out=SLOTs[sl3][:, :], in_=w3_s[ei, q]), writes=[SL[sl3]],
                    reads=[RWSPLIT] if tile == 0 else (), ch=f"sl{sl3}")
                dod("sync", lambda e, ei=ei, q=q, sl2=sl2: e.dma_start(
                    out=SLOTs[sl2][:, :], in_=w2_s[ei, q]), writes=[SL[sl2]],
                    reads=[RWSPLIT] if tile == 0 else (), ch=f"sl{sl2}")
                for hc in range(4):
                    bkG = [0, 2][rr_state["gu"] % 2]
                    bkU = bkG + 1
                    rr_state["gu"] += 1

                    lhs1 = [(lambda e_cc: (lambda h, sl1=sl1, e_cc=e_cc, hc=hc:
                             SLOTs[sl1][:, h * HB + e_cc * HQ + hc * 128:
                                        h * HB + e_cc * HQ + hc * 128 + 128]))(cc)
                            for cc in range(CC)]
                    lhs3 = [(lambda e_cc: (lambda h, sl3=sl3, e_cc=e_cc, hc=hc:
                             SLOTs[sl3][:, h * HB + e_cc * HQ + hc * 128:
                                        h * HB + e_cc * HQ + hc * 128 + 128]))(cc)
                            for cc in range(CC)]
                    rhs = [(lambda e_cc: (lambda h, e_cc=e_cc: xn_half(e_cc, h)))(cc)
                           for cc in range(CC)]
                    mm3(bkG, lhs1, rhs, [SL[sl1], RXN])
                    mm3(bkU, lhs3, rhs, [SL[sl3], RXN])
                    do("scalar", lambda e, bkG=bkG: e.activation(
                        SGs[0][:, :], PBs[bkG][:, :], AF.Silu),
                       reads=[PB[bkG]], writes=[RSG[0]])
                    do("vector", lambda e, bkU=bkU: e.tensor_tensor(
                        SGs[1][:, :], PBs[bkU][:, :], SGs[0][:, :], op=ALU.mult),
                       reads=[PB[bkU], RSG[0]], writes=[RSG[1]])
                    if routed:
                        do("vector", lambda e, csl=csl: e.tensor_tensor(
                            SGs[0][:, :], SGs[1][:, :], CWs[csl][:, :], op=ALU.mult),
                           reads=[RSG[1], RCW[csl]], writes=[RSG[0]])
                        fin, rfin = SGs[0], RSG[0]
                    else:
                        fin, rfin = SGs[1], RSG[1]
                    do("scalar", lambda e, hc=hc, fin=fin: e.activation(
                        RHs[:, hc * 512:hc * 512 + 512], fin[:, :], AF.Copy),
                       reads=[rfin], writes=[RH])
                    do("vector", lambda e, hc=hc, fin=fin: e.tensor_tensor(
                        RHs[:, 2048 + hc * 512:2048 + hc * 512 + 512], fin[:, :],
                        RHs[:, hc * 512:hc * 512 + 512], op=ALU.subtract),
                       reads=[rfin], writes=[RH])
                for oc in range(CC):
                    bkW = [4, 5][rr_state["w2"] % 2]
                    rr_state["w2"] += 1
                    lhs2 = [(lambda e_hc: (lambda h, sl2=sl2, e_hc=e_hc, oc=oc:
                             SLOTs[sl2][:, h * HB + e_hc * D + oc * 128:
                                        h * HB + e_hc * D + oc * 128 + 128]))(hc)
                            for hc in range(4)]
                    rhsh = [(lambda e_hc: (lambda h, e_hc=e_hc:
                             RHs[:, h * 2048 + e_hc * 512:h * 2048 + e_hc * 512 + 512]))(hc)
                            for hc in range(4)]
                    mm3(bkW, lhs2, rhsh, [SL[sl2], RH])
                    do("vector", lambda e, oc=oc, bkW=bkW: e.tensor_tensor(
                        RYs[:, oc * TILE:(oc + 1) * TILE],
                        RYs[:, oc * TILE:(oc + 1) * TILE], PBs[bkW][:, :],
                        op=ALU.add), reads=[PB[bkW]], writes=[RY])
        if debug and tile == 0:
            dod("scalar", lambda e: e.dma_start(out=dbg["y_ffn"], in_=RYs[:, :]),
                reads=[RY], ch="dbg")

        dod("scalar", lambda e, tile=tile: e.dma_start(
            out=ysh_scr[tile], in_=RYs[:, :]), reads=[RY], writes=[RYSH], ch="ysh")


    w1flat = w1_s.rearrange("a b p c -> (a b p) c")
    w3flat = w3_s.rearrange("a b p c -> (a b p) c")
    w2flat = w2_s.rearrange("a b p c -> (a b p) c")
    def emit_tok_gather(k):
        par = k % 2
        xg_buf, xg_reg = (RXs, RX) if par == 0 else (RDs, RD)
        trow = LGs if par == 0 else CWFs
        dod("scalar", lambda e, k=k, trow=trow: e.dma_start(
            out=trow[0:1, :],
            in_=tok_scr[512 * k:512 * (k + 1), :].rearrange("a b -> (a b)").unsqueeze(0)),
            reads=[RTOKSCR],
            writes=[RTOKROW2[par], RLG if par == 0 else RCWF], ch="tk")
        for s in range(SUB):
            col = par * 4 + s
            tdep = RTOKROW2[par].for_read() + RCONST.for_read() + PB[7].for_write()
            tmm = p.op("tensor", lambda e, s=s, col=col, trow=trow: e.transpose(
                PBs[7][:, col:col + 1], trow[0:1, s * 128:(s + 1) * 128],
                IDs[0:1, 0:1]), deps=tdep)
            RTOKROW2[par].did_read(tmm)
            PB[7].did_write(tmm)
            do("vector", lambda e, col=col: e.tensor_copy(
                TOKSBs[:, col:col + 1], PBs[7][:, col:col + 1]),
               reads=[PB[7]], writes=[RTOKSB2[par]])
            dod("gpsimd", lambda e, s=s, col=col, xg_buf=xg_buf, par=par:
                e.indirect_dma_start(
                    out=xg_buf[:, s * D:(s + 1) * D], out_offset=None,
                    in_=xn_scr, in_offset=IOA(TOKSBs[:, col:col + 1], 0)),
                reads=[RTOKSB2[par], RXNSCR], writes=[xg_reg], ch="xg")

    emit_tok_gather(0)
    for k in range(NCHSP):
        if k + 1 < NCHSP:
            emit_tok_gather(k + 1)
        xnb = XNs if k % 2 == 0 else RBs[:, :].bitcast(BF16)
        rxnb = RXN if k % 2 == 0 else RB
        xg_buf, xg_reg = (RXs, RX) if k % 2 == 0 else (RDs, RD)
        for g4 in range(SUB * CC // 4):
            s, ch4 = divmod(g4, 2)
            cc0 = ch4 * 4
            bk = 6 + (g4 % 2)
            for q in range(4):
                cc = cc0 + q
                do("tensor", lambda e, s=s, cc=cc, bk=bk, q=q, xg_buf=xg_buf:
                   e.transpose(
                       PBs[bk][:, q * 128:(q + 1) * 128],
                       xg_buf[:, s * D + cc * 128:s * D + cc * 128 + 128],
                       IDs[:, :]),
                   reads=[xg_reg, RCONST], writes=[PB[bk]])
            pview = PBs[bk][:, :].rearrange("p (q c) -> p q c", q=4)

            def xnb_3d(h, xnb=xnb):
                return xnb[:, h * HB:(h + 1) * HB].rearrange(
                    "p (c t) -> p c t", c=CC)
            do("scalar", lambda e, s=s, cc0=cc0, pview=pview, xnb_3d=xnb_3d: e.activation(
                xnb_3d(0)[:, cc0:cc0 + 4, s * 128:(s + 1) * 128], pview, AF.Copy),
               reads=[PB[bk]], writes=[rxnb])
            do("vector", lambda e, s=s, cc0=cc0, pview=pview, xnb_3d=xnb_3d: e.tensor_tensor(
                xnb_3d(1)[:, cc0:cc0 + 4, s * 128:(s + 1) * 128], pview,
                xnb_3d(0)[:, cc0:cc0 + 4, s * 128:(s + 1) * 128], op=ALU.subtract),
               reads=[PB[bk]], writes=[rxnb])
        csl = k % 2
        dod("scalar", lambda e, k=k, csl=csl: e.dma_start(
            out=CWs[csl][:, :],
            in_=cw2_scr[512 * k:512 * (k + 1), :].rearrange("a b -> (a b)")
            .unsqueeze(0).broadcast_to((128, TILE))),
            reads=[RCW2SCR], writes=[RCW[csl]], ch=f"cw{csl}")
        for q in range(NQ):
            kq = k * 4 + q
            sl1, sl3, sl2 = next_slot5(), next_slot5(), next_slot5()
            for sl_i, wflat in [(sl1, w1flat), (sl3, w3flat), (sl2, w2flat)]:
                dod("gpsimd", lambda e, sl_i=sl_i, wflat=wflat, kq=kq:
                    e.indirect_dma_start(
                        out=SLOTs[sl_i][:, :], out_offset=None,
                        in_=wflat, in_offset=IOA(WIDXs[:, kq:kq + 1], 0)),
                    reads=[RWIDX, RWSPLITR], writes=[SL[sl_i]], ch="wg")
            for hc in range(4):
                bkG = [0, 2][rr_state["gu"] % 2]
                bkU = bkG + 1
                rr_state["gu"] += 1
                lhs1 = [(lambda e_cc: (lambda h, sl1=sl1, e_cc=e_cc, hc=hc:
                         SLOTs[sl1][:, h * HB + e_cc * HQ + hc * 128:
                                    h * HB + e_cc * HQ + hc * 128 + 128]))(cc)
                        for cc in range(CC)]
                lhs3 = [(lambda e_cc: (lambda h, sl3=sl3, e_cc=e_cc, hc=hc:
                         SLOTs[sl3][:, h * HB + e_cc * HQ + hc * 128:
                                    h * HB + e_cc * HQ + hc * 128 + 128]))(cc)
                        for cc in range(CC)]
                rhs = [(lambda e_cc: (lambda h, e_cc=e_cc, xnb=xnb:
                        xnb[:, h * HB + e_cc * TILE:h * HB + e_cc * TILE + TILE]))(cc)
                       for cc in range(CC)]
                mm3(bkG, lhs1, rhs, [SL[sl1], rxnb])
                mm3(bkU, lhs3, rhs, [SL[sl3], rxnb])
                do("scalar", lambda e, bkG=bkG: e.activation(
                    SGs[0][:, :], PBs[bkG][:, :], AF.Silu),
                   reads=[PB[bkG]], writes=[RSG[0]])
                do("vector", lambda e, bkU=bkU: e.tensor_tensor(
                    SGs[1][:, :], PBs[bkU][:, :], SGs[0][:, :], op=ALU.mult),
                   reads=[PB[bkU], RSG[0]], writes=[RSG[1]])
                do("vector", lambda e, csl=csl: e.tensor_tensor(
                    SGs[0][:, :], SGs[1][:, :], CWs[csl][:, :], op=ALU.mult),
                   reads=[RSG[1], RCW[csl]], writes=[RSG[0]])
                do("scalar", lambda e, hc=hc: e.activation(
                    RHs[:, hc * 512:hc * 512 + 512], SGs[0][:, :], AF.Copy),
                   reads=[RSG[0]], writes=[RH])
                do("vector", lambda e, hc=hc: e.tensor_tensor(
                    RHs[:, 2048 + hc * 512:2048 + hc * 512 + 512], SGs[0][:, :],
                    RHs[:, hc * 512:hc * 512 + 512], op=ALU.subtract),
                   reads=[RSG[0]], writes=[RH])
            for oc in range(CC):
                bkW = [4, 5][rr_state["w2"] % 2]
                rr_state["w2"] += 1
                lhs2 = [(lambda e_hc: (lambda h, sl2=sl2, e_hc=e_hc, oc=oc:
                         SLOTs[sl2][:, h * HB + e_hc * D + oc * 128:
                                    h * HB + e_hc * D + oc * 128 + 128]))(hc)
                        for hc in range(4)]
                rhsh = [(lambda e_hc: (lambda h, e_hc=e_hc:
                         RHs[:, h * 2048 + e_hc * 512:h * 2048 + e_hc * 512 + 512]))(hc)
                        for hc in range(4)]
                mm3(bkW, lhs2, rhsh, [SL[sl2], RH])
                if q == 0:
                    do("vector", lambda e, oc=oc, bkW=bkW: e.tensor_copy(
                        RYs[:, oc * TILE:(oc + 1) * TILE], PBs[bkW][:, :]),
                       reads=[PB[bkW]], writes=[RY])
                else:
                    do("vector", lambda e, oc=oc, bkW=bkW: e.tensor_tensor(
                        RYs[:, oc * TILE:(oc + 1) * TILE],
                        RYs[:, oc * TILE:(oc + 1) * TILE], PBs[bkW][:, :],
                        op=ALU.add), reads=[PB[bkW]], writes=[RY])
        for oc in range(CC):
            bk = 6 + oc % 2
            for s in range(SUB):
                do("tensor", lambda e, oc=oc, s=s, bk=bk: e.transpose(
                    PBs[bk][:, s * 128:(s + 1) * 128],
                    RYs[:, oc * TILE + s * 128:oc * TILE + s * 128 + 128], IDs[:, :]),
                   reads=[RY, RCONST], writes=[PB[bk]])
            do("vector", lambda e, oc=oc, bk=bk: e.tensor_copy(
                RAs[:, :].rearrange("p (s c) -> p s c", s=SUB)
                [:, :, oc * 128:(oc + 1) * 128],
                PBs[bk][:, :].rearrange("p (q c) -> p q c", q=4)),
               reads=[PB[bk]], writes=[RA])
        dod("scalar", lambda e, k=k: e.dma_start(
            out=og_scr[512 * k:512 * (k + 1), :].rearrange("(s p) c -> p s c", p=128),
            in_=RAs[:, :]), reads=[RA], writes=[ROG], ch="og")

    for tile in range(N_TILES):
        t0 = tile * TILE
        if tile % 2 == 0:
            fD, fDr = RDs[:, :], RD
            fX, fXr = RXs[:, :], RX
            fA, fAr = RAs[:, :], RA
            fB, fBr = RBs[:, :], RB
        else:
            fD, fDr = RYs[:, :], RY
            fX, fXr = XNs[:, :].bitcast(F32), RXN
            fA, fAr = slotf(0), SL[0]
            fB, fBr = slotf(1), SL[1]
        dod("scalar", lambda e, tile=tile, fD=fD: e.dma_start(
            out=fD, in_=ysh_scr[tile]), reads=[RYSH], writes=[fDr], ch="ysh2")
        dod("scalar", lambda e, t0=t0, fX=fX: e.dma_start(
            out=fX.rearrange("p (s c) -> p s c", s=SUB),
            in_=x_in[t0:t0 + TILE, :].rearrange("(s p) c -> p s c", p=128)),
            writes=[fXr], ch="x")
        for s in range(SUB):
            for li, (buf, reg) in enumerate([(fA, fAr), (fB, fBr)]):
                icol = tile * 8 + s * 2 + li
                dod("gpsimd", lambda e, s=s, buf=buf, icol=icol: e.indirect_dma_start(
                    out=buf[:, s * D:(s + 1) * D], out_offset=None,
                    in_=og_scr, in_offset=IOA(IDXALLs[:, icol:icol + 1], 0)),
                    reads=[RIDXALL, ROG], writes=[reg], ch="fg")
        for oc in range(CC):
            bk = 6 + oc % 2
            for s in range(SUB):
                do("tensor", lambda e, oc=oc, s=s, bk=bk, fD=fD: e.transpose(
                    PBs[bk][:, s * 128:(s + 1) * 128],
                    fD[:, oc * TILE + s * 128:oc * TILE + s * 128 + 128], IDs[:, :]),
                   reads=[fDr, RCONST], writes=[PB[bk]])
            do("vector", lambda e, oc=oc, bk=bk, fX=fX: e.tensor_tensor(
                fX.rearrange("p (s c) -> p s c", s=SUB)
                [:, :, oc * 128:(oc + 1) * 128],
                PBs[bk][:, :].rearrange("p (q c) -> p q c", q=4),
                fX.rearrange("p (s c) -> p s c", s=SUB)
                [:, :, oc * 128:(oc + 1) * 128], op=ALU.add),
               reads=[PB[bk]], writes=[fXr])
        do("vector", lambda e, fX=fX, fA=fA: e.tensor_tensor(
            fX, fX, fA, op=ALU.add), reads=[fAr], writes=[fXr])
        do("vector", lambda e, fX=fX, fB=fB: e.tensor_tensor(
            fX, fX, fB, op=ALU.add), reads=[fBr], writes=[fXr])
        st = dod("scalar", lambda e, t0=t0, fX=fX: e.dma_start(
            out=y_out[t0:t0 + TILE, :].rearrange("(s p) c -> p s c", p=128),
            in_=fX.rearrange("p (s c) -> p s c", s=SUB)),
            reads=[fXr], writes=[], ch="st")
        store_ops.append(st)

    p.op("vector", lambda e: e.memset(R3s[0:1, 0:1], 0.0), deps=store_ops)

    p.emit(nc, stack)
    return nc, stack, dbg


def _pack(inputs):
    f32 = np.float32
    Wqkv = np.asarray(inputs["L0_attn_Wqkv"], f32)
    Wout = np.asarray(inputs["L0_attn_Wout"], f32)
    w1 = np.asarray(inputs["L0_ffn_w1"], f32)
    w3 = np.asarray(inputs["L0_ffn_w3"], f32)
    w2 = np.asarray(inputs["L0_ffn_w2"], f32)
    sw1 = np.asarray(inputs["L0_ffn_sw1"], f32)
    sw3 = np.asarray(inputs["L0_ffn_sw3"], f32)
    sw2 = np.asarray(inputs["L0_ffn_sw2"], f32)

    wqkvp = np.ascontiguousarray(
        Wqkv.reshape(6, 512, 8, 128).transpose(0, 3, 2, 1)).reshape(6, 128, 4096)
    woutp = np.ascontiguousarray(
        Wout.reshape(2, 512, 8, 128).transpose(0, 3, 2, 1)).reshape(2, 128, 4096)

    def pack13(w):
        e = w.shape[0]
        return np.ascontiguousarray(
            w.reshape(e, NQ, HQ, 8, 128).transpose(0, 1, 4, 3, 2)).reshape(
                e, NQ, 128, 4096)

    def pack2(w):
        e = w.shape[0]
        return np.ascontiguousarray(
            w.reshape(e, D, NQ, 4, 128).transpose(0, 2, 4, 3, 1)).reshape(
                e, NQ, 128, 4096)

    w1p = np.concatenate([pack13(sw1), pack13(w1)], axis=0)
    w3p = np.concatenate([pack13(sw3), pack13(w3)], axis=0)
    w2p = np.concatenate([pack2(sw2), pack2(w2)], axis=0)

    router_w = np.asarray(inputs["L0_ffn_router_w"], f32)
    rwp = np.ascontiguousarray(
        router_w.reshape(8, 8, 128).transpose(2, 1, 0)).reshape(128, 64)
    boutp = np.ascontiguousarray(
        np.asarray(inputs["L0_attn_bout"], f32).reshape(8, 128).T)

    common = {
        "wqkvp": wqkvp, "woutp": woutp,
        "w1p": w1p, "w3p": w3p, "w2p": w2p,
        "rwp": rwp, "boutp": boutp,
        "ebias": np.pad(np.asarray(inputs["L0_ffn_expert_bias"], f32).reshape(8, 1),
                        ((0, 120), (0, 0))),
        "bl1": np.asarray(inputs["L0_attn_bl1"], f32).reshape(64, 1),
        "bl2": np.asarray(inputs["L0_attn_bl2"], f32).reshape(16, 1),
        "wl1row": np.pad(np.asarray(inputs["L0_attn_Wl1"], f32).T, ((0, 15), (0, 0))),
        "wl2t": np.ascontiguousarray(np.asarray(inputs["L0_attn_Wl2"], f32).T),
        "idx8": np.tile(np.arange(8, dtype=f32), (128, 1)),
    }
    x_all = np.ascontiguousarray(np.asarray(inputs["L0_attn_x"], f32)).reshape(
        16 * 2048, D)
    return common, x_all


def kernel(**inputs):
    common, x_all = _pack(inputs)
    nc, stack, _ = build_program(debug=False)
    in_maps = []
    for c in range(NCORES):
        m = dict(common)
        m["x"] = np.ascontiguousarray(x_all[c * T_CORE:(c + 1) * T_CORE])
        in_maps.append(m)
    res = run_bass_kernel_spmd(nc, in_maps, core_ids=list(range(NCORES)))
    out = np.concatenate([r["y"] for r in res.results], axis=0)
    return out.reshape(16, 2048, D)


# version 14
# speedup vs baseline: 1.3204x; 1.0173x over previous
import contextlib
import numpy as np
import concourse.bass as bass
import concourse.mybir as mybir
from concourse.bass_utils import run_bass_kernel_spmd

F32 = mybir.dt.float32
BF16 = mybir.dt.bfloat16
AF = mybir.ActivationFunctionType
ALU = mybir.AluOpType
AX = mybir.AxisListType

NCORES = 8
D = 1024
T_CORE = 4096
TILE = 512
N_TILES = T_CORE // TILE
SUB = TILE // 128
CC = D // 128
HID = 2048
NQ = 4
HQ = HID // NQ
HEADS = 16
DH = 64
CS = 256
NCH = TILE // CS
NEXP = 10
NSH = 2
NCHSP = 23
NSLSP = NCHSP * 512
BIG = 1.0e30
HB = 4096


class _Op:
    __slots__ = ("eng", "kind", "ch", "fn", "deps", "need_inc", "val", "seq")

    def __init__(self, eng, kind, ch, fn, deps, seq):
        self.eng, self.kind, self.ch, self.fn = eng, kind, ch, fn
        self.deps = deps
        self.need_inc = kind == "d"
        self.val = None
        self.seq = seq


class Prog:
    ENGINES = ("tensor", "vector", "scalar", "gpsimd", "sync")

    def __init__(self):
        self.ops = []

    def op(self, eng, fn, deps=()):
        o = _Op(eng, "c", "", fn, [d for d in deps if d is not None], len(self.ops))
        self.ops.append(o)
        return o

    def dma(self, eng, fn, deps=(), ch=""):
        o = _Op(eng, "d", ch, fn, [d for d in deps if d is not None], len(self.ops))
        self.ops.append(o)
        return o

    def emit(self, nc, stack):
        for o in self.ops:
            for d in o.deps:
                d.need_inc = True
        cnt = {}
        for o in self.ops:
            if o.need_inc:
                key = (o.eng, o.kind, o.ch)
                step = 16 if o.kind == "d" else 1
                cnt[key] = cnt.get(key, 0) + step
                o.val = (key, cnt[key])
        sems = {k: stack.enter_context(nc.semaphore(f"s_{k[0]}_{k[1]}{k[2]}"))
                for k in cnt}
        block = stack.enter_context(nc.Block())
        per_eng = {e: [o for o in self.ops if o.eng == e] for e in self.ENGINES}
        for eng_name in self.ENGINES:
            if not per_eng[eng_name]:
                continue

            def make(engn):
                ops = per_eng[engn]

                def emitter(eng):
                    waited = {}
                    for o in ops:
                        need = {}
                        for d in o.deps:
                            if d.val is None:
                                continue
                            key, v = d.val
                            if v > need.get(key, 0):
                                need[key] = v
                        for key, v in need.items():
                            if waited.get(key, 0) >= v:
                                continue
                            eng.wait_ge(sems[key], v)
                            waited[key] = v
                        instr = o.fn(eng)
                        if o.need_inc:
                            key, _ = o.val
                            instr.then_inc(sems[key], 16 if o.kind == "d" else 1)
                return emitter
            getattr(block, eng_name)(make(eng_name))


class Reg:
    __slots__ = ("w", "r")

    def __init__(self):
        self.w = []
        self.r = []

    def for_read(self):
        return list(self.w)

    def for_write(self):
        return list(self.w) + list(self.r)

    def did_read(self, op):
        self.r.append(op)

    def did_write(self, op):
        if self.r:
            self.w = [op]
            self.r = []
        else:
            self.w.append(op)


def build_program(debug=False):
    nc = bass.Bass("TRN2", target_bir_lowering=False)
    p = Prog()
    stack = contextlib.ExitStack()
    E = stack.enter_context

    def din(name, shape):
        return nc.dram_tensor(name, list(shape), F32, kind="ExternalInput").ap()

    x_in = din("x", (T_CORE, D))
    wqkvp = din("wqkvp", (6, 128, CC * 512))
    woutp = din("woutp", (2, 128, CC * 512))
    w1p = din("w1p", (NEXP, NQ, 128, CC * HQ))
    w3p = din("w3p", (NEXP, NQ, 128, CC * HQ))
    w2p = din("w2p", (NEXP, NQ, 128, NQ * D))
    rwp = din("rwp", (128, CC * 8))
    boutp = din("boutp", (128, CC))
    ebias_in = din("ebias", (128, 1))
    bl1_in = din("bl1", (64, 1))
    bl2_in = din("bl2", (16, 1))
    wl1row = din("wl1row", (16, 64))
    wl2t = din("wl2t", (64, 16))
    idx8_in = din("idx8", (128, 8))

    y_out = nc.dram_tensor("y", [T_CORE, D], F32, kind="ExternalOutput").ap()
    lam_scr = nc.dram_tensor("lam_scr", [HEADS, 2048], F32, kind="Internal").ap()
    cw_scr = nc.dram_tensor("cw_scr", [8, TILE], F32, kind="Internal").ap()
    wqkv_s = nc.dram_tensor("wqkv_s", [6, 128, 2 * HB], BF16, kind="Internal").ap()
    w1_s = nc.dram_tensor("w1_s", [NEXP, NQ, 128, 2 * HB], BF16, kind="Internal").ap()
    w3_s = nc.dram_tensor("w3_s", [NEXP, NQ, 128, 2 * HB], BF16, kind="Internal").ap()
    w2_s = nc.dram_tensor("w2_s", [NEXP, NQ, 128, 2 * HB], BF16, kind="Internal").ap()
    xn_scr = nc.dram_tensor("xn_scr", [T_CORE, D], F32, kind="Internal").ap()
    ysh_scr = nc.dram_tensor("ysh_scr", [N_TILES, 128, CC * TILE], F32, kind="Internal").ap()
    og_scr = nc.dram_tensor("og_scr", [NSLSP, D], F32, kind="Internal").ap()
    tok_scr = nc.dram_tensor("tok_scr", [NSLSP, 1], F32, kind="Internal").ap()
    cw2_scr = nc.dram_tensor("cw2_scr", [NSLSP, 1], F32, kind="Internal").ap()
    idx_scr = nc.dram_tensor("idx_scr", [N_TILES, TILE, 6], F32, kind="Internal").ap()

    dbg = {}
    if debug:
        def dout(name, shape):
            dbg[name] = nc.dram_tensor("dbg_" + name, list(shape), F32,
                                       kind="ExternalOutput").ap()
        dout("q_t", (128, CC * TILE))
        dout("k_t", (128, CC * TILE))
        dout("v_tok", (128, SUB * D))
        dout("attn_out", (128, CC * TILE))
        dout("y_proj", (128, CC * TILE))
        dout("lg_tok", (128, SUB * 8))
        dout("cw_fm", (8, TILE))
        dout("lam_fm", (HEADS, 2048))
        dout("y_ffn", (128, CC * TILE))

    def S(name, shape, dtype=F32):
        return E(nc.sbuf_tensor(name, list(shape), dtype))

    RXs = S("RX", (128, SUB * D))
    RAs = S("RA", (128, SUB * D))
    RBs = S("RB", (128, SUB * D))
    RDs = S("RD", (128, SUB * D))
    XNs = S("XN", (128, 2 * HB), BF16)
    RYs = S("RY", (128, CC * TILE))
    RHs = S("RH", (128, 4096), BF16)
    SGs = [S(f"SG{i}", (128, TILE)) for i in range(2)]
    LAMs = [S("LAM0", (128, TILE))]
    CWs = [S(f"CW{i}", (128, TILE)) for i in range(2)]
    RSTGs = [S("RSTG0", (128, 512))]
    STATs = S("STAT", (128, 80))
    STATAs = S("STATA", (128, 24))
    LGs = S("LG", (8, TILE))
    LGTs = S("LGT", (128, SUB * 8))
    R1s = S("R1", (128, 8))
    R2s = S("R2", (128, 8))
    R3s = S("R3", (128, 8))
    RSs = S("RS", (128, 16))
    CWTs = S("CWT", (128, SUB * 8))
    CWFs = S("CWF", (8, TILE))
    IDs = S("ID", (128, 128))
    IDX8s = S("IDX8", (128, 8))
    EBs = S("EB", (128, 1))
    BL1s = S("BL1", (64, 1))
    BL2s = S("BL2", (16, 1))
    WL1s = S("WL1", (16, 64))
    WL2s = S("WL2", (64, 16))
    RWs = S("RW", (128, CC * 8))
    BOUTs = S("BOUT", (128, CC))
    LTs = S("LT", (128, 128))
    ONESs = S("ONES", (128, 128))
    IOTAPs = S("IOTAP", (128, 1))
    M8s = S("M8", (128, 8))
    PRKs = S("PRK", (128, 8))
    IDXTs = S("IDXT", (128, SUB * 6))
    RBASEs = S("RBASE", (1, 8))
    SEGs = S("SEG", (8, 4))
    SEGRs = S("SEGR", (1, 8))
    SEGBCs = S("SEGBC", (128, 8))
    KTHs = S("KTH", (8, 24))
    EOCRs = S("EOCR", (1, 24))
    IDXALLs = S("IDXALL", (128, 8 * 8), mybir.dt.uint32)
    TOKSBs = S("TOKSB", (128, 8), mybir.dt.uint32)
    WIDXs = S("WIDX", (128, 96), mybir.dt.uint32)
    VALFs = S("VALF", (128, 2))
    NSLOT = 5
    SLOTs = [S(f"SLOT{i}", (128, 2 * HB), BF16) for i in range(NSLOT)]

    def slotf(i):
        return SLOTs[i][:, :].bitcast(F32)

    PBs = [E(nc.psum_tensor(f"pb{i}", [128, 512], F32)) for i in range(8)]

    RX, RA, RB, RD, RXN, RY, RH = (Reg() for _ in range(7))
    RSG = [Reg(), Reg()]
    RLAM = [Reg()]
    RCW = [Reg(), Reg()]
    RSTG = [Reg()]
    RSTAT = Reg()
    RSTATA3 = [Reg(), Reg(), Reg()]
    RLG, RLGT, RR1, RR2, RR3, RRS, RCWT, RCWF = (Reg() for _ in range(8))
    RCONST = Reg()
    RLAMSCR, RCWSCR, RWSPLIT, RWSPLITR, RWSPLITQ = (Reg() for _ in range(5))
    RXNSCR, RYSH, ROG, RTOKSCR, RCW2SCR, RIDXSCR = (Reg() for _ in range(6))
    RM8, RPRK, RIDXT, RRBASE, RSEG, RSEGR, RSEGBC = (Reg() for _ in range(7))
    RTHR, RKTH, REOCR, RIDXALL, RWIDX, RVAL = (Reg() for _ in range(6))
    RTOKSB2 = [Reg(), Reg()]
    RTOKROW2 = [Reg(), Reg()]
    PB = [Reg() for _ in range(8)]
    SL = [Reg() for _ in range(NSLOT)]

    def do(eng, fn, reads=(), writes=(), extra=()):
        deps = [d for r in reads for d in r.for_read()]
        deps += [d for w in writes for d in w.for_write()]
        deps += [d for d in extra if d is not None]
        o = p.op(eng, fn, deps)
        for r in reads:
            r.did_read(o)
        for w in writes:
            w.did_write(o)
        return o

    def dod(eng, fn, reads=(), writes=(), extra=(), ch=""):
        deps = [d for r in reads for d in r.for_read()]
        deps += [d for w in writes for d in w.for_write()]
        deps += [d for d in extra if d is not None]
        o = p.dma(eng, fn, deps, ch=ch)
        for r in reads:
            r.did_read(o)
        for w in writes:
            w.did_write(o)
        return o

    dod("scalar", lambda e: e.dma_start(
        out=RXs[:, :],
        in_=x_in[0:TILE, :].rearrange("(s p) c -> p s c", p=128)),
        writes=[RX], ch="x")
    for t, ap_ in [(IDX8s, idx8_in), (EBs, ebias_in),
                   (BL1s, bl1_in), (BL2s, bl2_in),
                   (WL1s, wl1row), (WL2s, wl2t),
                   (RWs, rwp), (BOUTs, boutp)]:
        o = dod("sync", lambda e, t=t, a=ap_: e.dma_start(out=t[:, :], in_=a),
                ch="c0")
        RCONST.w.append(o)

    o = do("gpsimd", lambda e: e.iota(IDs[:, :], pattern=[[1, 128]], base=0,
                                      channel_multiplier=-1,
                                      allow_small_or_imprecise_dtypes=True),
           writes=[RCONST])
    do("vector", lambda e: e.tensor_scalar(IDs[:, :], IDs[:, :], 0.0, None,
                                           op0=ALU.is_equal), writes=[RCONST])
    o = do("gpsimd", lambda e: e.iota(LTs[:, :], pattern=[[1, 128]], base=0,
                                      channel_multiplier=-1,
                                      allow_small_or_imprecise_dtypes=True),
           writes=[RCONST])
    do("vector", lambda e: e.tensor_scalar(LTs[:, :], LTs[:, :], 0.0, None,
                                           op0=ALU.is_gt), writes=[RCONST])
    do("vector", lambda e: e.memset(ONESs[:, :], 1.0), writes=[RCONST])
    do("gpsimd", lambda e: e.iota(IOTAPs[:, :], pattern=[[0, 1]], base=0,
                                  channel_multiplier=1,
                                  allow_small_or_imprecise_dtypes=True),
       writes=[RCONST])
    do("vector", lambda e: e.memset(RBASEs[:, :], 0.0), writes=[RRBASE])
    zop = do("vector", lambda e: e.memset(M8s[:, :], 0.0), writes=[RM8])
    for zi in range(NCHSP):
        dod("gpsimd", lambda e, zi=zi: e.dma_start(
            out=tok_scr[zi * 512:(zi + 1) * 512].rearrange("(p a) b -> p (a b)", p=128),
            in_=M8s[:, 0:4]), reads=[RM8], writes=[RTOKSCR], ch="z")
        dod("gpsimd", lambda e, zi=zi: e.dma_start(
            out=cw2_scr[zi * 512:(zi + 1) * 512].rearrange("(p a) b -> p (a b)", p=128),
            in_=M8s[:, 0:4]), reads=[RM8], writes=[RCW2SCR], ch="z")

    h1v = RYs[0:64, 0:2048]
    lamv = RYs[0:16, 2048:4096]
    lp_hi = RDs[64:65, 0:2048]
    lpv = RDs[0:1, 0:2048]
    o = do("gpsimd", lambda e: e.iota(lp_hi, pattern=[[1, 2048]], base=1,
                                      channel_multiplier=0,
                                      allow_small_or_imprecise_dtypes=True),
           writes=[RD])
    do("scalar", lambda e: e.activation(lpv, lp_hi, AF.Ln),
       reads=[RD], writes=[RD])
    for sl in range(4):
        mm = do("tensor", lambda e, sl=sl: e.matmul(
            PBs[0][0:64, :], WL1s[0:1, :], lpv[0:1, sl * 512:(sl + 1) * 512],
            start=True, stop=True), reads=[RCONST, RD], writes=[PB[0]])
        do("scalar", lambda e, sl=sl: e.activation(
            h1v[:, sl * 512:(sl + 1) * 512], PBs[0][0:64, :], AF.Relu,
            bias=BL1s[:, 0:1]), reads=[PB[0], RCONST], writes=[RY])
    for sl in range(4):
        mm = do("tensor", lambda e, sl=sl: e.matmul(
            PBs[1][0:16, :], WL2s[:, :], h1v[:, sl * 512:(sl + 1) * 512],
            start=True, stop=True), reads=[RCONST, RY], writes=[PB[1]])
        do("scalar", lambda e, sl=sl: e.activation(
            lamv[:, sl * 512:(sl + 1) * 512], PBs[1][0:16, :], AF.Sigmoid,
            bias=BL2s[:, 0:1]), reads=[PB[1], RCONST], writes=[RY])
    dod("scalar", lambda e: e.dma_start(out=lam_scr, in_=lamv[:, :]),
        reads=[RY], writes=[RLAMSCR], ch="lsm")
    if debug:
        dod("scalar", lambda e: e.dma_start(out=dbg["lam_fm"], in_=lamv[:, :]),
            reads=[RY], ch="dbg")

    units = [(wqkvp[ms], wqkv_s[ms]) for ms in range(6)]
    for ei in range(NSH):
        for q in range(NQ):
            units.append((w1p[ei, q], w1_s[ei, q]))
            units.append((w3p[ei, q], w3_s[ei, q]))
            units.append((w2p[ei, q], w2_s[ei, q]))
    for ei in range(NSH, NEXP):
        for q in range(NQ):
            units.append((w1p[ei, q], w1_s[ei, q]))
            units.append((w3p[ei, q], w3_s[ei, q]))
            units.append((w2p[ei, q], w2_s[ei, q]))

    def record_split_unit(u, src_, dst, routed):
        a, b = 3, 4
        dod("gpsimd", lambda e, src_=src_, a=a: e.dma_start(out=slotf(a), in_=src_),
            writes=[SL[a]], ch=f"sl{a}")
        do("gpsimd", lambda e, a=a, b=b: e.tensor_copy(
            SLOTs[b][:, 0:HB], slotf(a)),
           reads=[SL[a]], writes=[SL[b]])
        do("gpsimd", lambda e, a=a, b=b: e.tensor_tensor(
            SLOTs[b][:, HB:2 * HB], slotf(a), SLOTs[b][:, 0:HB], op=ALU.subtract),
           reads=[SL[a]], writes=[SL[b]])
        wreg = RWSPLITQ if u < 6 else (RWSPLITR if routed else RWSPLIT)
        dod("gpsimd", lambda e, b=b, dst=dst: e.dma_start(out=dst, in_=SLOTs[b][:, :]),
            reads=[SL[b]], writes=[wreg], ch=f"ws{b}")

    for u, (src_, dst) in enumerate(units):
        record_split_unit(u, src_, dst, routed=(u >= 30))

    qkv_bank_rr = [4, 5]
    att_bank_rr = [3, 4, 5, 6, 7]
    rr_state = {"qkv": 0, "att": 0, "slot": 0, "slot5": 0, "gu": 0, "w2": 0,
                "stg": 0}

    def next_bank(pool, key):
        i = pool[rr_state[key] % len(pool)]
        rr_state[key] += 1
        return i

    def next_slot():
        i = rr_state["slot"] % 3
        rr_state["slot"] += 1
        return i

    def next_slot5():
        i = rr_state["slot5"] % NSLOT
        rr_state["slot5"] += 1
        return i

    from concourse.bass import IndirectOffsetOnAxis as IOA

    def record_index_finalize():
        cdep = RRBASE.for_read() + RCONST.for_read() + PB[6].for_write()
        c1 = p.op("tensor", lambda e: e.matmul(PBs[6][0:8, 0:1], RBASEs[0:1, :],
                                               ONESs[0:1, 0:1], start=True, stop=True),
                  deps=cdep)
        RRBASE.did_read(c1)
        PB[6].did_write(c1)
        do("vector", lambda e: e.tensor_copy(SEGs[0:8, 0:1], PBs[6][0:8, 0:1]),
           reads=[PB[6]], writes=[RSEG])
        do("gpsimd", lambda e: e.iota(KTHs[0:8, 0:16], pattern=[[512, 16]], base=0,
                                      channel_multiplier=0,
                                      allow_small_or_imprecise_dtypes=True),
           writes=[RKTH])
        do("vector", lambda e: e.tensor_scalar(KTHs[0:8, 0:16], KTHs[0:8, 0:16],
                                               SEGs[0:8, 0:1], None, op0=ALU.is_lt),
           reads=[RSEG], writes=[RKTH])
        do("vector", lambda e: e.tensor_reduce(SEGs[0:8, 1:2], KTHs[0:8, 0:16],
                                               axis=AX.X, op=ALU.add),
           reads=[RKTH], writes=[RSEG])
        do("vector", lambda e: e.tensor_scalar(SEGs[0:8, 1:2], SEGs[0:8, 1:2], 512.0,
                                               None, op0=ALU.mult), writes=[RSEG])
        sdep = RSEG.for_read() + RCONST.for_read() + PB[6].for_write()
        smm = p.op("tensor", lambda e: e.matmul(PBs[6][0:8, 4:5], LTs[0:8, 0:8],
                                                SEGs[0:8, 1:2], start=True, stop=True),
                   deps=sdep)
        RSEG.did_read(smm)
        PB[6].did_write(smm)
        do("vector", lambda e: e.tensor_copy(SEGs[0:8, 2:3], PBs[6][0:8, 4:5]),
           reads=[PB[6]], writes=[RSEG])
        do("vector", lambda e: e.tensor_tensor(SEGs[0:8, 3:4], SEGs[0:8, 2:3],
                                               SEGs[0:8, 1:2], op=ALU.add), writes=[RSEG])
        rdep = RSEG.for_read() + RCONST.for_read() + PB[6].for_write()
        rmm = p.op("tensor", lambda e: e.matmul(PBs[6][0:1, 8:16], SEGs[0:8, 2:3],
                                                IDs[0:8, 0:8], start=True, stop=True),
                   deps=rdep)
        RSEG.did_read(rmm)
        PB[6].did_write(rmm)
        do("vector", lambda e: e.tensor_copy(SEGRs[0:1, :], PBs[6][0:1, 8:16]),
           reads=[PB[6]], writes=[RSEGR])
        bdep = RSEGR.for_read() + RCONST.for_read() + PB[6].for_write()
        bmm = p.op("tensor", lambda e: e.matmul(PBs[6][:, 16:24], ONESs[0:1, :],
                                                SEGRs[0:1, :], start=True, stop=True),
                   deps=bdep)
        RSEGR.did_read(bmm)
        PB[6].did_write(bmm)
        do("vector", lambda e: e.tensor_copy(SEGBCs[:, :], PBs[6][:, 16:24]),
           reads=[PB[6]], writes=[RSEGBC])
        do("gpsimd", lambda e: e.iota(KTHs[:, :], pattern=[[512, 24]], base=0,
                                      channel_multiplier=0,
                                      allow_small_or_imprecise_dtypes=True),
           writes=[RKTH])
        do("vector", lambda e: e.tensor_scalar(IDXTs[0:8, 0:24], KTHs[0:8, :],
                                               SEGs[0:8, 2:3], None, op0=ALU.is_ge),
           reads=[RKTH, RSEG], writes=[RIDXT])
        do("vector", lambda e: e.tensor_scalar(KTHs[0:8, :], KTHs[0:8, :],
                                               SEGs[0:8, 3:4], None, op0=ALU.is_lt),
           reads=[RSEG], writes=[RKTH])
        do("vector", lambda e: e.tensor_tensor(IDXTs[0:8, 0:24], IDXTs[0:8, 0:24],
                                               KTHs[0:8, :], op=ALU.mult),
           reads=[RKTH], writes=[RIDXT])
        edep = RIDXT.for_read() + RCONST.for_read() + PB[6].for_write()
        emm = p.op("tensor", lambda e: e.matmul(PBs[6][0:1, 24:48], IOTAPs[0:8, 0:1],
                                                IDXTs[0:8, 0:24], start=True, stop=True),
                   deps=edep)
        RIDXT.did_read(emm)
        PB[6].did_write(emm)
        do("vector", lambda e: e.tensor_copy(EOCRs[0:1, :], PBs[6][0:1, 24:48]),
           reads=[PB[6]], writes=[REOCR])
        for k in range(NCHSP):
            wdep = REOCR.for_read() + RCONST.for_read() + PB[7].for_write()
            wmm = p.op("tensor", lambda e, k=k: e.matmul(
                PBs[7][:, 8:9], ONESs[0:1, :], EOCRs[0:1, k:k + 1],
                start=True, stop=True), deps=wdep)
            REOCR.did_read(wmm)
            PB[7].did_write(wmm)
            for q in range(NQ):
                do("vector", lambda e, q=q: e.tensor_scalar(
                    VALFs[:, 0:1], PBs[7][:, 8:9], 512.0, float(q * 128 + NSH * NQ * 128),
                    op0=ALU.mult, op1=ALU.add), reads=[PB[7]], writes=[RVAL])
                do("vector", lambda e: e.tensor_tensor(
                    VALFs[:, 0:1], VALFs[:, 0:1], IOTAPs[:, :], op=ALU.add),
                   reads=[RCONST], writes=[RVAL])
                do("vector", lambda e, k=k, q=q: e.tensor_copy(
                    WIDXs[:, k * 4 + q:k * 4 + q + 1], VALFs[:, 0:1]),
                   reads=[RVAL], writes=[RWIDX])

        for tile in range(N_TILES):
            t0 = tile * TILE
            dod("scalar", lambda e, tile=tile: e.dma_start(
                out=IDXTs[:, :].rearrange("p (s r) -> p s r", s=SUB),
                in_=idx_scr[tile].rearrange("(s p) r -> p s r", p=128)),
                reads=[RIDXSCR], writes=[RIDXT], ch="ix")
            for s in range(SUB):
                for li in range(2):
                    icol = tile * 8 + s * 2 + li
                    do("vector", lambda e, s=s, li=li: e.tensor_scalar(
                        R1s[:, :], IDX8s[:, :], IDXTs[:, s * 6 + 2 + li:s * 6 + 3 + li],
                        None, op0=ALU.is_equal),
                       reads=[RCONST, RIDXT], writes=[RR1])
                    do("vector", lambda e: e.tensor_tensor(
                        R3s[:, :], R1s[:, :], SEGBCs[:, :], op=ALU.mult),
                       reads=[RR1, RSEGBC], writes=[RR3])
                    do("vector", lambda e: e.tensor_reduce(
                        RSs[:, 0:1], R3s[:, :], axis=AX.X, op=ALU.add),
                       reads=[RR3], writes=[RRS])
                    do("vector", lambda e, s=s, li=li: e.tensor_tensor(
                        VALFs[:, 0:1], RSs[:, 0:1], IDXTs[:, s * 6 + li:s * 6 + 1 + li],
                        op=ALU.add), reads=[RRS, RIDXT], writes=[RVAL])
                    do("vector", lambda e, icol=icol: e.tensor_copy(
                        IDXALLs[:, icol:icol + 1], VALFs[:, 0:1]),
                       reads=[RVAL], writes=[RIDXALL])
                    do("vector", lambda e, t0=t0, s=s: e.tensor_scalar(
                        VALFs[:, 1:2], IOTAPs[:, :], float(t0 + s * 128), None,
                        op0=ALU.add), reads=[RCONST], writes=[RVAL])
                    dod("gpsimd", lambda e, icol=icol: e.indirect_dma_start(
                        out=tok_scr, out_offset=IOA(IDXALLs[:, icol:icol + 1], 0),
                        in_=VALFs[:, 1:2], in_offset=None),
                        reads=[RVAL, RIDXALL], writes=[RTOKSCR], ch="sc")
                    dod("gpsimd", lambda e, icol=icol, s=s, li=li: e.indirect_dma_start(
                        out=cw2_scr, out_offset=IOA(IDXALLs[:, icol:icol + 1], 0),
                        in_=IDXTs[:, s * 6 + 4 + li:s * 6 + 5 + li], in_offset=None),
                        reads=[RIDXT, RIDXALL], writes=[RCW2SCR], ch="sc")



    store_ops = []

    def emit_x_ln(t):
        t0n = t * TILE
        if t > 0:
            dod("scalar", lambda e, t0n=t0n: e.dma_start(
                out=RXs[:, :],
                in_=x_in[t0n:t0n + TILE, :].rearrange("(s p) c -> p s c", p=128)),
                writes=[RX], ch="x")
        for s in range(SUB):
            xsl = RXs[:, s * D:(s + 1) * D]
            c0 = 20 * s
            do("vector", lambda e, xsl=xsl, c0=c0: e.bn_stats(
                STATs[:, c0:c0 + 6], xsl[:, 0:512]), reads=[RX], writes=[RSTAT])
            do("vector", lambda e, xsl=xsl, c0=c0: e.bn_stats(
                STATs[:, c0 + 6:c0 + 12], xsl[:, 512:1024]),
               reads=[RX], writes=[RSTAT])
            do("vector", lambda e, c0=c0: e.bn_aggr(
                STATs[:, c0 + 12:c0 + 14], STATs[:, c0:c0 + 12]), writes=[RSTAT])
            do("vector", lambda e, c0=c0: e.tensor_scalar(
                STATs[:, c0 + 14:c0 + 15], STATs[:, c0 + 13:c0 + 14], 1e-5, None,
                op0=ALU.add), writes=[RSTAT])
            do("scalar", lambda e, c0=c0: e.activation(
                STATs[:, c0 + 15:c0 + 16], STATs[:, c0 + 14:c0 + 15], AF.Sqrt),
               writes=[RSTAT])
            do("vector", lambda e, c0=c0: e.reciprocal(
                STATs[:, c0 + 16:c0 + 17], STATs[:, c0 + 15:c0 + 16]),
               writes=[RSTAT])
            do("vector", lambda e, xsl=xsl, s=s, c0=c0: e.tensor_scalar(
                RAs[:, s * D:(s + 1) * D], xsl, STATs[:, c0 + 12:c0 + 13],
                STATs[:, c0 + 16:c0 + 17], op0=ALU.subtract, op1=ALU.mult),
               reads=[RX, RSTAT], writes=[RA])

    emit_x_ln(0)

    for tile in range(N_TILES):
        t0 = tile * TILE
        pos0 = t0 % 2048

        def xn_3d(h):
            return XNs[:, h * HB:(h + 1) * HB].rearrange(
                "p (c t) -> p c t", c=CC)

        for g4 in range(SUB * CC // 4):
            s, ch4 = divmod(g4, 2)
            cc0 = ch4 * 4
            bk = 6 + (g4 % 2)
            for q in range(4):
                cc = cc0 + q
                do("tensor", lambda e, s=s, cc=cc, bk=bk, q=q: e.transpose(
                    PBs[bk][:, q * 128:(q + 1) * 128],
                    RAs[:, s * D + cc * 128:s * D + cc * 128 + 128], IDs[:, :]),
                    reads=[RA, RCONST], writes=[PB[bk]])
            pview = PBs[bk][:, :].rearrange("p (q c) -> p q c", q=4)
            do("scalar", lambda e, s=s, cc0=cc0, bk=bk, pview=pview: e.activation(
                xn_3d(0)[:, cc0:cc0 + 4, s * 128:(s + 1) * 128], pview, AF.Copy),
                reads=[PB[bk]], writes=[RXN])
            do("vector", lambda e, s=s, cc0=cc0, bk=bk, pview=pview: e.tensor_tensor(
                xn_3d(1)[:, cc0:cc0 + 4, s * 128:(s + 1) * 128], pview,
                xn_3d(0)[:, cc0:cc0 + 4, s * 128:(s + 1) * 128], op=ALU.subtract),
               reads=[PB[bk], RXN], writes=[RXN])
            do("vector", lambda e, bk=bk: e.tensor_copy(
                RSTGs[0][:, :], PBs[bk][:, :]),
               reads=[PB[bk]], writes=[RSTG[0]])
            for q in range(4):
                cc = cc0 + q
                lg_deps = (RCONST.for_read() + RSTG[0].for_read()
                           + (PB[3].for_write() if cc == 0 else []))
                lgmm = p.op("tensor", lambda e, s=s, cc=cc, q=q: e.matmul(
                    PBs[3][0:8, s * 128:(s + 1) * 128], RWs[:, cc * 8:(cc + 1) * 8],
                    RSTGs[0][:, q * 128:(q + 1) * 128],
                    start=(cc == 0), stop=(cc == CC - 1)),
                    deps=lg_deps)
                RSTG[0].did_read(lgmm)
                if cc == CC - 1:
                    PB[3].did_write(lgmm)
        do("vector", lambda e: e.tensor_scalar(
            LGs[:, :], PBs[3][0:8, :], EBs[0:8, 0:1], None, op0=ALU.add),
           reads=[PB[3], RCONST], writes=[RLG])

        dod("scalar", lambda e, t0=t0: e.dma_start(
            out=xn_scr[t0:t0 + TILE, :].rearrange("(s p) c -> p s c", p=128),
            in_=RAs[:, :]), reads=[RA], writes=[RXNSCR], ch="xn")

        for tc in range(SUB):
            do("tensor", lambda e, tc=tc: e.transpose(
                PBs[6][:, tc * 8:(tc + 1) * 8],
                LGs[:, tc * 128:(tc + 1) * 128], IDs[0:8, 0:8]),
               reads=[RLG, RCONST], writes=[PB[6]])
        do("vector", lambda e: e.tensor_copy(
            LGTs[:, :], PBs[6][:, 0:SUB * 8]), reads=[PB[6]], writes=[RLGT])
        if debug and tile == 0:
            dod("scalar", lambda e: e.dma_start(out=dbg["lg_tok"], in_=LGTs[:, :]),
                reads=[RLGT], ch="dbg")
        for tc in range(SUB):
            lg = LGTs[:, tc * 8:(tc + 1) * 8]
            do("vector", lambda e, lg=lg: e.tensor_reduce(
                RSs[:, 0:1], lg, axis=AX.X, op=ALU.max), reads=[RLGT], writes=[RRS])
            do("vector", lambda e, lg=lg: e.tensor_scalar(
                R1s[:, :], lg, RSs[:, 0:1], None, op0=ALU.is_ge),
               reads=[RLGT, RRS], writes=[RR1])
            do("vector", lambda e: e.scalar_tensor_tensor(
                R2s[:, :], IDX8s[:, :], 9.0, R1s[:, :], op0=ALU.subtract, op1=ALU.mult),
               reads=[RCONST, RR1], writes=[RR2])
            do("vector", lambda e: e.tensor_reduce(
                RSs[:, 1:2], R2s[:, :], axis=AX.X, op=ALU.min),
               reads=[RR2], writes=[RRS])
            do("vector", lambda e: e.tensor_scalar(
                RSs[:, 1:2], RSs[:, 1:2], 9.0, None, op0=ALU.add), writes=[RRS])
            do("vector", lambda e: e.tensor_scalar(
                R1s[:, :], IDX8s[:, :], RSs[:, 1:2], None, op0=ALU.is_equal),
               reads=[RCONST, RRS], writes=[RR1])
            do("vector", lambda e, lg=lg: e.scalar_tensor_tensor(
                R3s[:, :], R1s[:, :], -BIG, lg, op0=ALU.mult, op1=ALU.add),
               reads=[RR1, RLGT], writes=[RR3])
            do("vector", lambda e: e.tensor_reduce(
                RSs[:, 2:3], R3s[:, :], axis=AX.X, op=ALU.max),
               reads=[RR3], writes=[RRS])
            do("vector", lambda e: e.tensor_scalar(
                R2s[:, :], R3s[:, :], RSs[:, 2:3], None, op0=ALU.is_ge),
               reads=[RR3, RRS], writes=[RR2])
            do("vector", lambda e: e.scalar_tensor_tensor(
                R2s[:, :], IDX8s[:, :], 9.0, R2s[:, :], op0=ALU.subtract, op1=ALU.mult),
               reads=[RCONST], writes=[RR2])
            do("vector", lambda e: e.tensor_reduce(
                RSs[:, 3:4], R2s[:, :], axis=AX.X, op=ALU.min),
               reads=[RR2], writes=[RRS])
            do("vector", lambda e: e.tensor_scalar(
                RSs[:, 3:4], RSs[:, 3:4], 9.0, None, op0=ALU.add), writes=[RRS])
            do("vector", lambda e: e.tensor_scalar(
                R2s[:, :], IDX8s[:, :], RSs[:, 3:4], None, op0=ALU.is_equal),
               reads=[RCONST, RRS], writes=[RR2])
            do("vector", lambda e: e.tensor_tensor(
                M8s[:, :], R1s[:, :], R2s[:, :], op=ALU.add),
               reads=[RR1, RR2], writes=[RM8])
            pf_deps = (RCONST.for_read() + RM8.for_read() + RRBASE.for_read()
                       + PB[3].for_write())
            pf1 = p.op("tensor", lambda e, tc=tc: e.matmul(
                PBs[3][:, tc * 8:(tc + 1) * 8], LTs[:, :], M8s[:, :],
                start=True, stop=False), deps=pf_deps)
            pf2 = p.op("tensor", lambda e, tc=tc: e.matmul(
                PBs[3][:, tc * 8:(tc + 1) * 8], ONESs[0:1, :], RBASEs[0:1, :],
                start=False, stop=True), deps=[])
            RM8.did_read(pf2)
            RRBASE.did_read(pf2)
            PB[3].did_write(pf2)
            cnt_deps = RM8.for_read() + RCONST.for_read() + PB[3].for_write()
            cntmm = p.op("tensor", lambda e, tc=tc: e.matmul(
                PBs[3][0:1, 32 + tc * 8:40 + tc * 8], ONESs[:, 0:1], M8s[:, :],
                start=True, stop=True), deps=cnt_deps)
            RM8.did_read(cntmm)
            PB[3].did_write(cntmm)
            do("vector", lambda e, tc=tc: e.tensor_copy(
                PRKs[:, :], PBs[3][:, tc * 8:(tc + 1) * 8]),
               reads=[PB[3]], writes=[RPRK])
            do("vector", lambda e: e.tensor_tensor(
                R3s[:, :], R1s[:, :], PRKs[:, :], op=ALU.mult),
               reads=[RR1, RPRK], writes=[RR3])
            do("vector", lambda e, tc=tc: e.tensor_reduce(
                IDXTs[:, tc * 6:tc * 6 + 1], R3s[:, :], axis=AX.X, op=ALU.add),
               reads=[RR3], writes=[RIDXT])
            do("vector", lambda e: e.tensor_tensor(
                R3s[:, :], R2s[:, :], PRKs[:, :], op=ALU.mult),
               reads=[RR2, RPRK], writes=[RR3])
            do("vector", lambda e, tc=tc: e.tensor_reduce(
                IDXTs[:, tc * 6 + 1:tc * 6 + 2], R3s[:, :], axis=AX.X, op=ALU.add),
               reads=[RR3], writes=[RIDXT])
            do("vector", lambda e, tc=tc: e.tensor_tensor(
                RBASEs[0:1, :], RBASEs[0:1, :], PBs[3][0:1, 32 + tc * 8:40 + tc * 8],
                op=ALU.add), reads=[PB[3]], writes=[RRBASE])
            do("vector", lambda e, tc=tc: e.tensor_copy(
                IDXTs[:, tc * 6 + 2:tc * 6 + 3], RSs[:, 1:2]),
               reads=[RRS], writes=[RIDXT])
            do("vector", lambda e, tc=tc: e.tensor_copy(
                IDXTs[:, tc * 6 + 3:tc * 6 + 4], RSs[:, 3:4]),
               reads=[RRS], writes=[RIDXT])
            do("vector", lambda e: e.tensor_tensor(
                RSs[:, 4:5], RSs[:, 0:1], RSs[:, 2:3], op=ALU.subtract), writes=[RRS])
            do("scalar", lambda e: e.activation(
                RSs[:, 4:5], RSs[:, 4:5], AF.Sigmoid), writes=[RRS])
            do("vector", lambda e: e.tensor_scalar(
                RSs[:, 5:6], RSs[:, 4:5], -1.0, 1.0, op0=ALU.mult, op1=ALU.add),
               writes=[RRS])
            do("vector", lambda e: e.tensor_scalar(
                R1s[:, :], R1s[:, :], RSs[:, 4:5], None, op0=ALU.mult),
               reads=[RRS], writes=[RR1])
            do("vector", lambda e, tc=tc: e.scalar_tensor_tensor(
                CWTs[:, tc * 8:(tc + 1) * 8], R2s[:, :], RSs[:, 5:6], R1s[:, :],
                op0=ALU.mult, op1=ALU.add),
               reads=[RR2, RRS, RR1], writes=[RCWT])
            do("vector", lambda e, tc=tc: e.tensor_copy(
                IDXTs[:, tc * 6 + 4:tc * 6 + 5], RSs[:, 4:5]),
               reads=[RRS], writes=[RIDXT])
            do("vector", lambda e, tc=tc: e.tensor_copy(
                IDXTs[:, tc * 6 + 5:tc * 6 + 6], RSs[:, 5:6]),
               reads=[RRS], writes=[RIDXT])
        for tc in range(SUB):
            do("tensor", lambda e, tc=tc: e.transpose(
                PBs[7][0:8, tc * 128:(tc + 1) * 128],
                CWTs[:, tc * 8:(tc + 1) * 8], IDs[:, :]),
               reads=[RCWT, RCONST], writes=[PB[7]])
        do("vector", lambda e: e.tensor_copy(CWFs[:, :], PBs[7][0:8, 0:TILE]),
           reads=[PB[7]], writes=[RCWF])
        dod("scalar", lambda e: e.dma_start(out=cw_scr, in_=CWFs[:, :]),
            reads=[RCWF], writes=[RCWSCR], ch="sm")
        dod("scalar", lambda e, tile=tile: e.dma_start(
            out=idx_scr[tile].rearrange("(s p) r -> p s r", p=128),
            in_=IDXTs[:, :].rearrange("p (s r) -> p s r", s=SUB)),
            reads=[RIDXT], writes=[RIDXSCR], ch="sm2")
        if tile == N_TILES - 1:
            record_index_finalize()
        if debug and tile == 0:
            dod("scalar", lambda e: e.dma_start(out=dbg["cw_fm"], in_=CWFs[:, :]),
                reads=[RCWF], ch="dbg")

        def mm3(bk, lhs_pairs, rhs_pairs, lreads, out_ap=None):
            n = len(lhs_pairs)
            first_deps = ([d for r in lreads for d in r.for_read()]
                          + PB[bk].for_write())
            last = None
            for j in range(n):
                for pi, (lh, rh) in enumerate([(0, 0), (0, 1), (1, 0)]):
                    st_ = (j == 0 and pi == 0)
                    sp_ = (j == n - 1 and pi == 2)
                    last = p.op("tensor",
                                lambda e, j=j, lh=lh, rh=rh, st_=st_, sp_=sp_: e.matmul(
                                    (out_ap or PBs[bk][:, :]),
                                    lhs_pairs[j](lh), rhs_pairs[j](rh),
                                    start=st_, stop=sp_),
                                deps=first_deps if st_ else [])
            for r in lreads:
                r.did_read(last)
            PB[bk].did_write(last)
            return last

        def xn_half(cc, h):
            return XNs[:, h * HB + cc * TILE:h * HB + cc * TILE + TILE]

        def xn_half_tc(cc, tc, h):
            o = h * HB + cc * TILE + tc * 128
            return XNs[:, o:o + 128]

        for ms in range(6):
            sl_i = next_slot()
            dod("sync", lambda e, ms=ms, sl_i=sl_i: e.dma_start(
                out=SLOTs[sl_i][:, :], in_=wqkv_s[ms]), writes=[SL[sl_i]],
                reads=[RWSPLITQ] if tile == 0 else (), ch=f"sl{sl_i}")
            if ms < 4:
                for mc in range(4):
                    mg = ms * 4 + mc
                    bk = next_bank(qkv_bank_rr, "qkv")
                    lhs = [(lambda e_cc: (lambda h, sl_i=sl_i, e_cc=e_cc, mc=mc:
                            SLOTs[sl_i][:, h * HB + e_cc * 512 + mc * 128:
                                         h * HB + e_cc * 512 + mc * 128 + 128]))(cc)
                           for cc in range(CC)]
                    rhs = [(lambda e_cc: (lambda h, e_cc=e_cc: xn_half(e_cc, h)))(cc)
                           for cc in range(CC)]
                    mm3(bk, lhs, rhs, [SL[sl_i], RXN])
                    if mg < 8:
                        hp = mg
                        do("scalar", lambda e, hp=hp, bk=bk: e.activation(
                            RAs[:, hp * TILE:(hp + 1) * TILE], PBs[bk][:, :],
                            AF.Copy, scale=0.125),
                           reads=[PB[bk]], writes=[RA])
                    else:
                        hp = mg - 8
                        lsl = 0
                        for half in range(2):
                            dod("scalar", lambda e, hp=hp, lsl=lsl, pos0=pos0, half=half: e.dma_start(
                                out=LAMs[lsl][half * 64:(half + 1) * 64, :],
                                in_=lam_scr[2 * hp + half:2 * hp + half + 1,
                                            pos0:pos0 + TILE].broadcast_to((64, TILE))),
                                reads=[RLAMSCR], writes=[RLAM[lsl]], ch=f"lam{lsl}")
                        do("vector", lambda e, hp=hp, lsl=lsl, bk=bk: e.tensor_tensor(
                            RBs[:, hp * TILE:(hp + 1) * TILE], PBs[bk][:, :],
                            LAMs[lsl][:, :], op=ALU.mult),
                           reads=[PB[bk], RLAM[lsl]], writes=[RB])
            else:
                vs = ms - 4
                for tc in range(SUB):
                    bk = next_bank(qkv_bank_rr, "qkv")
                    lhs = [(lambda e_cc: (lambda h, e_cc=e_cc, tc=tc:
                            xn_half_tc(e_cc, tc, h)))(cc) for cc in range(CC)]
                    rhs = [(lambda e_cc: (lambda h, sl_i=sl_i, e_cc=e_cc:
                            SLOTs[sl_i][:, h * HB + e_cc * 512:
                                         h * HB + e_cc * 512 + 512]))(cc)
                           for cc in range(CC)]
                    mm3(bk, lhs, rhs, [SL[sl_i], RXN])
                    do("vector", lambda e, tc=tc, vs=vs, bk=bk: e.tensor_copy(
                        RXs[:, tc * D + vs * 512:tc * D + vs * 512 + 512],
                        PBs[bk][:, :]), reads=[PB[bk]], writes=[RX])
        if debug and tile == 0:
            dod("scalar", lambda e: e.dma_start(out=dbg["q_t"], in_=RAs[:, :]),
                reads=[RA], ch="dbg")
            dod("scalar", lambda e: e.dma_start(out=dbg["k_t"], in_=RBs[:, :]),
                reads=[RB], ch="dbg")
            dod("scalar", lambda e: e.dma_start(out=dbg["v_tok"], in_=RXs[:, :]),
                reads=[RX], ch="dbg")

        def att_ap(pr, a, b):
            return RYs[:, pr * 1024 + a:pr * 1024 + b]

        def atn_ap(pr, a, b):
            return RYs[:, pr * 1024 + 512 + a:pr * 1024 + 512 + b]

        att_reg = [RY, RY, RY]
        atn_reg = [RY, RY, RY]

        def emit_scores_softmax(it):
            ci, h = divmod(it, HEADS)
            hp, off = h // 2, 64 * (h % 2)
            pr = it % 3
            par = pr * 8
            bkS = next_bank(att_bank_rr, "att")
            for qs in range(2):
                do("tensor", lambda e, qs=qs, ci=ci, hp=hp, off=off, bkS=bkS: e.matmul(
                    PBs[bkS][:, qs * CS:(qs + 1) * CS],
                    RAs[off:off + 64,
                        hp * TILE + ci * CS + qs * 128:hp * TILE + ci * CS + qs * 128 + 128],
                    RBs[off:off + 64, hp * TILE + ci * CS:hp * TILE + ci * CS + CS],
                    start=True, stop=True),
                   reads=[RA, RB], writes=[PB[bkS]])
            do("vector", lambda e, bkS=bkS, par=par, pr=pr: e.tensor_reduce(
                STATAs[:, par:par + 2],
                PBs[bkS][:, :].rearrange("p (a b) -> p a b", a=2),
                axis=AX.X, op=ALU.max, negate=True),
               reads=[PB[bkS]], writes=[RSTATA3[pr]])
            for qs in range(2):
                do("scalar", lambda e, qs=qs, bkS=bkS, par=par, pr=pr: e.activation(
                    att_ap(pr, qs * CS, (qs + 1) * CS),
                    PBs[bkS][:, qs * CS:(qs + 1) * CS],
                    AF.Exp, bias=STATAs[:, par + qs:par + qs + 1],
                    accum_out=STATAs[:, par + 2 + qs:par + 3 + qs]),
                   reads=[PB[bkS], RSTATA3[pr]], writes=[att_reg[pr], RSTATA3[pr]])
            do("vector", lambda e, par=par: e.reciprocal(
                STATAs[:, par + 4:par + 6], STATAs[:, par + 2:par + 4]),
               writes=[RSTATA3[pr]])
            for qs in range(2):
                do("vector", lambda e, qs=qs, par=par, pr=pr: e.tensor_scalar(
                    atn_ap(pr, qs * CS, (qs + 1) * CS),
                    att_ap(pr, qs * CS, (qs + 1) * CS),
                    STATAs[:, par + 4 + qs:par + 5 + qs], None, op0=ALU.mult),
                   reads=[att_reg[pr], RSTATA3[pr]], writes=[atn_reg[pr]])

        def emit_transpose_av(it):
            ci, h = divmod(it, HEADS)
            hp, off = h // 2, 64 * (h % 2)
            pr = it % 3
            bkT = next_bank(att_bank_rr, "att")
            for qs in range(2):
                for kc in range(2):
                    do("tensor", lambda e, qs=qs, kc=kc, bkT=bkT, pr=pr: e.transpose(
                        PBs[bkT][:, (qs * 2 + kc) * 128:(qs * 2 + kc) * 128 + 128],
                        atn_ap(pr, qs * CS + kc * 128, qs * CS + kc * 128 + 128),
                        IDs[:, :]),
                       reads=[atn_reg[pr], RCONST], writes=[PB[bkT]])
            for qs in range(2):
                do("vector", lambda e, qs=qs, bkT=bkT, pr=pr: e.tensor_copy(
                    att_ap(pr, 0, 512).rearrange("p (k c) -> p k c", k=2)
                    [:, :, qs * 128:(qs + 1) * 128],
                    PBs[bkT][:, qs * 256:(qs + 1) * 256].rearrange(
                        "p (k c) -> p k c", k=2)),
                   reads=[PB[bkT]], writes=[att_reg[pr]])
            bkA = next_bank(att_bank_rr, "att")
            av_deps = (RX.for_read() + att_reg[pr].for_read()
                       + PB[bkA].for_write())
            avmm = None
            for kc in range(2):
                s = ci * 2 + kc
                avmm = p.op("tensor", lambda e, s=s, h=h, kc=kc, bkA=bkA, pr=pr: e.matmul(
                    PBs[bkA][0:64, 0:CS],
                    RXs[:, s * D + h * DH:s * D + h * DH + DH],
                    att_ap(pr, kc * CS, kc * CS + CS),
                    start=(kc == 0), stop=(kc == 1)),
                    deps=av_deps if kc == 0 else [])
            RX.did_read(avmm)
            att_reg[pr].did_read(avmm)
            PB[bkA].did_write(avmm)
            do("vector", lambda e, off=off, hp=hp, ci=ci, bkA=bkA: e.tensor_copy(
                RDs[off:off + 64, hp * TILE + ci * CS:hp * TILE + ci * CS + CS],
                PBs[bkA][0:64, 0:CS]),
               reads=[PB[bkA]], writes=[RD])

        for it in range(NCH * HEADS):
            emit_scores_softmax(it)
            if it >= 2:
                emit_transpose_av(it - 2)
        emit_transpose_av(NCH * HEADS - 2)
        emit_transpose_av(NCH * HEADS - 1)
        if tile + 1 < N_TILES:
            emit_x_ln(tile + 1)
        if debug and tile == 0:
            dod("scalar", lambda e: e.dma_start(out=dbg["attn_out"], in_=RDs[:, :]),
                reads=[RD], ch="dbg")

        for os_ in range(2):
            sl_i = next_slot()
            dod("sync", lambda e, os_=os_, sl_i=sl_i: e.dma_start(
                out=slotf(sl_i), in_=woutp[os_]), writes=[SL[sl_i]],
                ch=f"sl{sl_i}")
            for oc4 in range(4):
                oc = os_ * 4 + oc4
                bk = next_bank(qkv_bank_rr, "qkv")
                pr_deps = (SL[sl_i].for_read() + RD.for_read()
                           + PB[bk].for_write())
                prmm = None
                for cc in range(CC):
                    prmm = p.op("tensor", lambda e, sl_i=sl_i, cc=cc, oc4=oc4, bk=bk: e.matmul(
                        PBs[bk][:, :],
                        slotf(sl_i)[:, cc * 512 + oc4 * 128:cc * 512 + oc4 * 128 + 128],
                        RDs[:, cc * TILE:cc * TILE + TILE],
                        start=(cc == 0), stop=(cc == CC - 1)),
                        deps=pr_deps if cc == 0 else [])
                SL[sl_i].did_read(prmm)
                RD.did_read(prmm)
                PB[bk].did_write(prmm)
                do("vector", lambda e, oc=oc, bk=bk: e.tensor_scalar(
                    RYs[:, oc * TILE:(oc + 1) * TILE], PBs[bk][:, :],
                    BOUTs[:, oc:oc + 1], None, op0=ALU.add),
                   reads=[PB[bk], RCONST], writes=[RY])
        if debug and tile == 0:
            dod("scalar", lambda e: e.dma_start(out=dbg["y_proj"], in_=RYs[:, :]),
                reads=[RY], ch="dbg")

        for ei in range(NSH):
            routed = False
            csl = 0
            for q in range(NQ):
                sl1, sl3, sl2 = next_slot(), next_slot(), next_slot()
                dod("sync", lambda e, ei=ei, q=q, sl1=sl1: e.dma_start(
                    out=SLOTs[sl1][:, :], in_=w1_s[ei, q]), writes=[SL[sl1]],
                    reads=[RWSPLIT] if tile == 0 else (), ch=f"sl{sl1}")
                dod("sync", lambda e, ei=ei, q=q, sl3=sl3: e.dma_start(
                    out=SLOTs[sl3][:, :], in_=w3_s[ei, q]), writes=[SL[sl3]],
                    reads=[RWSPLIT] if tile == 0 else (), ch=f"sl{sl3}")
                dod("sync", lambda e, ei=ei, q=q, sl2=sl2: e.dma_start(
                    out=SLOTs[sl2][:, :], in_=w2_s[ei, q]), writes=[SL[sl2]],
                    reads=[RWSPLIT] if tile == 0 else (), ch=f"sl{sl2}")
                for hc in range(4):
                    bkG = [0, 2][rr_state["gu"] % 2]
                    bkU = bkG + 1
                    rr_state["gu"] += 1

                    lhs1 = [(lambda e_cc: (lambda h, sl1=sl1, e_cc=e_cc, hc=hc:
                             SLOTs[sl1][:, h * HB + e_cc * HQ + hc * 128:
                                        h * HB + e_cc * HQ + hc * 128 + 128]))(cc)
                            for cc in range(CC)]
                    lhs3 = [(lambda e_cc: (lambda h, sl3=sl3, e_cc=e_cc, hc=hc:
                             SLOTs[sl3][:, h * HB + e_cc * HQ + hc * 128:
                                        h * HB + e_cc * HQ + hc * 128 + 128]))(cc)
                            for cc in range(CC)]
                    rhs = [(lambda e_cc: (lambda h, e_cc=e_cc: xn_half(e_cc, h)))(cc)
                           for cc in range(CC)]
                    mm3(bkG, lhs1, rhs, [SL[sl1], RXN])
                    mm3(bkU, lhs3, rhs, [SL[sl3], RXN])
                    do("scalar", lambda e, bkG=bkG: e.activation(
                        SGs[0][:, :], PBs[bkG][:, :], AF.Silu),
                       reads=[PB[bkG]], writes=[RSG[0]])
                    do("vector", lambda e, bkU=bkU: e.tensor_tensor(
                        SGs[1][:, :], PBs[bkU][:, :], SGs[0][:, :], op=ALU.mult),
                       reads=[PB[bkU], RSG[0]], writes=[RSG[1]])
                    if routed:
                        do("vector", lambda e, csl=csl: e.tensor_tensor(
                            SGs[0][:, :], SGs[1][:, :], CWs[csl][:, :], op=ALU.mult),
                           reads=[RSG[1], RCW[csl]], writes=[RSG[0]])
                        fin, rfin = SGs[0], RSG[0]
                    else:
                        fin, rfin = SGs[1], RSG[1]
                    do("scalar", lambda e, hc=hc, fin=fin: e.activation(
                        RHs[:, hc * 512:hc * 512 + 512], fin[:, :], AF.Copy),
                       reads=[rfin], writes=[RH])
                    do("vector", lambda e, hc=hc, fin=fin: e.tensor_tensor(
                        RHs[:, 2048 + hc * 512:2048 + hc * 512 + 512], fin[:, :],
                        RHs[:, hc * 512:hc * 512 + 512], op=ALU.subtract),
                       reads=[rfin], writes=[RH])
                for oc in range(CC):
                    bkW = [4, 5][rr_state["w2"] % 2]
                    rr_state["w2"] += 1
                    lhs2 = [(lambda e_hc: (lambda h, sl2=sl2, e_hc=e_hc, oc=oc:
                             SLOTs[sl2][:, h * HB + e_hc * D + oc * 128:
                                        h * HB + e_hc * D + oc * 128 + 128]))(hc)
                            for hc in range(4)]
                    rhsh = [(lambda e_hc: (lambda h, e_hc=e_hc:
                             RHs[:, h * 2048 + e_hc * 512:h * 2048 + e_hc * 512 + 512]))(hc)
                            for hc in range(4)]
                    mm3(bkW, lhs2, rhsh, [SL[sl2], RH])
                    do("vector", lambda e, oc=oc, bkW=bkW: e.tensor_tensor(
                        RYs[:, oc * TILE:(oc + 1) * TILE],
                        RYs[:, oc * TILE:(oc + 1) * TILE], PBs[bkW][:, :],
                        op=ALU.add), reads=[PB[bkW]], writes=[RY])
        if debug and tile == 0:
            dod("scalar", lambda e: e.dma_start(out=dbg["y_ffn"], in_=RYs[:, :]),
                reads=[RY], ch="dbg")

        dod("scalar", lambda e, tile=tile: e.dma_start(
            out=ysh_scr[tile], in_=RYs[:, :]), reads=[RY], writes=[RYSH], ch="ysh")


    w1flat = w1_s.rearrange("a b p c -> (a b p) c")
    w3flat = w3_s.rearrange("a b p c -> (a b p) c")
    w2flat = w2_s.rearrange("a b p c -> (a b p) c")
    def emit_tok_gather(k):
        par = k % 2
        xg_buf, xg_reg = (RXs, RX) if par == 0 else (RDs, RD)
        trow = LGs if par == 0 else CWFs
        dod("scalar", lambda e, k=k, trow=trow: e.dma_start(
            out=trow[0:1, :],
            in_=tok_scr[512 * k:512 * (k + 1), :].rearrange("a b -> (a b)").unsqueeze(0)),
            reads=[RTOKSCR],
            writes=[RTOKROW2[par], RLG if par == 0 else RCWF], ch="tk")
        for s in range(SUB):
            col = par * 4 + s
            tdep = RTOKROW2[par].for_read() + RCONST.for_read() + PB[7].for_write()
            tmm = p.op("tensor", lambda e, s=s, col=col, trow=trow: e.transpose(
                PBs[7][:, col:col + 1], trow[0:1, s * 128:(s + 1) * 128],
                IDs[0:1, 0:1]), deps=tdep)
            RTOKROW2[par].did_read(tmm)
            PB[7].did_write(tmm)
            do("vector", lambda e, col=col: e.tensor_copy(
                TOKSBs[:, col:col + 1], PBs[7][:, col:col + 1]),
               reads=[PB[7]], writes=[RTOKSB2[par]])
            dod("gpsimd", lambda e, s=s, col=col, xg_buf=xg_buf, par=par:
                e.indirect_dma_start(
                    out=xg_buf[:, s * D:(s + 1) * D], out_offset=None,
                    in_=xn_scr, in_offset=IOA(TOKSBs[:, col:col + 1], 0)),
                reads=[RTOKSB2[par], RXNSCR], writes=[xg_reg], ch="xg")

    emit_tok_gather(0)
    for k in range(NCHSP):
        if k + 1 < NCHSP:
            emit_tok_gather(k + 1)
        xnb = XNs if k % 2 == 0 else RBs[:, :].bitcast(BF16)
        rxnb = RXN if k % 2 == 0 else RB
        xg_buf, xg_reg = (RXs, RX) if k % 2 == 0 else (RDs, RD)
        for g4 in range(SUB * CC // 4):
            s, ch4 = divmod(g4, 2)
            cc0 = ch4 * 4
            bk = 6 + (g4 % 2)
            for q in range(4):
                cc = cc0 + q
                do("tensor", lambda e, s=s, cc=cc, bk=bk, q=q, xg_buf=xg_buf:
                   e.transpose(
                       PBs[bk][:, q * 128:(q + 1) * 128],
                       xg_buf[:, s * D + cc * 128:s * D + cc * 128 + 128],
                       IDs[:, :]),
                   reads=[xg_reg, RCONST], writes=[PB[bk]])
            pview = PBs[bk][:, :].rearrange("p (q c) -> p q c", q=4)

            def xnb_3d(h, xnb=xnb):
                return xnb[:, h * HB:(h + 1) * HB].rearrange(
                    "p (c t) -> p c t", c=CC)
            do("scalar", lambda e, s=s, cc0=cc0, pview=pview, xnb_3d=xnb_3d: e.activation(
                xnb_3d(0)[:, cc0:cc0 + 4, s * 128:(s + 1) * 128], pview, AF.Copy),
               reads=[PB[bk]], writes=[rxnb])
            do("vector", lambda e, s=s, cc0=cc0, pview=pview, xnb_3d=xnb_3d: e.tensor_tensor(
                xnb_3d(1)[:, cc0:cc0 + 4, s * 128:(s + 1) * 128], pview,
                xnb_3d(0)[:, cc0:cc0 + 4, s * 128:(s + 1) * 128], op=ALU.subtract),
               reads=[PB[bk]], writes=[rxnb])
        csl = k % 2
        dod("scalar", lambda e, k=k, csl=csl: e.dma_start(
            out=CWs[csl][:, :],
            in_=cw2_scr[512 * k:512 * (k + 1), :].rearrange("a b -> (a b)")
            .unsqueeze(0).broadcast_to((128, TILE))),
            reads=[RCW2SCR], writes=[RCW[csl]], ch=f"cw{csl}")
        for q in range(NQ):
            kq = k * 4 + q
            sl1, sl3, sl2 = next_slot5(), next_slot5(), next_slot5()
            for sl_i, wflat in [(sl1, w1flat), (sl3, w3flat), (sl2, w2flat)]:
                dod("gpsimd", lambda e, sl_i=sl_i, wflat=wflat, kq=kq:
                    e.indirect_dma_start(
                        out=SLOTs[sl_i][:, :], out_offset=None,
                        in_=wflat, in_offset=IOA(WIDXs[:, kq:kq + 1], 0)),
                    reads=[RWIDX, RWSPLITR], writes=[SL[sl_i]], ch="wg")
            for hc in range(4):
                bkG = [0, 2][rr_state["gu"] % 2]
                bkU = bkG + 1
                rr_state["gu"] += 1
                lhs1 = [(lambda e_cc: (lambda h, sl1=sl1, e_cc=e_cc, hc=hc:
                         SLOTs[sl1][:, h * HB + e_cc * HQ + hc * 128:
                                    h * HB + e_cc * HQ + hc * 128 + 128]))(cc)
                        for cc in range(CC)]
                lhs3 = [(lambda e_cc: (lambda h, sl3=sl3, e_cc=e_cc, hc=hc:
                         SLOTs[sl3][:, h * HB + e_cc * HQ + hc * 128:
                                    h * HB + e_cc * HQ + hc * 128 + 128]))(cc)
                        for cc in range(CC)]
                rhs = [(lambda e_cc: (lambda h, e_cc=e_cc, xnb=xnb:
                        xnb[:, h * HB + e_cc * TILE:h * HB + e_cc * TILE + TILE]))(cc)
                       for cc in range(CC)]
                mm3(bkG, lhs1, rhs, [SL[sl1], rxnb])
                mm3(bkU, lhs3, rhs, [SL[sl3], rxnb])
                do("scalar", lambda e, bkG=bkG: e.activation(
                    SGs[0][:, :], PBs[bkG][:, :], AF.Silu),
                   reads=[PB[bkG]], writes=[RSG[0]])
                do("vector", lambda e, bkU=bkU: e.tensor_tensor(
                    SGs[1][:, :], PBs[bkU][:, :], SGs[0][:, :], op=ALU.mult),
                   reads=[PB[bkU], RSG[0]], writes=[RSG[1]])
                do("vector", lambda e, csl=csl: e.tensor_tensor(
                    SGs[0][:, :], SGs[1][:, :], CWs[csl][:, :], op=ALU.mult),
                   reads=[RSG[1], RCW[csl]], writes=[RSG[0]])
                do("scalar", lambda e, hc=hc: e.activation(
                    RHs[:, hc * 512:hc * 512 + 512], SGs[0][:, :], AF.Copy),
                   reads=[RSG[0]], writes=[RH])
                do("vector", lambda e, hc=hc: e.tensor_tensor(
                    RHs[:, 2048 + hc * 512:2048 + hc * 512 + 512], SGs[0][:, :],
                    RHs[:, hc * 512:hc * 512 + 512], op=ALU.subtract),
                   reads=[RSG[0]], writes=[RH])
            for oc in range(CC):
                bkW = [4, 5][rr_state["w2"] % 2]
                rr_state["w2"] += 1
                lhs2 = [(lambda e_hc: (lambda h, sl2=sl2, e_hc=e_hc, oc=oc:
                         SLOTs[sl2][:, h * HB + e_hc * D + oc * 128:
                                    h * HB + e_hc * D + oc * 128 + 128]))(hc)
                        for hc in range(4)]
                rhsh = [(lambda e_hc: (lambda h, e_hc=e_hc:
                         RHs[:, h * 2048 + e_hc * 512:h * 2048 + e_hc * 512 + 512]))(hc)
                        for hc in range(4)]
                mm3(bkW, lhs2, rhsh, [SL[sl2], RH])
                if q == 0:
                    do("vector", lambda e, oc=oc, bkW=bkW: e.tensor_copy(
                        RYs[:, oc * TILE:(oc + 1) * TILE], PBs[bkW][:, :]),
                       reads=[PB[bkW]], writes=[RY])
                else:
                    do("vector", lambda e, oc=oc, bkW=bkW: e.tensor_tensor(
                        RYs[:, oc * TILE:(oc + 1) * TILE],
                        RYs[:, oc * TILE:(oc + 1) * TILE], PBs[bkW][:, :],
                        op=ALU.add), reads=[PB[bkW]], writes=[RY])
        for oc in range(CC):
            bk = 6 + oc % 2
            for s in range(SUB):
                do("tensor", lambda e, oc=oc, s=s, bk=bk: e.transpose(
                    PBs[bk][:, s * 128:(s + 1) * 128],
                    RYs[:, oc * TILE + s * 128:oc * TILE + s * 128 + 128], IDs[:, :]),
                   reads=[RY, RCONST], writes=[PB[bk]])
            do("vector", lambda e, oc=oc, bk=bk: e.tensor_copy(
                RAs[:, :].rearrange("p (s c) -> p s c", s=SUB)
                [:, :, oc * 128:(oc + 1) * 128],
                PBs[bk][:, :].rearrange("p (q c) -> p q c", q=4)),
               reads=[PB[bk]], writes=[RA])
        dod("scalar", lambda e, k=k: e.dma_start(
            out=og_scr[512 * k:512 * (k + 1), :].rearrange("(s p) c -> p s c", p=128),
            in_=RAs[:, :]), reads=[RA], writes=[ROG], ch="og")

    for tile in range(N_TILES):
        t0 = tile * TILE
        if tile % 2 == 0:
            fD, fDr = RDs[:, :], RD
            fX, fXr = RXs[:, :], RX
            fA, fAr = RAs[:, :], RA
            fB, fBr = RBs[:, :], RB
        else:
            fD, fDr = RYs[:, :], RY
            fX, fXr = XNs[:, :].bitcast(F32), RXN
            fA, fAr = slotf(0), SL[0]
            fB, fBr = slotf(1), SL[1]
        dod("scalar", lambda e, tile=tile, fD=fD: e.dma_start(
            out=fD, in_=ysh_scr[tile]), reads=[RYSH], writes=[fDr], ch="ysh2")
        dod("scalar", lambda e, t0=t0, fX=fX: e.dma_start(
            out=fX.rearrange("p (s c) -> p s c", s=SUB),
            in_=x_in[t0:t0 + TILE, :].rearrange("(s p) c -> p s c", p=128)),
            writes=[fXr], ch="x")
        for s in range(SUB):
            for li, (buf, reg) in enumerate([(fA, fAr), (fB, fBr)]):
                icol = tile * 8 + s * 2 + li
                dod("gpsimd", lambda e, s=s, buf=buf, icol=icol: e.indirect_dma_start(
                    out=buf[:, s * D:(s + 1) * D], out_offset=None,
                    in_=og_scr, in_offset=IOA(IDXALLs[:, icol:icol + 1], 0)),
                    reads=[RIDXALL, ROG], writes=[reg], ch="fg")
        for oc in range(CC):
            bk = 6 + oc % 2
            for s in range(SUB):
                do("tensor", lambda e, oc=oc, s=s, bk=bk, fD=fD: e.transpose(
                    PBs[bk][:, s * 128:(s + 1) * 128],
                    fD[:, oc * TILE + s * 128:oc * TILE + s * 128 + 128], IDs[:, :]),
                   reads=[fDr, RCONST], writes=[PB[bk]])
            do("vector", lambda e, oc=oc, bk=bk, fX=fX: e.tensor_tensor(
                fX.rearrange("p (s c) -> p s c", s=SUB)
                [:, :, oc * 128:(oc + 1) * 128],
                PBs[bk][:, :].rearrange("p (q c) -> p q c", q=4),
                fX.rearrange("p (s c) -> p s c", s=SUB)
                [:, :, oc * 128:(oc + 1) * 128], op=ALU.add),
               reads=[PB[bk]], writes=[fXr])
        do("vector", lambda e, fX=fX, fA=fA: e.tensor_tensor(
            fX, fX, fA, op=ALU.add), reads=[fAr], writes=[fXr])
        do("vector", lambda e, fX=fX, fB=fB: e.tensor_tensor(
            fX, fX, fB, op=ALU.add), reads=[fBr], writes=[fXr])
        st = dod("scalar", lambda e, t0=t0, fX=fX: e.dma_start(
            out=y_out[t0:t0 + TILE, :].rearrange("(s p) c -> p s c", p=128),
            in_=fX.rearrange("p (s c) -> p s c", s=SUB)),
            reads=[fXr], writes=[], ch="st")
        store_ops.append(st)

    p.op("vector", lambda e: e.memset(R3s[0:1, 0:1], 0.0), deps=store_ops)

    p.emit(nc, stack)
    return nc, stack, dbg


def _pack(inputs):
    f32 = np.float32
    Wqkv = np.asarray(inputs["L0_attn_Wqkv"], f32)
    Wout = np.asarray(inputs["L0_attn_Wout"], f32)
    w1 = np.asarray(inputs["L0_ffn_w1"], f32)
    w3 = np.asarray(inputs["L0_ffn_w3"], f32)
    w2 = np.asarray(inputs["L0_ffn_w2"], f32)
    sw1 = np.asarray(inputs["L0_ffn_sw1"], f32)
    sw3 = np.asarray(inputs["L0_ffn_sw3"], f32)
    sw2 = np.asarray(inputs["L0_ffn_sw2"], f32)

    wqkvp = np.ascontiguousarray(
        Wqkv.reshape(6, 512, 8, 128).transpose(0, 3, 2, 1)).reshape(6, 128, 4096)
    woutp = np.ascontiguousarray(
        Wout.reshape(2, 512, 8, 128).transpose(0, 3, 2, 1)).reshape(2, 128, 4096)

    def pack13(w):
        e = w.shape[0]
        return np.ascontiguousarray(
            w.reshape(e, NQ, HQ, 8, 128).transpose(0, 1, 4, 3, 2)).reshape(
                e, NQ, 128, 4096)

    def pack2(w):
        e = w.shape[0]
        return np.ascontiguousarray(
            w.reshape(e, D, NQ, 4, 128).transpose(0, 2, 4, 3, 1)).reshape(
                e, NQ, 128, 4096)

    w1p = np.concatenate([pack13(sw1), pack13(w1)], axis=0)
    w3p = np.concatenate([pack13(sw3), pack13(w3)], axis=0)
    w2p = np.concatenate([pack2(sw2), pack2(w2)], axis=0)

    router_w = np.asarray(inputs["L0_ffn_router_w"], f32)
    rwp = np.ascontiguousarray(
        router_w.reshape(8, 8, 128).transpose(2, 1, 0)).reshape(128, 64)
    boutp = np.ascontiguousarray(
        np.asarray(inputs["L0_attn_bout"], f32).reshape(8, 128).T)

    common = {
        "wqkvp": wqkvp, "woutp": woutp,
        "w1p": w1p, "w3p": w3p, "w2p": w2p,
        "rwp": rwp, "boutp": boutp,
        "ebias": np.pad(np.asarray(inputs["L0_ffn_expert_bias"], f32).reshape(8, 1),
                        ((0, 120), (0, 0))),
        "bl1": np.asarray(inputs["L0_attn_bl1"], f32).reshape(64, 1),
        "bl2": np.asarray(inputs["L0_attn_bl2"], f32).reshape(16, 1),
        "wl1row": np.pad(np.asarray(inputs["L0_attn_Wl1"], f32).T, ((0, 15), (0, 0))),
        "wl2t": np.ascontiguousarray(np.asarray(inputs["L0_attn_Wl2"], f32).T),
        "idx8": np.tile(np.arange(8, dtype=f32), (128, 1)),
    }
    x_all = np.ascontiguousarray(np.asarray(inputs["L0_attn_x"], f32)).reshape(
        16 * 2048, D)
    return common, x_all


def kernel(**inputs):
    common, x_all = _pack(inputs)
    nc, stack, _ = build_program(debug=False)
    in_maps = []
    for c in range(NCORES):
        m = dict(common)
        m["x"] = np.ascontiguousarray(x_all[c * T_CORE:(c + 1) * T_CORE])
        in_maps.append(m)
    res = run_bass_kernel_spmd(nc, in_maps, core_ids=list(range(NCORES)))
    out = np.concatenate([r["y"] for r in res.results], axis=0)
    return out.reshape(16, 2048, D)
